# version 01
# baseline (speedup 1.0000x reference)
import numpy as np
from contextlib import ExitStack

import concourse.bass as bass
import concourse.mybir as mybir
import bass_rust
from concourse.bass_utils import run_bass_kernel_spmd

F32 = mybir.dt.float32
F16 = mybir.dt.float16
BF16 = mybir.dt.bfloat16
AF = mybir.ActivationFunctionType
ALU = mybir.AluOpType
AX = mybir.AxisListType

NCORES = 8
TAU = 3840
NT = 30
MC = 32
D = 1024
EXT = [3456, 3200, 2944, 2688, 2432, 2176]
AST = [3840 - e for e in EXT]
DEPTH = 6
GRP = 4
NGRP = 32


def mkap(base_ap, pairs, offset):
    c = base_ap.copy()
    c.ap = bass_rust.VecI64Pair(pairs)
    c.offset = offset
    return c


class PB:
    ENG = ("sync", "tensor", "vector", "scalar", "gpsimd")
    OWN = {"tensor": "tp"}
    DMA_SEMS = ("dq", "gq")

    def __init__(self, nc, es):
        self.nc, self.es = nc, es
        self.ops = {e: [] for e in self.ENG}
        self.semc = {}
        self.sems = {}
        self.waited = {e: {} for e in self.ENG}
        self.sbuf_bytes = 0
        self.ninstr = 0
        self.bank_owner = {}
        self.scopes = []

    def sem(self, name):
        if name not in self.sems:
            self.sems[name] = self.es.enter_context(self.nc.semaphore(name))
            self.semc[name] = 0
        return self.sems[name]

    def tok(self, name):
        return (name, self.semc.get(name, 0))

    def push_scope(self):
        self.scopes.append(ExitStack())

    def pop_scope(self):
        self.scopes.pop().close()

    def sbuf(self, name, shape, dtype):
        st_ = self.scopes[-1] if self.scopes else self.es
        t = st_.enter_context(self.nc.sbuf_tensor(name, list(shape), dtype))
        pp = int(np.prod(shape[1:])) * mybir.dt.size(dtype)
        self.sbuf_bytes += ((pp + 31) // 32) * 32 * 128
        return t

    def psum(self, name, shape, dtype):
        return self.es.enter_context(self.nc.psum_tensor(name, list(shape), dtype))

    def step(self, eng, fn, deps=(), inc=None, amt=1):
        own = self.OWN.get(eng)
        for d in deps:
            if d is None:
                continue
            sn, v = d
            if sn in self.DMA_SEMS:
                v = self.semc.get(sn, 0)
            if v <= 0 or sn == own:
                continue
            w = self.waited[eng].get(sn, 0)
            if v > w:
                s = self.sem(sn)
                self.ops[eng].append(lambda e, s=s, v=v: e.wait_ge(s, v))
                self.waited[eng][sn] = v
                self.ninstr += 1
        self.ninstr += 1
        if inc is not None:
            s = self.sem(inc)
            self.semc[inc] += amt
            v = self.semc[inc]
            self.ops[eng].append(lambda e, fn=fn, s=s, amt=amt: fn(e).then_inc(s, amt))
            return (inc, v)
        self.ops[eng].append(fn)
        return None

    def dma(self, dst, src, deps=(), queue="sync", sem="dq", **kw):
        return self.step(queue, lambda e, dst=dst, src=src, kw=kw: e.dma_start(out=dst, in_=src, **kw),
                         deps=deps, inc=sem, amt=16)

    FENCE = {"tensor": ("tp",), "vector": ("vp",), "scalar": ("sp",),
             "gpsimd": ("gp", "gq"), "sync": ("dq",)}

    def barrier(self, engines=None):
        engines = engines or self.ENG
        covered = set()
        for e in engines:
            for sn in self.FENCE[e]:
                covered.add(sn)
                v = self.semc.get(sn, 0)
                if v > self.waited[e].get(sn, 0):
                    sd = self.sem(sn)
                    self.ops[e].append(lambda eng, sd=sd, v=v: eng.wait_ge(sd, v))
                    self.waited[e][sn] = v
                    self.ninstr += 1
        e0 = engines[0]
        for ds in self.DMA_SEMS:
            if ds in covered:
                continue
            v = self.semc.get(ds, 0)
            if v > self.waited[e0].get(ds, 0):
                sd = self.sem(ds)
                self.ops[e0].append(lambda eng, sd=sd, v=v: eng.wait_ge(sd, v))
                self.waited[e0][ds] = v
                self.ninstr += 1
        s = self.sem("bar")
        tgt = self.semc["bar"] + len(engines)
        self.semc["bar"] = tgt
        for e in engines:
            self.ops[e].append(lambda eng, s=s: eng.nop().then_inc(s, 1))
        for e in engines:
            self.ops[e].append(lambda eng, s=s, v=tgt: eng.wait_ge(s, v))
            self.waited[e]["bar"] = tgt
        self.ninstr += 2 * len(engines)
        return ("bar", tgt)


def build_program(depth=DEPTH, debug=False, upto="all", ngrp=NGRP):
    nc = bass.Bass("TRN2", target_bir_lowering=False)
    es = ExitStack()
    p = PB(nc, es)

    IN = {}
    IN["u_ssm"] = nc.dram_tensor("u_ssm", [4096, D], F32, kind="ExternalInput").ap()
    for nm in ("a", "b", "c"):
        IN[nm] = nc.dram_tensor("in_" + nm, [256, 64], F32, kind="ExternalInput").ap()
    IN["Wqkv"] = nc.dram_tensor("Wqkv", [6, 1024, 3072], F32, kind="ExternalInput").ap()
    IN["qs"] = nc.dram_tensor("qs", [6, 64], F32, kind="ExternalInput").ap()
    IN["ks"] = nc.dram_tensor("ks", [6, 64], F32, kind="ExternalInput").ap()
    IN["Wo"] = nc.dram_tensor("Wo", [6, 1024, 1024], F32, kind="ExternalInput").ap()
    IN["ln_g"] = nc.dram_tensor("ln_g", [6, 1024], F32, kind="ExternalInput").ap()
    IN["ln_b"] = nc.dram_tensor("ln_b", [6, 1024], F32, kind="ExternalInput").ap()
    IN["W1"] = nc.dram_tensor("W1", [6, 1024, 5460], F32, kind="ExternalInput").ap()
    IN["b1"] = nc.dram_tensor("b1", [6, 5460], F32, kind="ExternalInput").ap()
    IN["W2"] = nc.dram_tensor("W2", [6, 2730, 1024], F32, kind="ExternalInput").ap()
    IN["b2"] = nc.dram_tensor("b2", [6, 1024], F32, kind="ExternalInput").ap()
    IN["pos"] = nc.dram_tensor("pos", [1, TAU], F32, kind="ExternalInput").ap()
    IN["sv"] = nc.dram_tensor("sv", [1, TAU], F32, kind="ExternalInput").ap()
    IN["cthr"] = nc.dram_tensor("cthr", [NT, 128], F32, kind="ExternalInput").ap()

    OUT = nc.dram_tensor("out_slice", [2048, D], F32, kind="ExternalOutput").ap()

    dbg_outs = {}

    def dbg(name, shape, dtype=F32):
        if not debug:
            return None
        ap = nc.dram_tensor("dbg_" + name, list(shape), dtype, kind="ExternalOutput").ap()
        dbg_outs[name] = ap
        return ap

    DR = {}
    DR["x1"] = nc.dram_tensor("x1_dram", [D, TAU], F32).ap()
    DR["xt0"] = nc.dram_tensor("xt0_dram", [D, TAU], F32).ap()
    DR["xt1"] = nc.dram_tensor("xt1_dram", [D, TAU], F32).ap()
    DR["ksh"] = nc.dram_tensor("ksh_dram", [256, 384], F16).ap()

    with nc.Block() as block:
        T = {}
        for i in range(8):
            T[f"pbank{i}"] = p.psum(f"pbank{i}", [128, 512], F32)
        phase_static(p, T)
        p.push_scope()
        if upto not in ("static",):
            phase_uln(p, T, IN, dbg)
        if upto not in ("static", "uln"):
            phase_abc(p, T, IN)
            phase_ssm(p, T, DR, dbg, ngrp=ngrp)
        if upto in ("x1", "all", "tf"):
            phase_x1(p, T, IN, DR, dbg)
        p.pop_scope()
        if upto in ("all", "tf"):
            phase_tables(p, T, IN)
            p.push_scope()
            phase_transformer(p, T, IN, DR, dbg, depth)
            p.pop_scope()
            phase_final(p, T, DR, OUT, depth, dbg)
        else:
            phase_out(p, T, DR, OUT, src="x1")
        for ename in PB.ENG:
            lst = p.ops[ename]
            getattr(block, ename)(lambda eng, lst=lst: [fn(eng) for fn in lst])
    es.close()
    return nc, list(dbg_outs.keys()), p


def phase_static(p, T):
    sb = p.sbuf
    for nm, shape, dt in (("iota_a", [128, 128], F32), ("iota_b", [128, 128], F32),
                          ("scr0", [128, 128], F32), ("scr1", [128, 128], F32),
                          ("pcol", [128, 1], F32), ("subdiag", [128, 128], F16),
                          ("superdiag", [128, 128], F16), ("ident_bf", [128, 128], BF16),
                          ("ident_f16", [128, 128], F16), ("ident_f32", [128, 128], F32),
                          ("rev_bf", [128, 128], BF16), ("ones_col", [128, 1], F32),
                          ("ones_row", [1, 128], F32), ("eps_col", [128, 1], F32)):
        T[nm] = sb(nm, shape, dt)

    ia, ib, s0, s1, pc = T["iota_a"], T["iota_b"], T["scr0"], T["scr1"], T["pcol"]
    t0 = p.step("gpsimd", lambda e: e.iota(ia[:], pattern=[[1, 128]], base=1,
                channel_multiplier=-1, allow_small_or_imprecise_dtypes=True), inc="gp")
    t1 = p.step("gpsimd", lambda e: e.iota(ib[:], pattern=[[1, 128]], base=0,
                channel_multiplier=0, allow_small_or_imprecise_dtypes=True), inc="gp")
    t2 = p.step("gpsimd", lambda e: e.iota(pc[:], pattern=[[0, 1]], base=0,
                channel_multiplier=1, allow_small_or_imprecise_dtypes=True), inc="gp")
    p.step("gpsimd", lambda e: e.memset(T["ones_col"][:], 1.0), inc="gp")
    p.step("gpsimd", lambda e: e.memset(T["ones_row"][:], 1.0), inc="gp")
    p.step("gpsimd", lambda e: e.memset(T["eps_col"][:], 1e-5), inc="gp")

    v = p.step("vector", lambda e: e.tensor_scalar(out=s0[:], in0=ib[:], scalar1=64.0,
               scalar2=None, op0=ALU.is_ge), deps=[t1], inc="vp")
    v = p.step("vector", lambda e: e.tensor_scalar(out=pc[:], in0=pc[:], scalar1=64.0,
               scalar2=None, op0=ALU.is_ge), deps=[t2], inc="vp")
    blk = p.step("vector", lambda e: e.tensor_scalar(out=s0[:], in0=s0[:], scalar1=pc[:],
                 scalar2=None, op0=ALU.is_equal), deps=[v], inc="vp")
    v = p.step("vector", lambda e: e.tensor_scalar(out=s1[:], in0=ia[:], scalar1=0.0,
               scalar2=None, op0=ALU.is_equal), deps=[t0], inc="vp")
    v = p.step("vector", lambda e: e.tensor_tensor(out=T["subdiag"][:], in0=s1[:], in1=s0[:],
               op=ALU.mult), deps=[v, blk], inc="vp")
    v = p.step("vector", lambda e: e.tensor_scalar(out=s1[:], in0=ia[:], scalar1=2.0,
               scalar2=None, op0=ALU.is_equal), deps=[v], inc="vp")
    v = p.step("vector", lambda e: e.tensor_tensor(out=T["superdiag"][:], in0=s1[:], in1=s0[:],
               op=ALU.mult), deps=[v], inc="vp")
    for nm in ("ident_bf", "ident_f16", "ident_f32"):
        v = p.step("vector", lambda e, nm=nm: e.tensor_scalar(out=T[nm][:], in0=ia[:],
                   scalar1=1.0, scalar2=None, op0=ALU.is_equal), deps=[t0], inc="vp")
    t5 = p.step("gpsimd", lambda e: e.iota(ib[:], pattern=[[1, 128]], base=-127,
                channel_multiplier=1, allow_small_or_imprecise_dtypes=True),
                deps=[blk], inc="gp")
    p.step("vector", lambda e: e.tensor_scalar(out=T["rev_bf"][:], in0=ib[:], scalar1=0.0,
           scalar2=None, op0=ALU.is_equal), deps=[t5], inc="vp")
    p.barrier()


def phase_abc(p, T, IN):
    sb = p.sbuf
    for nm, shape, dt in (("a_tm", [128, 2, 64], F32), ("b_tm", [128, 2, 64], F32),
                          ("c_tm", [128, 2, 64], F32), ("an16", [128, 2, 64], F16),
                          ("b16", [128, 2, 64], F16), ("c16", [128, 2, 64], F16),
                          ("anorm", [128, 2], F32), ("arecip", [128, 2], F32),
                          ("sel128", [128, 1], F32), ("sm2", [2, 1], F32),
                          ("sm1", [1, 1], F32), ("anT2", [128, 256], F16),
                          ("bT2", [128, 256], F16), ("cT2", [128, 256], F16)):
        T[nm] = sb(nm, shape, dt)

    pgi = [(64, 128), (8192, 2), (1, 64)]
    da = p.dma(T["a_tm"][:], mkap(IN["a"], pgi, 0))
    db = p.dma(T["b_tm"][:], mkap(IN["b"], pgi, 0))
    dc = p.dma(T["c_tm"][:], mkap(IN["c"], pgi, 0))

    v = p.step("vector", lambda e: e.tensor_reduce(out=T["anorm"][:], in_=T["a_tm"][:],
               axis=AX.X, op=ALU.add, apply_absolute_value=True), deps=[da], inc="vp")
    tm = p.step("tensor", lambda e: e.matmul(T["pbank0"][0:2, 0:1], T["anorm"][:],
                T["ones_col"][:], start=True, stop=True), deps=[v], inc="tp")
    v = p.step("vector", lambda e: e.tensor_copy(T["sm2"][:], T["pbank0"][0:2, 0:1]),
               deps=[tm], inc="vp")
    tm = p.step("tensor", lambda e: e.matmul(T["pbank0"][0:1, 4:5], T["sm2"][:],
                T["ones_col"][0:2, :], start=True, stop=True), deps=[v], inc="tp")
    v = p.step("vector", lambda e: e.tensor_scalar(out=T["sm1"][:], in0=T["pbank0"][0:1, 4:5],
               scalar1=1.0 / 256.0, scalar2=1e-4, op0=ALU.mult, op1=ALU.is_gt),
               deps=[tm], inc="vp")
    tm = p.step("tensor", lambda e: e.matmul(T["pbank0"][0:128, 8:9], T["ones_row"][:],
                T["sm1"][:], start=True, stop=True), deps=[v], inc="tp")
    v = p.step("vector", lambda e: e.tensor_copy(T["sel128"][:], T["pbank0"][0:128, 8:9]),
               deps=[tm], inc="vp")
    v1 = p.step("vector", lambda e: e.reciprocal(T["arecip"][:], T["anorm"][:]), deps=[v], inc="vp")
    v1 = p.step("vector", lambda e: e.tensor_scalar(out=T["arecip"][:], in0=T["arecip"][:],
                scalar1=T["sel128"][:], scalar2=None, op0=ALU.mult), deps=[v1], inc="vp")
    v2 = p.step("vector", lambda e: e.tensor_scalar(out=T["sel128"][:], in0=T["sel128"][:],
                scalar1=-1.0, scalar2=1.0, op0=ALU.mult, op1=ALU.add), deps=[v1], inc="vp")
    v = p.step("vector", lambda e: e.tensor_scalar(out=T["arecip"][:], in0=T["arecip"][:],
               scalar1=T["sel128"][:], scalar2=None, op0=ALU.add), deps=[v2], inc="vp")
    van = p.step("vector", lambda e: e.tensor_tensor(
        out=T["an16"][:], in0=T["a_tm"][:],
        in1=T["arecip"][:].unsqueeze(2).broadcast_to((128, 2, 64)), op=ALU.mult),
        deps=[v], inc="vp")
    vb = p.step("vector", lambda e: e.tensor_copy(T["b16"][:], T["b_tm"][:]), deps=[db], inc="vp")
    vc = p.step("vector", lambda e: e.tensor_copy(T["c16"][:], T["c_tm"][:]), deps=[dc], inc="vp")

    ps16 = T["pbank1"][:].bitcast(F16)
    toks = []
    for idx, src in enumerate((T["an16"], T["b16"], T["c16"])):
        for g in range(2):
            col = (idx * 2 + g) * 128
            toks.append(p.step("tensor", lambda e, src=src, g=g, col=col: e.transpose(
                ps16[0:64, col:col + 128], src[:, g, :], T["ident_f16"][:]),
                deps=[van, vb, vc], inc="tp"))
    s1 = p.step("scalar", lambda e: e.copy(T["anT2"][0:64, :], ps16[0:64, 0:256]),
                deps=toks, inc="sp")
    s2 = p.step("scalar", lambda e: e.copy(T["bT2"][0:64, :], ps16[0:64, 256:512]),
                deps=toks, inc="sp")
    s3 = p.step("scalar", lambda e: e.copy(T["cT2"][0:64, :], ps16[0:64, 512:768]),
                deps=toks, inc="sp")
    for nm, s in (("anT2", s1), ("bT2", s2), ("cT2", s3)):
        p.dma(T[nm][64:128, :], T[nm][0:64, :], deps=[s], queue="gpsimd", sem="gq")
    p.barrier()


def phase_uln(p, T, IN, dbg):
    sb = p.sbuf
    T["urev"] = sb("urev", [128, MC, D], BF16)
    ut = [sb(f"ut{i}", [128, D], F32) for i in range(2)]
    usq = sb("usq", [128, D], F32)
    uxc = sb("uxc", [128, D], F32)
    un_bf = [sb(f"un_bf{i}", [128, D], BF16) for i in range(2)]
    T["usq"] = usq
    T["ut0"] = ut[0]
    T["un_bf0"] = un_bf[0]
    st = {nm: sb("ul" + nm, [128, 1], F32) for nm in ("s1", "s2", "negmu", "var", "rstd")}
    d_un = dbg("un", [4096, D])

    prev_rev = [None, None]
    prev_ap = [None, None]
    for m in range(MC):
        par = m % 2
        dtok = p.dma(ut[par][:], IN["u_ssm"][m * 128:(m + 1) * 128, :],
                     deps=[prev_ap[par]])
        ln = emit_ln_stats(p, T, ut[par][:], uxc, usq, st, deps=[dtok])
        ap = p.step("vector", lambda e, par=par: e.tensor_scalar(
            out=un_bf[par][:], in0=uxc[:], scalar1=st["rstd"][:], scalar2=None,
            op0=ALU.mult), deps=[ln, prev_rev[par]], inc="vp")
        prev_ap[par] = ap
        if d_un is not None:
            cast = p.step("scalar", lambda e, par=par: e.copy(usq[:], un_bf[par][:]),
                          deps=[ap], inc="sp")
            p.dma(d_un[m * 128:(m + 1) * 128, :], usq[:], deps=[cast], queue="gpsimd", sem="gq")
        pb = [T["pbank0"], T["pbank1"]] if par == 0 else [T["pbank2"], T["pbank3"]]
        mms = []
        for h in range(2):
            mms.append(p.step("tensor", lambda e, par=par, h=h, pb=pb: e.matmul(
                pb[h][:, 0:512], T["rev_bf"][:], un_bf[par][:, h * 512:(h + 1) * 512],
                start=True, stop=True), deps=[ap, prev_rev[par]], inc="tp"))
        ev = p.step("scalar", lambda e, m=m, pb=pb: e.copy(T["urev"][:, m, 0:512], pb[0][:]),
                    deps=mms, inc="sp")
        ev = p.step("scalar", lambda e, m=m, pb=pb: e.copy(T["urev"][:, m, 512:1024], pb[1][:]),
                    deps=mms, inc="sp")
        prev_rev[par] = p.barrier(("tensor", "vector", "scalar", "sync", "gpsimd"))
        prev_ap[par] = prev_rev[par]
    p.barrier()


def emit_ln_stats(p, T, src_ap, xcbuf, sqbuf, st, deps=()):
    v1 = p.step("vector", lambda e: e.tensor_reduce(out=st["s1"][:], in_=src_ap, axis=AX.X,
                op=ALU.add), deps=deps, inc="vp")
    v2 = p.step("vector", lambda e: e.tensor_scalar(out=st["negmu"][:], in0=st["s1"][:],
                scalar1=-1.0 / D, scalar2=None, op0=ALU.mult), deps=[v1], inc="vp")
    v3 = p.step("vector", lambda e: e.tensor_scalar(out=xcbuf[:], in0=src_ap,
                scalar1=st["negmu"][:], scalar2=None, op0=ALU.add), deps=[v2], inc="vp")
    s4 = p.step("scalar", lambda e: e.activation(out=sqbuf[:], in_=xcbuf[:], func=AF.Square),
                deps=[v3] + list(deps), inc="sp")
    v5 = p.step("vector", lambda e: e.tensor_reduce(out=st["s2"][:], in_=sqbuf[:], axis=AX.X,
                op=ALU.add), deps=[s4], inc="vp")
    s6 = p.step("scalar", lambda e: e.activation(out=st["rstd"][:], in_=st["s2"][:],
                func=AF.Ln, bias=T["eps_col"][:], scale=1.0 / D), deps=[v5], inc="sp")
    s7 = p.step("scalar", lambda e: e.activation(out=st["rstd"][:], in_=st["rstd"][:],
                func=AF.Exp, scale=-0.5), deps=[s6], inc="sp")
    return s7


SSM_STOP = "all"

def phase_ssm(p, T, DR, dbg, ngrp=NGRP):
    sb = p.sbuf
    T["y_all"] = sb("y_all", [128, NT, D], BF16)
    Aa = [sb(f"Aa{q}", [128, 128], F16) for q in range(GRP)]
    Ab = [sb(f"Ab{q}", [128, 128], F16) for q in range(GRP)]
    Ata = [sb(f"Ata{q}", [128, 128], F16) for q in range(GRP)]
    Atb = [sb(f"Atb{q}", [128, 128], F16) for q in range(GRP)]
    U = [sb(f"U{q}", [128, 256], F16) for q in range(GRP)]
    P2 = [sb(f"P2{q}", [128, 128], F16) for q in range(GRP)]
    Pk = [sb(f"Pk{q}", [128, 128], F16) for q in range(GRP)]
    Gt = [[sb(f"Gt{q}_{v}", [128, 128], F16) for v in range(5)] for q in range(GRP)]
    Uy = [[sb(f"Uy{q}_{hh}", [128, 128], F16) for hh in range(2)] for q in range(GRP)]
    H0 = [[sb(f"H0_{q}_{hh}", [128, 128], F16) for hh in range(2)] for q in range(GRP)]
    Xa = [sb(f"Xa{q}", [128, MC, 8], BF16) for q in range(GRP)]
    Xb = [sb(f"Xb{q}", [128, MC, 8], BF16) for q in range(GRP)]
    kstage = sb("kstage", [1, 256], F16)
    zeros16 = sb("zeros16", [128, 384], F16)
    g0 = p.step("gpsimd", lambda e: e.memset(zeros16[:], 0.0), inc="gp")
    p.step("gpsimd", lambda e: e.memset(T["y_all"][:], 0.0), inc="gp")
    zt = p.dma(DR["ksh"][0:128, :], zeros16[:], deps=[g0])
    zt = p.dma(DR["ksh"][128:256, :], zeros16[:], deps=[g0])
    banks = [T[f"pbank{i}"] for i in range(8)]
    d_ksh = dbg("ksh", [256, 384])
    d_y = dbg("yall", [TAU, D])

    for g in range(ngrp):
        pairs = list(range(g * GRP, (g + 1) * GRP))
        itoks = []
        for q, pp in enumerate(pairs):
            h0, h1 = 2 * pp, 2 * pp + 1
            vA = p.step("vector", lambda e, q=q: e.tensor_copy(Aa[q][:], T["subdiag"][:]), inc="vp")
            vAt = p.step("vector", lambda e, q=q: e.tensor_copy(Ata[q][:], T["superdiag"][:]), inc="vp")
            itoks.append(p.step("vector", lambda e, q=q, h0=h0: e.tensor_copy(
                Aa[q][0:64, 63:64], T["anT2"][0:64, h0:h0 + 1]), deps=[vA], inc="vp"))
            itoks.append(p.step("vector", lambda e, q=q, h1=h1: e.tensor_copy(
                Aa[q][64:128, 127:128], T["anT2"][64:128, h1:h1 + 1]), deps=[vA], inc="vp"))
            itoks.append(p.dma(Ata[q][63:64, 0:64],
                               T["an16"][h0 % 128:h0 % 128 + 1, h0 // 128, :],
                               deps=[vAt], queue="gpsimd", sem="gq"))
            itoks.append(p.dma(Ata[q][127:128, 64:128],
                               T["an16"][h1 % 128:h1 % 128 + 1, h1 // 128, :],
                               deps=[vAt], queue="gpsimd", sem="gq"))
            vz = p.step("vector", lambda e, q=q: e.tensor_copy(U[q][:, 0:128], zeros16[:, 0:128]),
                        deps=[g0], inc="vp")
            p.step("vector", lambda e, q=q: e.tensor_copy(U[q][:, 128:256], zeros16[:, 0:128]),
                   deps=[g0], inc="vp")
            itoks.append(p.step("vector", lambda e, q=q, h0=h0: e.tensor_copy(
                U[q][0:64, 0:1], T["cT2"][0:64, h0:h0 + 1]), deps=[vz], inc="vp"))
            itoks.append(p.step("vector", lambda e, q=q, h1=h1: e.tensor_copy(
                U[q][64:128, 0:1], T["cT2"][64:128, h1:h1 + 1]), deps=[vz], inc="vp"))
            vz2 = p.step("vector", lambda e, q=q: e.tensor_copy(P2[q][:], zeros16[:, 0:128]),
                         deps=[g0], inc="vp")
            itoks.append(p.step("vector", lambda e, q=q, h0=h0: e.tensor_copy(
                P2[q][0:64, 0:1], T["bT2"][0:64, h0:h0 + 1]), deps=[vz2], inc="vp"))
            itoks.append(p.step("vector", lambda e, q=q, h1=h1: e.tensor_copy(
                P2[q][64:128, 0:1], T["bT2"][64:128, h1:h1 + 1]), deps=[vz2], inc="vp"))
        init_tok = [p.tok("vp"), p.tok("gq")]
        if SSM_STOP == "init":
            p.barrier(); continue

        Acur, Anxt, Atcur, Atnxt = Aa, Ab, Ata, Atb
        ev_last = None
        nv = int(SSM_STOP[5:]) if SSM_STOP.startswith("chain") and SSM_STOP[5:].isdigit() else (1 if SSM_STOP == "chainA" else (4 if SSM_STOP == "chainB" else 8))
        for v in range(nv):
            w = 1 << v
            mm_toks = []
            for q in range(GRP):
                ps = banks[q]
                dep = [ev_last] + init_tok if ev_last else init_tok
                mm_toks.append(p.step("tensor", lambda e, q=q, w=w, ps=ps, Ac=Acur: e.matmul(
                    ps[:, 0:w], Ac[q][:], U[q][:, 0:w], start=True, stop=True),
                    deps=dep, inc="tp"))
                if v < 7:
                    mm_toks.append(p.step("tensor", lambda e, q=q, w=w, ps=ps, At_=Atcur: e.matmul(
                        ps[:, 128:128 + w], At_[q][:], P2[q][:, 0:w],
                        start=True, stop=True), inc="tp"))
                    mm_toks.append(p.step("tensor", lambda e, q=q, ps=ps, Ac=Acur, At_=Atcur: e.matmul(
                        ps[:, 256:384], At_[q][:], Ac[q][:], start=True, stop=True),
                        inc="tp"))
                    mm_toks.append(p.step("tensor", lambda e, q=q, ps=ps, Ac=Acur, At_=Atcur: e.matmul(
                        ps[:, 384:512], Ac[q][:], At_[q][:], start=True, stop=True),
                        inc="tp"))
            evs = []
            for q in range(GRP):
                ps = banks[q]
                evs.append(p.step("scalar", lambda e, q=q, w=w, ps=ps: e.copy(
                    U[q][:, w:2 * w], ps[:, 0:w]), deps=mm_toks, inc="sp"))
                if v < 7:
                    evs.append(p.step("scalar", lambda e, q=q, w=w, ps=ps: e.copy(
                        P2[q][:, w:2 * w], ps[:, 128:128 + w]), deps=mm_toks, inc="sp"))
                    evs.append(p.step("scalar", lambda e, q=q, ps=ps, An=Anxt: e.copy(
                        An[q][:], ps[:, 256:384]), deps=mm_toks, inc="sp"))
                    evs.append(p.step("scalar", lambda e, q=q, ps=ps, Atn=Atnxt: e.copy(
                        Atn[q][:], ps[:, 384:512]), deps=mm_toks, inc="sp"))
            ev_last = p.barrier()
            if v < 7:
                Acur, Anxt = Anxt, Acur
                Atcur, Atnxt = Atnxt, Atcur
        if SSM_STOP.startswith("chain") and SSM_STOP not in ("chain",):
            p.barrier(); continue
        for q in range(GRP):
            p.step("vector", lambda e, q=q, At_=Atcur: e.tensor_copy(Gt[q][0][:], At_[q][:]),
                   deps=[ev_last], inc="vp")
        for v in range(4):
            mm_toks = []
            for q in range(GRP):
                ps = banks[q]
                mm_toks.append(p.step("tensor", lambda e, q=q, ps=ps, Ac=Acur, At_=Atcur: e.matmul(
                    ps[:, 256:384], At_[q][:], Ac[q][:], start=True, stop=True),
                    deps=[ev_last, p.tok("vp")], inc="tp"))
                mm_toks.append(p.step("tensor", lambda e, q=q, ps=ps, Ac=Acur, At_=Atcur: e.matmul(
                    ps[:, 384:512], Ac[q][:], At_[q][:], start=True, stop=True), inc="tp"))
            for q in range(GRP):
                ps = banks[q]
                p.step("scalar", lambda e, q=q, ps=ps, An=Anxt: e.copy(An[q][:], ps[:, 256:384]),
                       deps=mm_toks, inc="sp")
                p.step("scalar", lambda e, q=q, ps=ps, Atn=Atnxt: e.copy(Atn[q][:], ps[:, 384:512]),
                       deps=mm_toks, inc="sp")
            ev_last = p.barrier()
            Acur, Anxt = Anxt, Acur
            Atcur, Atnxt = Atnxt, Atcur
            for q in range(GRP):
                p.step("vector", lambda e, q=q, v=v, At_=Atcur: e.tensor_copy(Gt[q][v + 1][:], At_[q][:]),
                       deps=[ev_last], inc="vp")
        gt_tok = p.tok("vp")
        if SSM_STOP in ("chain", "gt"):
            p.barrier(); continue

        uy_toks = []
        for q in range(GRP):
            for hh in range(2):
                base = 64 * hh
                g1 = p.step("gpsimd", lambda e, q=q, hh=hh: e.memset(Uy[q][hh][:], 0.0),
                            deps=[ev_last], inc="gp")
                uy_toks.append(p.step("scalar", lambda e, q=q, hh=hh, base=base: e.copy(
                    Uy[q][hh][base:base + 64, :], U[q][base:base + 64, 1:129]),
                    deps=[g1, ev_last, gt_tok], inc="sp"))

        pk_toks = []
        for q in range(GRP):
            ps16 = banks[q][:].bitcast(F16)
            tm = p.step("tensor", lambda e, q=q, ps16=ps16: e.transpose(
                ps16[:, 0:128], P2[q][:], T["ident_f16"][:]), deps=[ev_last, gt_tok], inc="tp")
            pk_toks.append(p.step("scalar", lambda e, q=q, ps16=ps16: e.copy(
                Pk[q][:], ps16[:, 0:128]), deps=[tm], inc="sp"))

        ktoks = []
        prev_kdma = None
        for q, pp in enumerate(pairs):
            for hh in range(2):
                h = 2 * pp + hh
                base = 64 * hh
                ps = banks[q]
                tm = p.step("tensor", lambda e, q=q, h=h, base=base, ps=ps: e.matmul(
                    ps[0:1, 0:256], T["bT2"][base:base + 64, h:h + 1],
                    U[q][base:base + 64, :], start=True, stop=True),
                    deps=[ev_last, gt_tok, pk_toks[-1], prev_kdma], inc="tp")
                ev = p.step("scalar", lambda e, ps=ps: e.copy(kstage[:], ps[0:1, 0:256]),
                            deps=[tm, prev_kdma], inc="sp")
                prev_kdma = p.dma(DR["ksh"][h:h + 1, 127:383], kstage[:], deps=[ev],
                                  queue="gpsimd", sem="gq")
                ktoks.append(prev_kdma)
        if SSM_STOP == "ksh":
            p.barrier(); continue
        htoks = []
        for q, pp in enumerate(pairs):
            for hh in range(2):
                h = 2 * pp + hh
                hsrc = mkap(DR["ksh"], [(1, 128), (1, 128)], h * 384)
                htoks.append(p.dma(H0[q][hh][:], hsrc, deps=[ktoks[-1]]))
        if SSM_STOP == "h0":
            p.barrier(); continue

        urev_flat = T["urev"][:]
        for q, pp in enumerate(pairs):
            psz = banks[q]
            zmm = p.step("tensor", lambda e, q=q, pp=pp, psz=psz: e.matmul(
                psz[:, 0:256], Pk[q][:],
                T["urev"][:, :, 8 * pp:8 * pp + 8],
                start=True, stop=True), deps=[pk_toks[-1], ktoks[-1]], inc="tp")
            ztok = p.step("scalar", lambda e, q=q, psz=psz: e.copy(
                Xa[q][:], psz[:, 0:256]), deps=[zmm], inc="sp")
            if SSM_STOP == "z":
                continue
            Xc, Xn = Xa, Xb
            for v in range(5 if SSM_STOP != "scan1" else 1):
                sh = 1 << v
                ps = banks[q]
                m2 = p.step("tensor", lambda e, q=q, v=v, sh=sh, ps=ps, Xc=Xc: e.matmul(
                    ps[:, 8 * sh:256], Gt[q][v][:],
                    Xc[q][:, 0:MC - sh, :],
                    start=True, stop=True), deps=[ztok, gt_tok], inc="tp")
                vt = p.step("vector", lambda e, q=q, sh=sh, ps=ps, Xc=Xc, Xn=Xn: e.tensor_tensor(
                    out=Xn[q][:, sh:MC, :],
                    in0=ps[:, 8 * sh:256],
                    in1=Xc[q][:, sh:MC, :], op=ALU.add),
                    deps=[m2], inc="vp")
                ztok = p.step("vector", lambda e, q=q, sh=sh, Xc=Xc, Xn=Xn: e.tensor_copy(
                    Xn[q][:, 0:sh, :],
                    Xc[q][:, 0:sh, :]), deps=[vt], inc="vp")
                Xc, Xn = Xn, Xc
            for hh in range(2):
                if SSM_STOP in ("scan", "scan1"):
                    continue
                h = 2 * pp + hh
                base = 64 * hh
                psy = banks[q]
                yc = 256 + 120 * hh
                m1 = p.step("tensor", lambda e, q=q, hh=hh, pp=pp, psy=psy, yc=yc: e.matmul(
                    psy[:, yc:yc + 120], H0[q][hh][:],
                    T["urev"][:, 2:32, 8 * pp + 4 * hh:8 * pp + 4 * hh + 4],
                    start=True, stop=False), deps=[htoks[-1], ztok], inc="tp")
                m2 = p.step("tensor", lambda e, q=q, hh=hh, psy=psy, Xc=Xc, yc=yc: e.matmul(
                    psy[:, yc:yc + 120], Uy[q][hh][:],
                    Xc[q][:, 1:31, 4 * hh:4 * hh + 4],
                    start=False, stop=True), deps=[uy_toks[-1]], inc="tp")
                p.step("scalar", lambda e, h=h, psy=psy, yc=yc: e.copy(
                    T["y_all"][:, :, 4 * h:4 * h + 4],
                    psy[:, yc:yc + 120]), deps=[m1, m2], inc="sp")
        p.barrier()

    if d_ksh is not None:
        ks32 = T["ut0"]
        ks16 = T["un_bf0"][:].bitcast(F16)
        for half in range(2):
            dt = p.dma(ks16[:, 0:384], DR["ksh"][128 * half:128 * (half + 1), :])
            cv = p.step("scalar", lambda e: e.copy(ks32[:, 0:384], ks16[:, 0:384]),
                        deps=[dt], inc="sp")
            p.dma(d_ksh[128 * half:128 * (half + 1), :], ks32[:, 0:384], deps=[cv],
                  queue="gpsimd", sem="gq")
            p.barrier(("scalar", "gpsimd", "sync"))
    if d_y is not None:
        ycast = T["usq"]
        for j in range(NT):
            v = p.step("scalar", lambda e, j=j: e.copy(ycast[:], T["y_all"][:, j, :]),
                       deps=[p.tok("gq")], inc="sp")
            p.dma(d_y[j * 128:(j + 1) * 128, :], ycast[:], deps=[v], queue="gpsimd", sem="gq")
            p.barrier(("scalar", "gpsimd"))
    p.barrier()


def phase_x1(p, T, IN, DR, dbg):
    sb = p.sbuf
    uxt = [sb(f"uxt{i}", [128, D], F32) for i in range(2)]
    ysq = sb("ysq", [128, D], F32)
    yxc = sb("yxc", [128, D], F32)
    x1t = sb("x1t", [128, D], F32)
    xt0t = sb("xt0t", [128, D], F32)
    xf = [sb(f"xf{i}", [128, 128], F32) for i in range(2)]
    st = {nm: sb("x1" + nm, [128, 1], F32) for nm in ("s1", "s2", "negmu", "var", "rstd")}

    prev = [None, None]
    prev_tr = None
    for j in range(NT):
        par = j % 2
        dtok = p.dma(uxt[par][:], IN["u_ssm"][j * 128 + 256:(j + 1) * 128 + 256, :],
                     deps=[prev[par]])
        ytile = T["y_all"][:, j, :]
        ln = emit_ln_stats(p, T, ytile, yxc, ysq, st, deps=[prev_tr])
        a1 = p.step("vector", lambda e: e.tensor_scalar(
            out=x1t[:], in0=yxc[:], scalar1=st["rstd"][:], scalar2=None,
            op0=ALU.mult), deps=[ln], inc="vp")
        a2 = p.step("vector", lambda e, par=par: e.tensor_tensor(
            out=x1t[:], in0=x1t[:], in1=uxt[par][:], op=ALU.add), deps=[a1, dtok], inc="vp")
        prev[par] = a2
        ln2 = emit_ln_stats(p, T, x1t[:], yxc, ysq, st, deps=[a2])
        a3 = p.step("vector", lambda e: e.tensor_scalar(
            out=xt0t[:], in0=yxc[:], scalar1=st["rstd"][:], scalar2=None,
            op0=ALU.mult), deps=[ln2], inc="vp")
        evs = []
        bank_free = {}
        xf_free = {}
        for (src, dst, tok, fpar) in ((x1t, DR["x1"], a2, 0), (xt0t, DR["xt0"], a3, 1)):
            for cch in range(8):
                bid = 4 + (cch % 2) + 2 * fpar
                ps = T[f"pbank{bid}"]
                tm = p.step("tensor", lambda e, src=src, cch=cch, ps=ps: e.transpose(
                    ps[:, 0:128], src[:, cch * 128:(cch + 1) * 128], T["ident_f32"][:]),
                    deps=[tok, prev_tr, bank_free.get(bid)], inc="tp")
                ev = p.step("scalar", lambda e, ps=ps, fpar=fpar: e.copy(xf[fpar][:], ps[:, 0:128]),
                            deps=[tm, bank_free.get(bid), xf_free.get(fpar)], inc="sp")
                bank_free[bid] = ev
                dt = p.dma(dst[cch * 128:(cch + 1) * 128, j * 128:(j + 1) * 128],
                           xf[fpar][:], deps=[ev], queue="gpsimd", sem="gq")
                xf_free[fpar] = dt
                evs.append(dt)
        prev_tr = p.barrier()
    p.barrier()


def phase_out(p, T, DR, OUT, src="xt0"):
    xin = p.sbuf("out_xin", [128, 128], F32)
    xrow = p.sbuf("out_xrow", [128, D], F32)
    final_tok = None
    for j in range(16):
        evs = []
        prev_dma = None
        bank_free = {}
        for cch in range(8):
            bid = cch % 4
            ps = T[f"pbank{bid}"]
            dt = p.dma(xin[:], DR[src][cch * 128:(cch + 1) * 128,
                       1792 + j * 128:1792 + (j + 1) * 128], deps=[prev_dma])
            tm = p.step("tensor", lambda e, ps=ps: e.transpose(
                ps[:, 0:128], xin[:], T["ident_f32"][:]),
                deps=[dt, bank_free.get(bid)], inc="tp")
            ev = p.step("scalar", lambda e, cch=cch, ps=ps: e.copy(
                xrow[:, cch * 128:(cch + 1) * 128], ps[:, 0:128]), deps=[tm], inc="sp")
            bank_free[bid] = ev
            evs.append(ev)
            prev_dma = tm
        final_tok = p.dma(OUT[j * 128:(j + 1) * 128, :], xrow[:], deps=evs,
                          queue="gpsimd", sem="gq")
        p.barrier(("tensor", "scalar", "gpsimd", "sync"))
    p.step("sync", lambda e: e.wait_ge(p.sem("gq"), p.semc["gq"]))



def phase_tables(p, T, IN):
    sb = p.sbuf
    T["Mrot"] = sb("Mrot", [128, 128], F16)
    T["band01"] = sb("band01", [128, 384], F32)
    T["jiota"] = sb("jiota", [128, 384], F32)
    T["if_row"] = sb("if_row", [1, 128], F32)
    T["lb128_row"] = sb("lb128_row", [1, 128], F32)
    T["lbm16col"] = sb("lbm16col", [128, 1], F32)
    T["lbp16col"] = sb("lbp16col", [128, 1], F32)
    T["onecol_bf"] = sb("onecol_bf", [128, 1], BF16)
    T["blkones2"] = sb("blkones2", [128, 2], BF16)
    T["bc2"] = sb("bc2", [2, 128], BF16)
    T["pihalf"] = sb("pihalf", [128, 1], F32)
    T["cthr_sb"] = sb("cthr_sb", [128, NT], F32)
    T["rowst"] = sb("rowst", [1, 512], F32)
    ia, ib, s0, s1 = T["iota_a"], T["iota_b"], T["scr0"], T["scr1"]

    g1 = p.step("gpsimd", lambda e: e.memset(T["pihalf"][:], np.pi / 2), inc="gp")
    g1 = p.step("gpsimd", lambda e: e.memset(T["onecol_bf"][:], 1.0), inc="gp")

    g2 = p.step("gpsimd", lambda e: e.iota(T["if_row"][:].rearrange("a (b c) -> a b c", b=4),
               pattern=[[0, 4], [1, 32]], base=0, channel_multiplier=0,
               allow_small_or_imprecise_dtypes=True), inc="gp")
    v2 = p.step("vector", lambda e: e.tensor_scalar(out=T["lb128_row"][:], in0=T["if_row"][:],
                scalar1=2.0 / 89.6, scalar2=25.6 / 89.6, op0=ALU.mult, op1=ALU.add),
                deps=[g2], inc="vp")
    s2 = p.step("scalar", lambda e: e.activation(out=T["lb128_row"][:], in_=T["lb128_row"][:],
                func=AF.Ln), deps=[v2], inc="sp")
    tmc = p.step("tensor", lambda e: e.transpose(T["pbank0"][:, 0:1],
                 T["lb128_row"][:], T["ident_f32"][0:1, 0:1]), deps=[s2], inc="tp")
    vv = p.step("vector", lambda e: e.tensor_scalar(out=T["lbm16col"][:],
                in0=T["pbank0"][:, 0:1], scalar1=-16.0, scalar2=None, op0=ALU.mult),
                deps=[tmc], inc="vp")
    vv2 = p.step("vector", lambda e: e.tensor_scalar(out=T["lbp16col"][:],
                 in0=T["pbank0"][:, 0:1], scalar1=16.0, scalar2=None, op0=ALU.mult),
                 deps=[tmc], inc="vp")
    v3 = p.step("vector", lambda e: e.tensor_scalar(out=T["lb128_row"][:],
                in0=T["lb128_row"][:], scalar1=1.0 / 128.0, scalar2=None, op0=ALU.mult),
                deps=[tmc], inc="vp")
    s3 = p.step("scalar", lambda e: e.activation(out=T["if_row"][:], in_=T["if_row"][:],
                func=AF.Exp, scale=-np.log(10000.0) / 32.0), deps=[g2, v2], inc="sp")

    g3 = p.step("gpsimd", lambda e: e.iota(ia[:], pattern=[[1, 128]], base=0,
                channel_multiplier=-1, allow_small_or_imprecise_dtypes=True), inc="gp")
    g4 = p.step("gpsimd", lambda e: e.iota(ib[:].rearrange("p (a b) -> p a b", a=2),
                pattern=[[0, 2], [1, 64]], base=0, channel_multiplier=0,
                allow_small_or_imprecise_dtypes=True), inc="gp")
    v = p.step("vector", lambda e: e.tensor_scalar(out=ib[:], in0=ib[:], scalar1=32.0,
               scalar2=None, op0=ALU.is_ge), deps=[g4], inc="vp")
    g5 = p.step("gpsimd", lambda e: e.iota(T["pcol"][:], pattern=[[0, 1]], base=0,
                channel_multiplier=1, allow_small_or_imprecise_dtypes=True), inc="gp")
    v3b = p.step("vector", lambda e: e.tensor_scalar(out=T["pcol"][:], in0=T["pcol"][:],
                 scalar1=64.0, scalar2=None, op0=ALU.is_ge), deps=[g5], inc="vp")
    g6 = p.step("gpsimd", lambda e: e.iota(s0[:], pattern=[[1, 128]], base=0,
                channel_multiplier=0, allow_small_or_imprecise_dtypes=True), inc="gp")
    v4 = p.step("vector", lambda e: e.tensor_scalar(out=s0[:], in0=s0[:], scalar1=64.0,
                scalar2=None, op0=ALU.is_ge), deps=[g6], inc="vp")
    v5 = p.step("vector", lambda e: e.tensor_scalar(out=s0[:], in0=s0[:],
                scalar1=T["pcol"][:], scalar2=None, op0=ALU.is_equal), deps=[v4, v3b], inc="vp")
    v6 = p.step("vector", lambda e: e.tensor_scalar(out=s1[:], in0=ia[:], scalar1=32.0,
                scalar2=None, op0=ALU.is_equal), deps=[g3], inc="vp")
    v7 = p.step("vector", lambda e: e.tensor_tensor(out=s1[:], in0=s1[:], in1=ib[:],
                op=ALU.mult), deps=[v6, v], inc="vp")
    v8 = p.step("vector", lambda e: e.tensor_scalar(out=ib[:], in0=ib[:], scalar1=-1.0,
                scalar2=1.0, op0=ALU.mult, op1=ALU.add), deps=[v7], inc="vp")
    v9 = p.step("vector", lambda e: e.tensor_scalar(out=ia[:], in0=ia[:], scalar1=-32.0,
                scalar2=None, op0=ALU.is_equal), deps=[v8], inc="vp")
    v10 = p.step("vector", lambda e: e.tensor_tensor(out=ia[:], in0=ia[:], in1=ib[:],
                 op=ALU.mult), deps=[v9], inc="vp")
    v11 = p.step("vector", lambda e: e.tensor_tensor(out=s1[:], in0=s1[:], in1=ia[:],
                 op=ALU.subtract), deps=[v10], inc="vp")
    p.step("vector", lambda e: e.tensor_tensor(out=T["Mrot"][:], in0=s1[:], in1=s0[:],
           op=ALU.mult), deps=[v11, v5], inc="vp")

    g8 = p.step("gpsimd", lambda e: e.iota(T["band01"][:], pattern=[[1, 384]], base=0,
                channel_multiplier=-1, allow_small_or_imprecise_dtypes=True), inc="gp")
    v12 = p.step("vector", lambda e: e.tensor_scalar(out=T["jiota"][:], in0=T["band01"][:],
                 scalar1=0.0, scalar2=None, op0=ALU.is_ge), deps=[g8], inc="vp")
    v13 = p.step("vector", lambda e: e.tensor_scalar(out=T["band01"][:], in0=T["band01"][:],
                 scalar1=256.0, scalar2=None, op0=ALU.is_le), deps=[v12], inc="vp")
    v14 = p.step("vector", lambda e: e.tensor_tensor(out=T["band01"][:], in0=T["band01"][:],
                 in1=T["jiota"][:], op=ALU.mult), deps=[v13], inc="vp")
    g7 = p.step("gpsimd", lambda e: e.iota(T["jiota"][:], pattern=[[1, 384]], base=0,
                channel_multiplier=0, allow_small_or_imprecise_dtypes=True),
                deps=[v14], inc="gp")

    g10 = p.step("gpsimd", lambda e: e.iota(T["pcol"][:], pattern=[[0, 1]], base=0,
                 channel_multiplier=1, allow_small_or_imprecise_dtypes=True), inc="gp")
    v18 = p.step("vector", lambda e: e.tensor_scalar(out=T["blkones2"][:, 1:2],
                 in0=T["pcol"][:], scalar1=64.0, scalar2=None, op0=ALU.is_ge),
                 deps=[g10], inc="vp")
    v19 = p.step("vector", lambda e: e.tensor_scalar(out=T["blkones2"][:, 0:1],
                 in0=T["pcol"][:], scalar1=64.0, scalar2=None, op0=ALU.is_lt),
                 deps=[g10], inc="vp")
    tmm = p.step("tensor", lambda e: e.transpose(
        T["pbank1"][:].bitcast(BF16)[0:2, 0:128], T["blkones2"][:], T["ident_bf"][:]),
        deps=[v18, v19], inc="tp")
    p.step("scalar", lambda e: e.copy(T["bc2"][:], T["pbank1"][:].bitcast(BF16)[0:2, 0:128]),
           deps=[tmm], inc="sp")
    T["tstage"] = sb("tstage", [64, 128], F32)
    dct = p.dma(T["tstage"][0:NT, :], IN["cthr"])
    tc = p.step("tensor", lambda e: e.transpose(T["pbank2"][:, 0:NT],
                T["tstage"][0:NT, :], T["ident_f32"][0:NT, 0:NT]),
                deps=[dct, p.tok("sp")], inc="tp")
    p.step("scalar", lambda e: e.copy(T["cthr_sb"][:], T["pbank2"][:, 0:NT]),
           deps=[tc], inc="sp")
    T["svb"] = sb("svb", [128, TAU], BF16)
    prev_mm = tc
    prev_cp = p.tok("sp")
    for i in range(8):
        c0 = i * 480
        d = p.dma(T["rowst"][0:1, 0:480], IN["sv"][0:1, c0:c0 + 480], deps=[prev_mm])
        prev_mm = p.step("tensor", lambda e, c0=c0: e.matmul(
            T["pbank3"][:, 0:480], T["ones_row"][:], T["rowst"][0:1, 0:480],
            start=True, stop=True), deps=[d, prev_cp], inc="tp")
        prev_cp = p.step("scalar", lambda e, c0=c0: e.copy(
            T["svb"][:, c0:c0 + 480], T["pbank3"][:, 0:480]),
            deps=[prev_mm], inc="sp")
    p.barrier()


def ln_row_stats(p, T, y2view, sqview, nm, n, deps):
    pbs = T["pbank6"]
    pbs2 = T["pbank7"]
    mms = []
    for dc in range(8):
        mms.append(p.step("tensor", lambda e, dc=dc: e.matmul(
            pbs[0:1, 0:n], T["onecol_bf"][:], y2view[:, dc, :],
            start=(dc == 0), stop=(dc == 7)), deps=deps, inc="tp"))
    sq = p.step("scalar", lambda e: e.activation(out=sqview[:], in_=y2view[:],
                func=AF.Square), deps=deps, inc="sp")
    mms2 = []
    for dc in range(8):
        mms2.append(p.step("tensor", lambda e, dc=dc: e.matmul(
            pbs2[0:1, 0:n], T["onecol_bf"][:], sqview[:, dc, :],
            start=(dc == 0), stop=(dc == 7)), deps=[sq], inc="tp"))
    st = T[nm]
    v1 = p.step("vector", lambda e: e.tensor_scalar(out=st["nmu"][0:1, 0:n],
                in0=pbs[0:1, 0:n], scalar1=-1.0 / D, scalar2=None, op0=ALU.mult),
                deps=mms, inc="vp")
    v2 = p.step("vector", lambda e: e.tensor_scalar(out=st["ex2"][0:1, 0:n],
                in0=pbs2[0:1, 0:n], scalar1=1.0 / D, scalar2=None, op0=ALU.mult),
                deps=mms2, inc="vp")
    v3 = p.step("vector", lambda e: e.tensor_tensor(out=st["musq"][0:1, 0:n],
                in0=st["nmu"][0:1, 0:n], in1=st["nmu"][0:1, 0:n], op=ALU.mult),
                deps=[v1], inc="vp")
    v4 = p.step("vector", lambda e: e.tensor_tensor(out=st["var"][0:1, 0:n],
                in0=st["ex2"][0:1, 0:n], in1=st["musq"][0:1, 0:n], op=ALU.subtract),
                deps=[v2, v3], inc="vp")
    s1 = p.step("scalar", lambda e: e.activation(out=st["rst"][0:1, 0:n],
                in_=st["var"][0:1, 0:n], func=AF.Ln, bias=T["eps_col"][0:1, :]),
                deps=[v4], inc="sp")
    s2 = p.step("scalar", lambda e: e.activation(out=st["rst"][0:1, 0:n],
                in_=st["rst"][0:1, 0:n], func=AF.Exp, scale=-0.5), deps=[s1], inc="sp")
    return v1, s2



def phase_transformer(p, T, IN, DR, dbg, depth):
    sb = p.sbuf
    OVL = sb("OVL", [128, 70144], BF16)
    qkvW = OVL[:, 0:24576].rearrange("p (a b) -> p a b", b=3072)
    woW = OVL[:, 24576:32768].rearrange("p (a b) -> p a b", b=1024)
    tabsW = OVL[:, 32768:47872].rearrange("p (a b) -> p a b", b=3776)
    y2b = OVL[:, 47872:51968].rearrange("p (a b) -> p a b", b=512)
    klin = OVL[:, 51968:58112].rearrange("p (a b) -> p a b", b=768)
    vt = [OVL[:, 58112 + 1024 * i:58112 + 1024 * (i + 1)] for i in range(6)]
    kshift = OVL[:, 64256:66304].rearrange("p (a b) -> p a b", b=256)
    Smb = OVL[:, 66304:67072].bitcast(F32)
    mbt = OVL[:, 67072:67456]
    Pb = OVL[:, 67456:67840]
    Ps = OVL[:, 67840:68224]
    PT = OVL[:, 68224:68608]
    rn2 = OVL[0:2, 68608:69632].bitcast(F32)
    rn2b = OVL[0:2, 69632:70144]
    w1W = OVL[:, 0:45056].rearrange("p (a b) -> p a b", b=5632)
    w2W = OVL[:, 45056:67584].rearrange("p (a b) -> p a b", b=1024)

    qb = sb("qb", [128, 8, 512], BF16)
    hbuf = sb("hbuf", [128, 11264], BF16)
    attnb = hbuf[:, 0:4096].rearrange("p (a b) -> p a b", b=512)
    sqv = hbuf[:, 4096:8192].rearrange("p (a b) -> p a b", b=512)
    hview = hbuf[:].rearrange("p (a b) -> p a b", b=512)
    tmpq = sb("tmpq", [128, 512], F32)
    tmpq2 = sb("tmpq2", [128, 512], F32)
    qnb = sb("qnb", [128, 512], BF16)
    rotb = sb("rotb", [128, 512], BF16)
    t1b = sb("t1b", [128, 512], BF16)
    t2b = sb("t2b", [128, 512], BF16)
    sqc = sb("sqc", [128, 512], BF16)
    xinoc = sb("xinoc", [128, 512], F32)
    xmc = sb("xmc", [128, 512], F32)
    den16 = sb("den16", [128, 16], F32)
    rec16 = sb("rec16", [128, 16], F32)
    nmax = sb("nmax", [128, 1], F32)
    st_r0 = sb("st_nmu", [1, 512], F32)
    st_r1 = sb("st_ex2", [1, 512], F32)
    st_r2 = sb("st_musq", [1, 512], F32)
    strow = {"nmu": st_r0, "ex2": st_r1, "musq": st_r2, "var": st_r1, "rst": st_r1}
    T["strow"] = strow
    T["tmpA"], T["tmpB"], T["tmpC"], T["tmpD"] = tmpq, tmpq2, xinoc, xmc
    gacol = sb("gacol", [128, 8], F32)
    becol = sb("becol", [128, 8], F32)
    b2col = sb("b2col", [128, 8], F32)
    b1acol = sb("b1acol", [128, 22], F32)
    b1gcol = sb("b1gcol", [128, 22], F32)
    qscol = sb("qscol", [128, 1], F32)
    kscol = sb("kscol", [128, 1], F32)
    vstage = T["tstage"]
    glub = OVL[:, 68608:69120]

    banks = [T[f"pbank{i}"] for i in range(8)]
    DR["xmid"] = p.nc.dram_tensor("xmid_dram", [D, TAU], F32).ap()
    DR["xn"] = p.nc.dram_tensor("xn_dram", [D, TAU], BF16).ap()

    gz0 = p.step("gpsimd", lambda e: e.memset(xinoc[:, 0:8], 0.0), inc="gp")
    for drn in ("xmid", "xt1"):
        dr = DR[drn]
        p.dma(mkap(dr, [(TAU, 128), (128 * TAU, 8), (1, 1)], dr.offset + 383),
              xinoc[:, 0:8].rearrange("p (a b) -> p a b", b=1), deps=[gz0],
              queue="gpsimd", sem="gq", allow_slow_non_contiguous=True)

    for l in range(depth):
        xin_d = DR["xt0"] if l % 2 == 0 else DR["xt1"]
        xout_d = DR["xt1"] if l % 2 == 0 else DR["xt0"]
        A0 = AST[l]
        E = EXT[l]
        kv0 = A0 - 256
        lvec_toks = []
        for (dst, srcap, n) in ((gacol, IN["ln_g"][l:l + 1, :], 8),
                                (becol, IN["ln_b"][l:l + 1, :], 8),
                                (b2col, IN["b2"][l:l + 1, :], 8)):
            dt = p.dma(vstage[0:n, 0:128], mkap(srcap, [(128, n), (1, 128)], srcap.offset),
                       deps=[p.tok("tp")])
            tm = p.step("tensor", lambda e, dst=dst, n=n: e.transpose(
                banks[7][:, 0:n], vstage[0:n, 0:128], T["ident_f32"][0:n, 0:n]),
                deps=[dt, p.tok("sp")], inc="tp")
            lvec_toks.append(p.step("scalar", lambda e, dst=dst, n=n: e.copy(
                dst[:, 0:n], banks[7][:, 0:n]), deps=[tm], inc="sp"))
        for (dst, off) in ((b1acol, 0), (b1gcol, 2730)):
            gz = p.step("gpsimd", lambda e: e.memset(vstage[:, :], 0.0),
                        deps=[p.tok("tp")], inc="gp")
            dt1 = p.dma(vstage[0:21, 0:128], mkap(IN["b1"], [(128, 21), (1, 128)],
                        IN["b1"].offset + l * 5460 + off), deps=[gz])
            dt2 = p.dma(vstage[21:22, 0:42], mkap(IN["b1"], [(128, 1), (1, 42)],
                        IN["b1"].offset + l * 5460 + off + 21 * 128), deps=[gz])
            tm = p.step("tensor", lambda e: e.transpose(
                banks[7][:, 0:22], vstage[0:22, 0:128], T["ident_f32"][0:22, 0:22]),
                deps=[dt1, dt2, p.tok("sp")], inc="tp")
            p.step("scalar", lambda e, dst=dst: e.copy(dst[:], banks[7][:, 0:22]),
                   deps=[tm], inc="sp")
        prev_fold = None
        for (dst, srcap, fold8) in ((qscol, IN["qs"], True), (kscol, IN["ks"], False)):
            dt = p.dma(vstage[0:1, 0:64], mkap(srcap, [(64, 1), (1, 64)],
                       srcap.offset + l * 64), deps=[p.tok("tp")])
            tm = p.step("tensor", lambda e: e.transpose(
                banks[7][0:64, 0:1], vstage[0:1, 0:64], T["ident_f32"][0:1, 0:1]),
                deps=[dt, p.tok("sp")], inc="tp")
            sv_ = p.step("scalar", lambda e, dst=dst: e.copy(dst[0:64, :], banks[7][0:64, 0:1]),
                         deps=[tm], inc="sp")
            dd = p.dma(dst[64:128, :], dst[0:64, :], deps=[sv_, prev_fold],
                       queue="gpsimd", sem="gq")
            if fold8:
                prev_fold = p.step("vector", lambda e, dst=dst: e.tensor_scalar(
                    out=dst[:], in0=dst[:], scalar1=8.0, scalar2=None, op0=ALU.mult),
                    deps=[dd], inc="vp")
        for dc in range(8):
            p.dma(qkvW[:, dc, :], IN["Wqkv"][l, dc * 128:(dc + 1) * 128, :],
                  queue="gpsimd", sem="gq", deps=[p.tok("tp")])
            p.dma(woW[:, dc, :], IN["Wo"][l, dc * 128:(dc + 1) * 128, :],
                  queue="gpsimd", sem="gq")
        Ekv = E + 256
        emit_tables_layer(p, T, IN, tabsW, kv0, Ekv, banks)
        p.barrier()

        sizes = []
        rem = E
        first = E - (E // 512) * 512
        if first:
            sizes.append(first)
            rem -= first
        sizes += [512] * (rem // 512)
        emit_attn_block(p, T, IN, DR, l, xin_d, kv0, 256, None, True,
                        qkvW, woW, tabsW, y2b, klin, vt, qb, attnb, sqv, kshift,
                        tmpq, tmpq2, qnb, rotb, t1b, t2b, sqc, Smb, mbt, Pb, Ps, PT,
                        xinoc, xmc, rn2, rn2b, den16, rec16, nmax, strow,
                        gacol, becol, qscol, kscol, banks, kv0, -1)
        p.barrier()
        t0 = A0
        prev_s = 256
        vbase = 2
        for bi, s_ in enumerate(sizes):
            emit_attn_block(p, T, IN, DR, l, xin_d, t0, s_, prev_s, False,
                            qkvW, woW, tabsW, y2b, klin, vt, qb, attnb, sqv, kshift,
                            tmpq, tmpq2, qnb, rotb, t1b, t2b, sqc, Smb, mbt, Pb, Ps, PT,
                            xinoc, xmc, rn2, rn2b, den16, rec16, nmax, strow,
                            gacol, becol, qscol, kscol, banks, kv0, vbase)
            p.barrier()
            t0 += s_
            prev_s = s_
            vbase += s_ // 128
        for dc in range(8):
            p.dma(w1W[:, dc, 0:2730], IN["W1"][l, dc * 128:(dc + 1) * 128, 0:2730],
                  queue="gpsimd", sem="gq", deps=[p.tok("tp"), p.tok("vp"), p.tok("sp")])
            p.dma(w1W[:, dc, 2816:5546], IN["W1"][l, dc * 128:(dc + 1) * 128, 2730:5460],
                  queue="gpsimd", sem="gq")
            g1 = p.step("gpsimd", lambda e, dc=dc: e.memset(w1W[:, dc, 2730:2816], 0.0), inc="gp")
            g1 = p.step("gpsimd", lambda e, dc=dc: e.memset(w1W[:, dc, 5546:5632], 0.0), inc="gp")
        for hc in range(22):
            if hc < 21:
                p.dma(w2W[:, hc, :], IN["W2"][l, hc * 128:(hc + 1) * 128, :],
                      queue="gpsimd", sem="gq")
            else:
                g2 = p.step("gpsimd", lambda e, hc=hc: e.memset(
                    w2W[:, hc, :].rearrange("p a -> p a"), 0.0), inc="gp")
                p.dma(w2W[0:42, hc, :], IN["W2"][l, 21 * 128:2730, :],
                      queue="gpsimd", sem="gq", deps=[g2])
        p.barrier()
        t0 = A0
        for bi, s_ in enumerate(sizes):
            emit_b2_block(p, T, DR, l, xout_d, t0, s_, w1W, w2W, hview, qb, glub,
                          xinoc, xmc, b1acol, b1gcol, b2col, banks)
            p.barrier()
            t0 += s_


def emit_tables_layer(p, T, IN, tabsW, kv0, Ekv, banks):
    npcs = (Ekv + 511) // 512
    ta, tb = T["tmpA"], T["tmpB"]
    tc_, td = T["tmpC"], T["tmpD"]
    TMP = {"tmpA": ta, "tmpB": tb, "tmpC": tc_, "tmpD": td}
    prev = [p.tok("tp")]
    for i in range(npcs):
        c0 = i * 512
        n = min(512, Ekv - c0)
        tau0 = kv0 + c0
        pb = banks[6]
        pc = banks[7]
        d = p.dma(T["rowst"][0:1, 0:n], IN["pos"][0:1, tau0:tau0 + n], deps=prev)
        m1 = p.step("tensor", lambda e, n=n, pb=pb: e.matmul(
            pb[:, 0:n], T["if_row"][:], T["rowst"][0:1, 0:n],
            start=True, stop=True), deps=[d, p.tok("sp"), p.tok("vp")], inc="tp")
        m2 = p.step("tensor", lambda e, n=n, pc=pc: e.matmul(
            pc[:, 0:n], T["lb128_row"][:], T["rowst"][0:1, 0:n],
            start=True, stop=True), deps=[d], inc="tp")
        prev = [m1, m2]
        v1 = p.step("vector", lambda e, n=n, pb=pb: e.tensor_scalar(
            out=ta[:, 0:n], in0=pb[:, 0:n], scalar1=1.0 / (2 * np.pi),
            scalar2=12582912.0, op0=ALU.mult, op1=ALU.add), deps=[m1], inc="vp")
        v2 = p.step("vector", lambda e, n=n: e.tensor_scalar(
            out=ta[:, 0:n], in0=ta[:, 0:n], scalar1=-12582912.0,
            scalar2=None, op0=ALU.add), deps=[v1], inc="vp")
        v3 = p.step("vector", lambda e, n=n, pb=pb: e.scalar_tensor_tensor(
            out=tb[:, 0:n], in0=ta[:, 0:n], scalar=-2 * np.pi,
            in1=pb[:, 0:n], op0=ALU.mult, op1=ALU.add), deps=[v2], inc="vp")
        v3c = p.step("vector", lambda e, n=n: e.tensor_scalar(
            out=tb[:, 0:n], in0=tb[:, 0:n], scalar1=3.141592, scalar2=-3.141592,
            op0=ALU.min, op1=ALU.max), deps=[v3], inc="vp")
        s1 = p.step("scalar", lambda e, n=n: e.activation(
            out=tc_[:, 0:n], in_=tb[:, 0:n], func=AF.Sin), deps=[v3c], inc="sp")
        v4 = p.step("vector", lambda e, n=n, pb=pb: e.tensor_scalar(
            out=ta[:, 0:n], in0=pb[:, 0:n], scalar1=1.0 / (2 * np.pi),
            scalar2=0.25, op0=ALU.mult, op1=ALU.add), deps=[s1], inc="vp")
        v4b = p.step("vector", lambda e, n=n: e.tensor_scalar(
            out=ta[:, 0:n], in0=ta[:, 0:n], scalar1=12582912.0,
            scalar2=None, op0=ALU.add), deps=[v4], inc="vp")
        v5 = p.step("vector", lambda e, n=n: e.tensor_scalar(
            out=ta[:, 0:n], in0=ta[:, 0:n], scalar1=-12582912.0,
            scalar2=None, op0=ALU.add), deps=[v4b], inc="vp")
        v6 = p.step("vector", lambda e, n=n, pb=pb: e.scalar_tensor_tensor(
            out=tb[:, 0:n], in0=ta[:, 0:n], scalar=-2 * np.pi,
            in1=pb[:, 0:n], op0=ALU.mult, op1=ALU.add), deps=[v5], inc="vp")
        v6c = p.step("vector", lambda e, n=n: e.tensor_scalar(
            out=tb[:, 0:n], in0=tb[:, 0:n], scalar1=-3.141592 - np.pi / 2,
            scalar2=3.141592 - np.pi / 2, op0=ALU.max, op1=ALU.min),
            deps=[v6], inc="vp")
        s2 = p.step("scalar", lambda e, n=n: e.activation(
            out=td[:, 0:n], in_=tb[:, 0:n], func=AF.Sin,
            bias=T["pihalf"][:]), deps=[v6c], inc="sp")
        s3 = p.step("scalar", lambda e, n=n, pc=pc: e.activation(
            out=ta[:, 0:n], in_=pc[:, 0:n], func=AF.Exp,
            bias=T["lbm16col"][:]), deps=[m2, s2], inc="sp")
        s4 = p.step("scalar", lambda e, n=n, pc=pc: e.activation(
            out=tb[:, 0:n], in_=pc[:, 0:n], func=AF.Exp, scale=-1.0,
            bias=T["lbp16col"][:]), deps=[s3], inc="sp")
        for (ti, srcT, scT) in ((0, "tmpD", "tmpA"), (1, "tmpC", "tmpA"),
                                (2, "tmpD", "tmpB"), (3, "tmpC", "tmpB")):
            prev.append(p.step("vector", lambda e, ti=ti, srcT=srcT, scT=scT, n=n,
                        c0=c0: e.tensor_tensor(
                out=tabsW[:, ti, c0:c0 + n], in0=TMP[srcT][:, 0:n], in1=TMP[scT][:, 0:n],
                op=ALU.mult), deps=[s2, s4], inc="vp"))

def emit_attn_block(p, T, IN, DR, l, xin_d, t0, s, prev_s, is_prologue,
                    qkvW, woW, tabsW, y2b, klin, vt, qb, attnb, sqv, kshift,
                    tmpq, tmpq2, qnb, rotb, t1b, t2b, sqc, Smb, mbt, Pb, Ps, PT,
                    xinoc, xmc, rn2, rn2b, den16, rec16, nmax, strow,
                    gacol, becol, qscol, kscol, banks, kv0, vbase):
    bk = {}

    def bank_dep(b):
        return bk.get(b)

    if not is_prologue:
        if prev_s >= 256:
            sh = p.step("vector", lambda e: e.tensor_copy(
                klin[:, :, 0:256], klin[:, :, prev_s:prev_s + 256]), inc="vp")
        else:
            sh1 = p.step("vector", lambda e: e.tensor_copy(
                kshift[:, :, :], klin[:, :, prev_s:prev_s + 256]), inc="vp")
            sh = p.step("vector", lambda e: e.tensor_copy(
                klin[:, :, 0:256], kshift[:, :, :]), deps=[sh1], inc="vp")
    else:
        sh = None

    d1 = p.dma(y2b[:, 0:4, 0:s], mkap(xin_d, [(TAU, 128), (128 * TAU, 4), (1, s)],
               xin_d.offset + t0), queue="gpsimd", sem="gq")
    d2 = p.dma(y2b[:, 4:8, 0:s], mkap(xin_d, [(TAU, 128), (128 * TAU, 4), (1, s)],
               xin_d.offset + 512 * TAU + t0 - 1), queue="gpsimd", sem="gq")
    yv = p.step("vector", lambda e: e.tensor_tensor(
        out=y2b[:, 4:8, 0:s], in0=y2b[:, 4:8, 0:s],
        in1=T["svb"][:, t0:t0 + s].unsqueeze(1).broadcast_to((128, 4, s)),
        op=ALU.mult), deps=[d1, d2], inc="vp")

    tb0 = t0 - kv0

    for oc in range(16):
        b = banks[oc % 2]
        mms = []
        for dc in range(8):
            mms.append(p.step("tensor", lambda e, oc=oc, dc=dc, b=b: e.matmul(
                b[:, 0:s], qkvW[:, dc, 128 * oc:128 * (oc + 1)], y2b[:, dc, 0:s],
                start=(dc == 0), stop=(dc == 7)),
                deps=[yv, bank_dep(oc % 2)], inc="tp"))
        e1 = p.step("scalar", lambda e, b=b: e.copy(tmpq[:, 0:s], b[:, 0:s]),
                    deps=mms + [bank_dep("tmpq")], inc="sp")
        e2 = p.step("scalar", lambda e, b=b: e.activation(out=sqc[:, 0:s], in_=b[:, 0:s],
                    func=AF.Square), deps=mms + [bank_dep("sqc")], inc="sp")
        bk[oc % 2] = e2
        nm = p.step("tensor", lambda e: e.matmul(banks[3][0:2, 0:s], T["blkones2"][:],
                    sqc[:, 0:s], start=True, stop=True),
                    deps=[e2, bank_dep(3)], inc="tp")
        bk["sqc"] = nm
        v1 = p.step("vector", lambda e: e.tensor_scalar(out=rn2[:, 0:s],
                    in0=banks[3][0:2, 0:s], scalar1=1e-24, scalar2=None, op0=ALU.max),
                    deps=[nm, bank_dep("rn")], inc="vp")
        bk[3] = v1
        s1 = p.step("scalar", lambda e: e.activation(out=rn2[:, 0:s], in_=rn2[:, 0:s],
                    func=AF.Ln), deps=[v1], inc="sp")
        s2 = p.step("scalar", lambda e: e.activation(out=rn2b[:, 0:s], in_=rn2[:, 0:s],
                    func=AF.Exp, scale=-0.5), deps=[s1, bank_dep("rb")], inc="sp")
        bk["rn"] = s2
        rb = p.step("tensor", lambda e: e.matmul(banks[4][:, 0:s], T["bc2"][:],
                    rn2b[:, 0:s], start=True, stop=True), deps=[s2, bank_dep(4)], inc="tp")
        bk["rb"] = rb
        qcol = qscol if oc < 8 else kscol
        ap = p.step("vector", lambda e, qcol=qcol: e.scalar_tensor_tensor(
            out=qnb[:, 0:s], in0=tmpq[:, 0:s], scalar=qcol[:], in1=banks[4][:, 0:s],
            op0=ALU.mult, op1=ALU.mult), deps=[rb, e1, bank_dep("qnb")], inc="vp")
        bk[4] = ap
        bk["tmpq"] = ap
        rm = p.step("tensor", lambda e: e.matmul(banks[5][:, 0:s], T["Mrot"][:],
                    qnb[:, 0:s], start=True, stop=True), deps=[ap, bank_dep(5)], inc="tp")
        re = p.step("scalar", lambda e: e.copy(rotb[:, 0:s], banks[5][:, 0:s]),
                    deps=[rm, bank_dep("rotb")], inc="sp")
        bk[5] = re
        ti_c, ti_s = (0, 1) if oc < 8 else (2, 3)
        c1 = p.step("vector", lambda e, ti_c=ti_c: e.tensor_tensor(
            out=t1b[:, 0:s], in0=qnb[:, 0:s], in1=tabsW[:, ti_c, tb0:tb0 + s],
            op=ALU.mult), deps=[ap, bank_dep("t12")], inc="vp")
        c2 = p.step("vector", lambda e, ti_s=ti_s: e.tensor_tensor(
            out=t2b[:, 0:s], in0=rotb[:, 0:s], in1=tabsW[:, ti_s, tb0:tb0 + s],
            op=ALU.mult), deps=[re], inc="vp")
        bk["qnb"] = c1
        bk["rotb"] = c2
        if oc < 8:
            dstap = qb[:, oc, 0:s]
        else:
            dstap = klin[:, oc - 8, 256:256 + s]
        dd = [sh] if sh is not None else []
        fa = p.step("vector", lambda e, dstap=dstap: e.tensor_tensor(
            out=dstap, in0=t1b[:, 0:s], in1=t2b[:, 0:s], op=ALU.add),
            deps=[c1, c2] + dd, inc="vp")
        bk["t12"] = fa

    ntt = s // 128
    for tt in range(ntt):
        vidx = (vbase + tt) % 6 if vbase >= 0 else tt
        for vh in range(2):
            mms = []
            for dc in range(8):
                mms.append(p.step("tensor", lambda e, tt=tt, vh=vh, dc=dc: e.matmul(
                    banks[2][:, 0:512], y2b[:, dc, 128 * tt:128 * (tt + 1)],
                    qkvW[:, dc, 2048 + 512 * vh:2048 + 512 * (vh + 1)],
                    start=(dc == 0), stop=(dc == 7)),
                    deps=[yv, bank_dep(2)], inc="tp"))
            bk[2] = p.step("scalar", lambda e, vidx=vidx, vh=vh: e.copy(
                vt[vidx][:, 512 * vh:512 * (vh + 1)], banks[2][:, 0:512]),
                deps=mms, inc="sp")

    if is_prologue:
        return

    qk_tok = p.tok("vp")
    for t in range(ntt):
        tau_tile = (t0 + 128 * t) // 128
        m1 = p.step("vector", lambda e, tau_tile=tau_tile: e.tensor_scalar(
            out=Smb[:], in0=T["jiota"][:], scalar1=T["cthr_sb"][:, tau_tile:tau_tile + 1],
            scalar2=None, op0=ALU.is_ge), deps=[qk_tok], inc="vp")
        m2 = p.step("vector", lambda e: e.tensor_tensor(out=Smb[:], in0=Smb[:],
                    in1=T["band01"][:], op=ALU.mult), deps=[m1], inc="vp")
        m3 = p.step("vector", lambda e: e.tensor_scalar(out=mbt[:], in0=Smb[:],
                    scalar1=-1.0, scalar2=3.0e38, op0=ALU.add, op1=ALU.mult),
                    deps=[m2], inc="vp")
        ev_tok = None
        for h in range(16):
            hb = 64 * (h % 2)
            hc = h // 2
            Sb = banks[2 + h % 2]
            sm = p.step("tensor", lambda e, h=h, hb=hb, hc=hc, t=t, Sb=Sb: e.matmul(
                Sb[:, 0:384], qb[hb:hb + 64, hc, 128 * t:128 * (t + 1)],
                klin[hb:hb + 64, hc, 128 * t:128 * t + 384],
                start=True, stop=True), deps=[m3, bank_dep(2 + h % 2), qk_tok], inc="tp")
            ma = p.step("vector", lambda e, Sb=Sb: e.tensor_tensor(out=Smb[:],
                        in0=Sb[:, 0:384], in1=mbt[:], op=ALU.add),
                        deps=[sm, bank_dep("Smb")], inc="vp")
            bk[2 + h % 2] = ma
            mx = p.step("vector", lambda e: e.reduce_max(nmax[:], Smb[:], axis=AX.X,
                        negate=True), deps=[ma], inc="vp")
            ex = p.step("scalar", lambda e, h=h: e.activation(out=Pb[:], in_=Smb[:],
                        func=AF.Exp, bias=nmax[:], accum_out=den16[:, h:h + 1]),
                        deps=[mx, bank_dep("Pb")], inc="sp")
            bk["Smb"] = ex
            l1 = p.step("scalar", lambda e, h=h: e.activation(out=rec16[:, h:h + 1],
                        in_=den16[:, h:h + 1], func=AF.Ln),
                        deps=[ex, bank_dep("rec")], inc="sp")
            l2 = p.step("scalar", lambda e, h=h: e.activation(out=rec16[:, h:h + 1],
                        in_=rec16[:, h:h + 1], func=AF.Exp, scale=-1.0), deps=[l1], inc="sp")
            ts_ = p.step("vector", lambda e, h=h: e.tensor_scalar(out=Ps[:], in0=Pb[:],
                         scalar1=rec16[:, h:h + 1], scalar2=None, op0=ALU.mult),
                         deps=[l2, ex, bank_dep("Ps")], inc="vp")
            bk["Pb"] = ts_
            bk["rec"] = ts_
            pt_toks = []
            for kc in range(3):
                pt_toks.append(p.step("tensor", lambda e, kc=kc: e.transpose(
                    banks[7][:].bitcast(BF16)[:, 128 * kc:128 * (kc + 1)],
                    Ps[:, 128 * kc:128 * (kc + 1)], T["ident_bf"][:]),
                    deps=[ts_, bank_dep(7)], inc="tp"))
            bk["Ps"] = pt_toks[-1]
            pe = p.step("scalar", lambda e: e.copy(PT[:], banks[7][:].bitcast(BF16)[:, 0:384]),
                        deps=pt_toks + [bank_dep("PT")], inc="sp")
            bk[7] = pe
            ab_ = banks[4 + hc // 4]
            ccol = (hc % 4) * 128
            for kc in range(3):
                vidx = (vbase + t - 2 + kc) % 6
                p.step("tensor", lambda e, h=h, hb=hb, kc=kc, vidx=vidx, ab_=ab_,
                       ccol=ccol: e.matmul(
                    ab_[hb:hb + 64, ccol:ccol + 128], vt[vidx][:, 64 * h:64 * (h + 1)],
                    PT[:, 128 * kc:128 * (kc + 1)],
                    start=(kc == 0), stop=(kc == 2)),
                    deps=[pe, bank_dep(4 + hc // 4)], inc="tp")
            bk["PT"] = p.tok("tp")
        e4 = p.step("scalar", lambda e, t=t: e.copy(
            attnb[:, 0:4, 128 * t:128 * (t + 1)],
            banks[4][:, 0:512].rearrange("p (a b) -> p a b", b=128)), deps=[p.tok("tp")], inc="sp")
        e5 = p.step("scalar", lambda e, t=t: e.copy(
            attnb[:, 4:8, 128 * t:128 * (t + 1)],
            banks[5][:, 0:512].rearrange("p (a b) -> p a b", b=128)), deps=[p.tok("tp")], inc="sp")
        bk[4] = e4
        bk[5] = e5

    for oc in range(8):
        b = banks[oc % 2]
        mms = []
        for dc in range(8):
            mms.append(p.step("tensor", lambda e, oc=oc, dc=dc, b=b: e.matmul(
                b[:, 0:s], woW[:, dc, 128 * oc:128 * (oc + 1)], attnb[:, dc, 0:s],
                start=(dc == 0), stop=(dc == 7)),
                deps=[bk[4], bk[5], bank_dep(oc % 2)], inc="tp"))
        di = p.dma(xinoc[:, 0:s], xin_d[128 * oc:128 * (oc + 1), t0:t0 + s],
                   deps=[bank_dep("xin")])
        rr = p.step("vector", lambda e, b=b: e.tensor_tensor(out=xmc[:, 0:s],
                    in0=b[:, 0:s], in1=xinoc[:, 0:s], op=ALU.add),
                    deps=mms + [di, bank_dep("xmc")], inc="vp")
        bk[oc % 2] = rr
        bk["xin"] = rr
        bk["xmc"] = p.dma(DR["xmid"][128 * oc:128 * (oc + 1), t0:t0 + s], xmc[:, 0:s],
                          deps=[rr], queue="gpsimd", sem="gq")

    d1 = p.dma(attnb[:, 0:4, 0:s], mkap(DR["xmid"], [(TAU, 128), (128 * TAU, 4), (1, s)],
               DR["xmid"].offset + t0), queue="gpsimd", sem="gq", deps=[bk["xmc"]])
    d2 = p.dma(attnb[:, 4:8, 0:s], mkap(DR["xmid"], [(TAU, 128), (128 * TAU, 4), (1, s)],
               DR["xmid"].offset + 512 * TAU + t0 - 1), queue="gpsimd", sem="gq")
    yv2 = p.step("vector", lambda e: e.tensor_tensor(
        out=attnb[:, 4:8, 0:s], in0=attnb[:, 4:8, 0:s],
        in1=T["svb"][:, t0:t0 + s].unsqueeze(1).broadcast_to((128, 4, s)),
        op=ALU.mult), deps=[d1, d2], inc="vp")
    ymu, yrs = ln_row_stats(p, T, attnb[:, :, 0:s], sqv[:, :, 0:s], "strow", s, [yv2])
    bn = p.step("tensor", lambda e: e.matmul(banks[6][:, 0:s], T["ones_row"][:],
                strow["nmu"][0:1, 0:s], start=True, stop=True),
                deps=[ymu, yrs, bank_dep(6)], inc="tp")
    br = p.step("tensor", lambda e: e.matmul(banks[7][:, 0:s], T["ones_row"][:],
                strow["rst"][0:1, 0:s], start=True, stop=True),
                deps=[yrs, bank_dep(7)], inc="tp")
    for dc in range(8):
        w1 = p.step("vector", lambda e, dc=dc: e.tensor_tensor(out=tmpq[:, 0:s],
                    in0=attnb[:, dc, 0:s], in1=banks[6][:, 0:s], op=ALU.add),
                    deps=[bn, yv2, bank_dep("tq")], inc="vp")
        w2 = p.step("vector", lambda e: e.tensor_tensor(out=tmpq2[:, 0:s],
                    in0=tmpq[:, 0:s], in1=banks[7][:, 0:s], op=ALU.mult),
                    deps=[w1, br], inc="vp")
        w3 = p.step("vector", lambda e, dc=dc: e.tensor_scalar(out=qb[:, dc, 0:s],
                    in0=tmpq2[:, 0:s], scalar1=gacol[:, dc:dc + 1],
                    scalar2=becol[:, dc:dc + 1], op0=ALU.mult, op1=ALU.add),
                    deps=[w2], inc="vp")
        bk["tq"] = w3
    p.dma(mkap(DR["xn"], [(TAU, 128), (128 * TAU, 8), (1, s)], DR["xn"].offset + t0),
          qb[:, :, 0:s], deps=[bk["tq"]], queue="gpsimd", sem="gq")
    bk[6] = bk["tq"]
    bk[7] = bk["tq"]


def emit_b2_block(p, T, DR, l, xout_d, t0, s, w1W, w2W, hview, qb, glub,
                  xinoc, xmc, b1acol, b1gcol, b2col, banks):
    bk = {}
    dn = p.dma(qb[:, :, 0:s], mkap(DR["xn"], [(TAU, 128), (128 * TAU, 8), (1, s)],
               DR["xn"].offset + t0))
    for ocp in range(22):
        ba, bg = banks[0], banks[1]
        mms = []
        for dc in range(8):
            mms.append(p.step("tensor", lambda e, ocp=ocp, dc=dc: e.matmul(
                ba[:, 0:s], w1W[:, dc, 128 * ocp:128 * (ocp + 1)], qb[:, dc, 0:s],
                start=(dc == 0), stop=(dc == 7)), deps=[dn, bk.get(0)], inc="tp"))
        mmsg = []
        for dc in range(8):
            mmsg.append(p.step("tensor", lambda e, ocp=ocp, dc=dc: e.matmul(
                bg[:, 0:s], w1W[:, dc, 2816 + 128 * ocp:2816 + 128 * (ocp + 1)],
                qb[:, dc, 0:s], start=(dc == 0), stop=(dc == 7)),
                deps=[dn, bk.get(1)], inc="tp"))
        ge = p.step("scalar", lambda e, ocp=ocp: e.activation(out=glub[:, 0:s],
                    in_=bg[:, 0:s], func=AF.Gelu, bias=b1gcol[:, ocp:ocp + 1]),
                    deps=mmsg + [bk.get("glub")], inc="sp")
        bk[1] = ge
        hh = p.step("vector", lambda e, ocp=ocp: e.scalar_tensor_tensor(
            out=hview[:, ocp, 0:s], in0=ba[:, 0:s], scalar=b1acol[:, ocp:ocp + 1],
            in1=glub[:, 0:s], op0=ALU.add, op1=ALU.mult), deps=mms + [ge], inc="vp")
        bk[0] = hh
        bk["glub"] = hh
        bk["h"] = hh
    for oc in range(8):
        b = banks[2 + oc % 2]
        mms = []
        for hc in range(22):
            mms.append(p.step("tensor", lambda e, oc=oc, hc=hc, b=b: e.matmul(
                b[:, 0:s], w2W[:, hc, 128 * oc:128 * (oc + 1)], hview[:, hc, 0:s],
                start=(hc == 0), stop=(hc == 21)),
                deps=[bk["h"], bk.get(2 + oc % 2)], inc="tp"))
        di = p.dma(xinoc[:, 0:s], DR["xmid"][128 * oc:128 * (oc + 1), t0:t0 + s],
                   deps=[bk.get("xi"), bk["h"]])
        rr = p.step("vector", lambda e, oc=oc, b=b: e.scalar_tensor_tensor(
            out=xmc[:, 0:s], in0=b[:, 0:s], scalar=b2col[:, oc:oc + 1],
            in1=xinoc[:, 0:s], op0=ALU.add, op1=ALU.add),
            deps=mms + [di, bk.get("xo")], inc="vp")
        bk[2 + oc % 2] = rr
        bk["xi"] = rr
        bk["xo"] = p.dma(xout_d[128 * oc:128 * (oc + 1), t0:t0 + s], xmc[:, 0:s],
                         deps=[rr], queue="gpsimd", sem="gq")


def phase_final(p, T, DR, OUT, depth, dbg):
    stream = DR["xt0"] if depth % 2 == 0 else DR["xt1"]
    sb = p.sbuf
    ffin = sb("ffin", [128, 8, 128], F32)
    fbf = sb("fbf", [128, 8, 128], BF16)
    fsq = sb("fsq", [128, 8, 128], BF16)
    fx1 = sb("fx1", [128, 8, 128], F32)
    outr = sb("outr", [128, 1024], F32)
    frow = {"nmu": sb("f_nmu", [1, 128], F32), "ex2": sb("f_ex2", [1, 128], F32),
            "musq": sb("f_musq", [1, 128], F32), "var": sb("f_var", [1, 128], F32),
            "rst": sb("f_rst", [1, 128], F32)}
    T["frow"] = frow
    T["tmpA"] = sb("f_tmpA", [128, 128], F32)
    T["tmpB"] = sb("f_tmpB", [128, 128], F32)
    banks = [T[f"pbank{i}"] for i in range(8)]
    for j in range(16):
        tau = 1792 + 128 * j
        d1 = p.dma(ffin[:, :, :], mkap(stream, [(TAU, 128), (128 * TAU, 8), (1, 128)],
                   stream.offset + tau))
        d2 = p.dma(fx1[:, :, :], mkap(DR["x1"], [(TAU, 128), (128 * TAU, 8), (1, 128)],
                   DR["x1"].offset + tau))
        cv = p.step("vector", lambda e: e.tensor_copy(fbf[:], ffin[:]), deps=[d1], inc="vp")
        ymu, yrs = ln_row_stats(p, T, fbf[:, :, :], fsq[:, :, :], "frow", 128, [cv])
        bn = p.step("tensor", lambda e: e.matmul(banks[6][:, 0:128], T["ones_row"][:],
                    frow["nmu"][0:1, :], start=True, stop=True), deps=[ymu, yrs], inc="tp")
        br = p.step("tensor", lambda e: e.matmul(banks[7][:, 0:128], T["ones_row"][:],
                    frow["rst"][0:1, :], start=True, stop=True), deps=[yrs], inc="tp")
        evs = []
        prev_tm = None
        for dc in range(8):
            w1 = p.step("vector", lambda e, dc=dc: e.tensor_tensor(out=T["tmpA"][:, 0:128],
                        in0=fbf[:, dc, :], in1=banks[6][:, 0:128], op=ALU.add),
                        deps=[bn, cv, prev_tm], inc="vp")
            w2 = p.step("vector", lambda e: e.tensor_tensor(out=T["tmpB"][:, 0:128],
                        in0=T["tmpA"][:, 0:128], in1=banks[7][:, 0:128], op=ALU.mult),
                        deps=[w1, br], inc="vp")
            w3 = p.step("vector", lambda e, dc=dc: e.tensor_tensor(out=T["tmpA"][:, 0:128],
                        in0=T["tmpB"][:, 0:128], in1=fx1[:, dc, :], op=ALU.add),
                        deps=[w2, d2], inc="vp")
            tm = p.step("tensor", lambda e, dc=dc: e.transpose(banks[dc % 2][:, 0:128],
                        T["tmpA"][:, 0:128], T["ident_f32"][:]),
                        deps=[w3, evs[dc - 2] if dc >= 2 else None], inc="tp")
            prev_tm = tm
            ev = p.step("scalar", lambda e, dc=dc: e.copy(outr[:, 128 * dc:128 * (dc + 1)],
                        banks[dc % 2][:, 0:128]), deps=[tm], inc="sp")
            evs.append(ev)
        p.dma(OUT[128 * j:128 * (j + 1), :], outr[:], deps=evs, queue="gpsimd", sem="gq")
        p.barrier()
    p.step("sync", lambda e: e.wait_ge(p.sem("gq"), p.semc["gq"]))
    p.step("sync", lambda e: e.wait_ge(p.sem("dq"), p.semc["dq"]))


def host_prep(inputs):
    u = np.ascontiguousarray(inputs["L0_attn_u"])
    per_core = []
    for c in range(NCORES):
        b, half = c // 2, c % 2
        if half == 0:
            u_ssm = np.concatenate([np.zeros((2048, D), np.float32), u[b, :2048]], axis=0)
        else:
            u_ssm = np.ascontiguousarray(u[b])
        goff = -1792 + 2048 * half
        pos = (np.arange(TAU, dtype=np.float32) + goff)[None, :]
        sv = ((np.arange(TAU) + goff - 1) >= 0).astype(np.float32)[None, :]
        cthr = np.zeros((NT, 128), np.float32)
        for i in range(NT):
            cthr[i, :] = 256.0 - (i * 128 + goff)
        per_core.append(dict(u_ssm=u_ssm, pos=pos, sv=sv, cthr=cthr))
    return per_core


def make_inmaps(inputs, per_core):
    maps = []
    for c in range(NCORES):
        pc = per_core[c]
        maps.append({
            "u_ssm": pc["u_ssm"],
            "in_a": inputs["L0_attn_a"], "in_b": inputs["L0_attn_b"], "in_c": inputs["L0_attn_c"],
            "Wqkv": inputs["L0_ffn_Wqkv"], "qs": inputs["L0_ffn_q_scale"],
            "ks": inputs["L0_ffn_k_scale"], "Wo": inputs["L0_ffn_Wo"],
            "ln_g": inputs["L0_ffn_ln_g"], "ln_b": inputs["L0_ffn_ln_b"],
            "W1": inputs["L0_ffn_W1"], "b1": inputs["L0_ffn_b1"],
            "W2": inputs["L0_ffn_W2"], "b2": inputs["L0_ffn_b2"],
            "pos": pc["pos"], "sv": pc["sv"], "cthr": pc["cthr"],
        })
    return maps


def kernel(**inputs):
    per_core = host_prep(inputs)
    nc, _, _ = build_program(depth=DEPTH, debug=False, upto="all")
    maps = make_inmaps(inputs, per_core)
    res = run_bass_kernel_spmd(nc, maps, core_ids=list(range(NCORES)))
    out = np.zeros((4, 4096, D), np.float32)
    for c in range(NCORES):
        b, half = c // 2, c % 2
        out[b, half * 2048:(half + 1) * 2048, :] = res.results[c]["out_slice"]
    return out


# version 03
# speedup vs baseline: 1.1524x; 1.1524x over previous
import numpy as np
from contextlib import ExitStack

import concourse.bass as bass
import concourse.mybir as mybir
import bass_rust
from concourse.bass_utils import run_bass_kernel_spmd

F32 = mybir.dt.float32
F16 = mybir.dt.float16
BF16 = mybir.dt.bfloat16
AF = mybir.ActivationFunctionType
ALU = mybir.AluOpType
AX = mybir.AxisListType

NCORES = 8
TAU = 3840
NT = 30
MC = 32
D = 1024
EXT = [3456, 3200, 2944, 2688, 2432, 2176]
AST = [3840 - e for e in EXT]
DEPTH = 6
GRP = 4
NGRP = 32


def mkap(base_ap, pairs, offset):
    c = base_ap.copy()
    c.ap = bass_rust.VecI64Pair(pairs)
    c.offset = offset
    return c


class PB:
    ENG = ("sync", "tensor", "vector", "scalar", "gpsimd")
    OWN = {"tensor": "tp"}
    DMA_SEMS = ("dq", "gq")

    def __init__(self, nc, es):
        self.nc, self.es = nc, es
        self.ops = {e: [] for e in self.ENG}
        self.semc = {}
        self.sems = {}
        self.waited = {e: {} for e in self.ENG}
        self.sbuf_bytes = 0
        self.ninstr = 0
        self.bank_owner = {}
        self.scopes = []

    def sem(self, name):
        if name not in self.sems:
            self.sems[name] = self.es.enter_context(self.nc.semaphore(name))
            self.semc[name] = 0
        return self.sems[name]

    def tok(self, name):
        return (name, self.semc.get(name, 0))

    def push_scope(self):
        self.scopes.append(ExitStack())

    def pop_scope(self):
        self.scopes.pop().close()

    def sbuf(self, name, shape, dtype):
        st_ = self.scopes[-1] if self.scopes else self.es
        t = st_.enter_context(self.nc.sbuf_tensor(name, list(shape), dtype))
        pp = int(np.prod(shape[1:])) * mybir.dt.size(dtype)
        self.sbuf_bytes += ((pp + 31) // 32) * 32 * 128
        return t

    def psum(self, name, shape, dtype):
        return self.es.enter_context(self.nc.psum_tensor(name, list(shape), dtype))

    def step(self, eng, fn, deps=(), inc=None, amt=1):
        own = self.OWN.get(eng)
        for d in deps:
            if d is None:
                continue
            sn, v = d
            if sn in self.DMA_SEMS:
                v = self.semc.get(sn, 0)
            if v <= 0 or sn == own:
                continue
            w = self.waited[eng].get(sn, 0)
            if v > w:
                s = self.sem(sn)
                self.ops[eng].append(lambda e, s=s, v=v: e.wait_ge(s, v))
                self.waited[eng][sn] = v
                self.ninstr += 1
        self.ninstr += 1
        if inc is not None:
            s = self.sem(inc)
            self.semc[inc] += amt
            v = self.semc[inc]
            self.ops[eng].append(lambda e, fn=fn, s=s, amt=amt: fn(e).then_inc(s, amt))
            return (inc, v)
        self.ops[eng].append(fn)
        return None

    def dma(self, dst, src, deps=(), queue="sync", sem="dq", **kw):
        return self.step(queue, lambda e, dst=dst, src=src, kw=kw: e.dma_start(out=dst, in_=src, **kw),
                         deps=deps, inc=sem, amt=16)

    FENCE = {"tensor": ("tp",), "vector": ("vp",), "scalar": ("sp",),
             "gpsimd": ("gp", "gq"), "sync": ("dq",)}

    def barrier(self, engines=None):
        engines = engines or self.ENG
        covered = set()
        for e in engines:
            for sn in self.FENCE[e]:
                covered.add(sn)
                v = self.semc.get(sn, 0)
                if v > self.waited[e].get(sn, 0):
                    sd = self.sem(sn)
                    self.ops[e].append(lambda eng, sd=sd, v=v: eng.wait_ge(sd, v))
                    self.waited[e][sn] = v
                    self.ninstr += 1
        e0 = engines[0]
        for ds in self.DMA_SEMS:
            if ds in covered:
                continue
            v = self.semc.get(ds, 0)
            if v > self.waited[e0].get(ds, 0):
                sd = self.sem(ds)
                self.ops[e0].append(lambda eng, sd=sd, v=v: eng.wait_ge(sd, v))
                self.waited[e0][ds] = v
                self.ninstr += 1
        s = self.sem("bar")
        tgt = self.semc["bar"] + len(engines)
        self.semc["bar"] = tgt
        for e in engines:
            self.ops[e].append(lambda eng, s=s: eng.nop().then_inc(s, 1))
        for e in engines:
            self.ops[e].append(lambda eng, s=s, v=tgt: eng.wait_ge(s, v))
            self.waited[e]["bar"] = tgt
        self.ninstr += 2 * len(engines)
        return ("bar", tgt)


def build_program(depth=DEPTH, debug=False, upto="all", ngrp=NGRP):
    nc = bass.Bass("TRN2", target_bir_lowering=False)
    es = ExitStack()
    p = PB(nc, es)

    IN = {}
    IN["u_ssm"] = nc.dram_tensor("u_ssm", [4096, D], F32, kind="ExternalInput").ap()
    for nm in ("a", "b", "c"):
        IN[nm] = nc.dram_tensor("in_" + nm, [256, 64], F32, kind="ExternalInput").ap()
    IN["Wqkv"] = nc.dram_tensor("Wqkv", [6, 1024, 3072], F32, kind="ExternalInput").ap()
    IN["qs"] = nc.dram_tensor("qs", [6, 64], F32, kind="ExternalInput").ap()
    IN["ks"] = nc.dram_tensor("ks", [6, 64], F32, kind="ExternalInput").ap()
    IN["Wo"] = nc.dram_tensor("Wo", [6, 1024, 1024], F32, kind="ExternalInput").ap()
    IN["ln_g"] = nc.dram_tensor("ln_g", [6, 1024], F32, kind="ExternalInput").ap()
    IN["ln_b"] = nc.dram_tensor("ln_b", [6, 1024], F32, kind="ExternalInput").ap()
    IN["W1"] = nc.dram_tensor("W1", [6, 1024, 5460], F32, kind="ExternalInput").ap()
    IN["b1"] = nc.dram_tensor("b1", [6, 5460], F32, kind="ExternalInput").ap()
    IN["W2"] = nc.dram_tensor("W2", [6, 2730, 1024], F32, kind="ExternalInput").ap()
    IN["b2"] = nc.dram_tensor("b2", [6, 1024], F32, kind="ExternalInput").ap()
    IN["pos"] = nc.dram_tensor("pos", [1, TAU], F32, kind="ExternalInput").ap()
    IN["sv"] = nc.dram_tensor("sv", [1, TAU], F32, kind="ExternalInput").ap()
    IN["cthr"] = nc.dram_tensor("cthr", [NT, 128], F32, kind="ExternalInput").ap()

    OUT = nc.dram_tensor("out_slice", [2048, D], F32, kind="ExternalOutput").ap()

    dbg_outs = {}

    def dbg(name, shape, dtype=F32):
        if not debug:
            return None
        ap = nc.dram_tensor("dbg_" + name, list(shape), dtype, kind="ExternalOutput").ap()
        dbg_outs[name] = ap
        return ap

    DR = {}
    DR["x1"] = nc.dram_tensor("x1_dram", [D, TAU], F32).ap()
    DR["xt0"] = nc.dram_tensor("xt0_dram", [D, TAU], F32).ap()
    DR["xt1"] = nc.dram_tensor("xt1_dram", [D, TAU], F32).ap()
    DR["ksh"] = nc.dram_tensor("ksh_dram", [256, 384], F16).ap()

    with nc.Block() as block:
        T = {}
        for i in range(8):
            T[f"pbank{i}"] = p.psum(f"pbank{i}", [128, 512], F32)
        phase_static(p, T)
        p.push_scope()
        if upto not in ("static",):
            phase_uln(p, T, IN, dbg)
        if upto not in ("static", "uln"):
            phase_abc(p, T, IN)
            phase_ssm(p, T, DR, dbg, ngrp=ngrp)
        if upto in ("x1", "all", "tf"):
            phase_x1(p, T, IN, DR, dbg)
        p.pop_scope()
        if upto in ("all", "tf"):
            phase_tables(p, T, IN)
            p.push_scope()
            phase_transformer(p, T, IN, DR, dbg, depth)
            p.pop_scope()
            phase_final(p, T, DR, OUT, depth, dbg)
        else:
            phase_out(p, T, DR, OUT, src="x1")
        for ename in PB.ENG:
            lst = p.ops[ename]
            getattr(block, ename)(lambda eng, lst=lst: [fn(eng) for fn in lst])
    es.close()
    return nc, list(dbg_outs.keys()), p


def phase_static(p, T):
    sb = p.sbuf
    for nm, shape, dt in (("iota_a", [128, 128], F32), ("iota_b", [128, 128], F32),
                          ("scr0", [128, 128], F32), ("scr1", [128, 128], F32),
                          ("pcol", [128, 1], F32), ("subdiag", [128, 128], F16),
                          ("superdiag", [128, 128], F16), ("ident_bf", [128, 128], BF16),
                          ("ident_f16", [128, 128], F16), ("ident_f32", [128, 128], F32),
                          ("rev_bf", [128, 128], BF16), ("ones_col", [128, 1], F32),
                          ("ones_row", [1, 128], F32), ("eps_col", [128, 1], F32)):
        T[nm] = sb(nm, shape, dt)

    ia, ib, s0, s1, pc = T["iota_a"], T["iota_b"], T["scr0"], T["scr1"], T["pcol"]
    t0 = p.step("gpsimd", lambda e: e.iota(ia[:], pattern=[[1, 128]], base=1,
                channel_multiplier=-1, allow_small_or_imprecise_dtypes=True), inc="gp")
    t1 = p.step("gpsimd", lambda e: e.iota(ib[:], pattern=[[1, 128]], base=0,
                channel_multiplier=0, allow_small_or_imprecise_dtypes=True), inc="gp")
    t2 = p.step("gpsimd", lambda e: e.iota(pc[:], pattern=[[0, 1]], base=0,
                channel_multiplier=1, allow_small_or_imprecise_dtypes=True), inc="gp")
    p.step("gpsimd", lambda e: e.memset(T["ones_col"][:], 1.0), inc="gp")
    p.step("gpsimd", lambda e: e.memset(T["ones_row"][:], 1.0), inc="gp")
    p.step("gpsimd", lambda e: e.memset(T["eps_col"][:], 1e-5), inc="gp")

    v = p.step("vector", lambda e: e.tensor_scalar(out=s0[:], in0=ib[:], scalar1=64.0,
               scalar2=None, op0=ALU.is_ge), deps=[t1], inc="vp")
    v = p.step("vector", lambda e: e.tensor_scalar(out=pc[:], in0=pc[:], scalar1=64.0,
               scalar2=None, op0=ALU.is_ge), deps=[t2], inc="vp")
    blk = p.step("vector", lambda e: e.tensor_scalar(out=s0[:], in0=s0[:], scalar1=pc[:],
                 scalar2=None, op0=ALU.is_equal), deps=[v], inc="vp")
    v = p.step("vector", lambda e: e.tensor_scalar(out=s1[:], in0=ia[:], scalar1=0.0,
               scalar2=None, op0=ALU.is_equal), deps=[t0], inc="vp")
    v = p.step("vector", lambda e: e.tensor_tensor(out=T["subdiag"][:], in0=s1[:], in1=s0[:],
               op=ALU.mult), deps=[v, blk], inc="vp")
    v = p.step("vector", lambda e: e.tensor_scalar(out=s1[:], in0=ia[:], scalar1=2.0,
               scalar2=None, op0=ALU.is_equal), deps=[v], inc="vp")
    v = p.step("vector", lambda e: e.tensor_tensor(out=T["superdiag"][:], in0=s1[:], in1=s0[:],
               op=ALU.mult), deps=[v], inc="vp")
    for nm in ("ident_bf", "ident_f16", "ident_f32"):
        v = p.step("vector", lambda e, nm=nm: e.tensor_scalar(out=T[nm][:], in0=ia[:],
                   scalar1=1.0, scalar2=None, op0=ALU.is_equal), deps=[t0], inc="vp")
    t5 = p.step("gpsimd", lambda e: e.iota(ib[:], pattern=[[1, 128]], base=-127,
                channel_multiplier=1, allow_small_or_imprecise_dtypes=True),
                deps=[blk], inc="gp")
    p.step("vector", lambda e: e.tensor_scalar(out=T["rev_bf"][:], in0=ib[:], scalar1=0.0,
           scalar2=None, op0=ALU.is_equal), deps=[t5], inc="vp")
    p.barrier()


def phase_abc(p, T, IN):
    sb = p.sbuf
    for nm, shape, dt in (("a_tm", [128, 2, 64], F32), ("b_tm", [128, 2, 64], F32),
                          ("c_tm", [128, 2, 64], F32), ("an16", [128, 2, 64], F16),
                          ("b16", [128, 2, 64], F16), ("c16", [128, 2, 64], F16),
                          ("anorm", [128, 2], F32), ("arecip", [128, 2], F32),
                          ("sel128", [128, 1], F32), ("sm2", [2, 1], F32),
                          ("sm1", [1, 1], F32), ("anT2", [128, 256], F16),
                          ("bT2", [128, 256], F16), ("cT2", [128, 256], F16)):
        T[nm] = sb(nm, shape, dt)

    pgi = [(64, 128), (8192, 2), (1, 64)]
    da = p.dma(T["a_tm"][:], mkap(IN["a"], pgi, 0))
    db = p.dma(T["b_tm"][:], mkap(IN["b"], pgi, 0))
    dc = p.dma(T["c_tm"][:], mkap(IN["c"], pgi, 0))

    v = p.step("vector", lambda e: e.tensor_reduce(out=T["anorm"][:], in_=T["a_tm"][:],
               axis=AX.X, op=ALU.add, apply_absolute_value=True), deps=[da], inc="vp")
    tm = p.step("tensor", lambda e: e.matmul(T["pbank0"][0:2, 0:1], T["anorm"][:],
                T["ones_col"][:], start=True, stop=True), deps=[v], inc="tp")
    v = p.step("vector", lambda e: e.tensor_copy(T["sm2"][:], T["pbank0"][0:2, 0:1]),
               deps=[tm], inc="vp")
    tm = p.step("tensor", lambda e: e.matmul(T["pbank0"][0:1, 4:5], T["sm2"][:],
                T["ones_col"][0:2, :], start=True, stop=True), deps=[v], inc="tp")
    v = p.step("vector", lambda e: e.tensor_scalar(out=T["sm1"][:], in0=T["pbank0"][0:1, 4:5],
               scalar1=1.0 / 256.0, scalar2=1e-4, op0=ALU.mult, op1=ALU.is_gt),
               deps=[tm], inc="vp")
    tm = p.step("tensor", lambda e: e.matmul(T["pbank0"][0:128, 8:9], T["ones_row"][:],
                T["sm1"][:], start=True, stop=True), deps=[v], inc="tp")
    v = p.step("vector", lambda e: e.tensor_copy(T["sel128"][:], T["pbank0"][0:128, 8:9]),
               deps=[tm], inc="vp")
    v1 = p.step("vector", lambda e: e.reciprocal(T["arecip"][:], T["anorm"][:]), deps=[v], inc="vp")
    v1 = p.step("vector", lambda e: e.tensor_scalar(out=T["arecip"][:], in0=T["arecip"][:],
                scalar1=T["sel128"][:], scalar2=None, op0=ALU.mult), deps=[v1], inc="vp")
    v2 = p.step("vector", lambda e: e.tensor_scalar(out=T["sel128"][:], in0=T["sel128"][:],
                scalar1=-1.0, scalar2=1.0, op0=ALU.mult, op1=ALU.add), deps=[v1], inc="vp")
    v = p.step("vector", lambda e: e.tensor_scalar(out=T["arecip"][:], in0=T["arecip"][:],
               scalar1=T["sel128"][:], scalar2=None, op0=ALU.add), deps=[v2], inc="vp")
    van = p.step("vector", lambda e: e.tensor_tensor(
        out=T["an16"][:], in0=T["a_tm"][:],
        in1=T["arecip"][:].unsqueeze(2).broadcast_to((128, 2, 64)), op=ALU.mult),
        deps=[v], inc="vp")
    vb = p.step("vector", lambda e: e.tensor_copy(T["b16"][:], T["b_tm"][:]), deps=[db], inc="vp")
    vc = p.step("vector", lambda e: e.tensor_copy(T["c16"][:], T["c_tm"][:]), deps=[dc], inc="vp")

    ps16 = T["pbank1"][:].bitcast(F16)
    toks = []
    for idx, src in enumerate((T["an16"], T["b16"], T["c16"])):
        for g in range(2):
            col = (idx * 2 + g) * 128
            toks.append(p.step("tensor", lambda e, src=src, g=g, col=col: e.transpose(
                ps16[0:64, col:col + 128], src[:, g, :], T["ident_f16"][:]),
                deps=[van, vb, vc], inc="tp"))
    s1 = p.step("scalar", lambda e: e.copy(T["anT2"][0:64, :], ps16[0:64, 0:256]),
                deps=toks, inc="sp")
    s2 = p.step("scalar", lambda e: e.copy(T["bT2"][0:64, :], ps16[0:64, 256:512]),
                deps=toks, inc="sp")
    s3 = p.step("scalar", lambda e: e.copy(T["cT2"][0:64, :], ps16[0:64, 512:768]),
                deps=toks, inc="sp")
    for nm, s in (("anT2", s1), ("bT2", s2), ("cT2", s3)):
        p.dma(T[nm][64:128, :], T[nm][0:64, :], deps=[s], queue="gpsimd", sem="gq")
    p.barrier()


def phase_uln(p, T, IN, dbg):
    sb = p.sbuf
    T["urev"] = sb("urev", [128, MC, D], BF16)
    ut = [sb(f"ut{i}", [128, D], F32) for i in range(2)]
    usq = sb("usq", [128, D], F32)
    uxc = sb("uxc", [128, D], F32)
    un_bf = [sb(f"un_bf{i}", [128, D], BF16) for i in range(2)]
    T["usq"] = usq
    T["ut0"] = ut[0]
    T["un_bf0"] = un_bf[0]
    st = {nm: sb("ul" + nm, [128, 1], F32) for nm in ("s1", "s2", "negmu", "var", "rstd")}
    d_un = dbg("un", [4096, D])

    prev_rev = [None, None]
    prev_ap = [None, None]
    for m in range(MC):
        par = m % 2
        dtok = p.dma(ut[par][:], IN["u_ssm"][m * 128:(m + 1) * 128, :],
                     deps=[prev_ap[par]])
        ln = emit_ln_stats(p, T, ut[par][:], uxc, usq, st, deps=[dtok])
        ap = p.step("vector", lambda e, par=par: e.tensor_scalar(
            out=un_bf[par][:], in0=uxc[:], scalar1=st["rstd"][:], scalar2=None,
            op0=ALU.mult), deps=[ln, prev_rev[par]], inc="vp")
        prev_ap[par] = ap
        if d_un is not None:
            cast = p.step("scalar", lambda e, par=par: e.copy(usq[:], un_bf[par][:]),
                          deps=[ap], inc="sp")
            p.dma(d_un[m * 128:(m + 1) * 128, :], usq[:], deps=[cast], queue="gpsimd", sem="gq")
        pb = [T["pbank0"], T["pbank1"]] if par == 0 else [T["pbank2"], T["pbank3"]]
        mms = []
        for h in range(2):
            mms.append(p.step("tensor", lambda e, par=par, h=h, pb=pb: e.matmul(
                pb[h][:, 0:512], T["rev_bf"][:], un_bf[par][:, h * 512:(h + 1) * 512],
                start=True, stop=True), deps=[ap, prev_rev[par]], inc="tp"))
        ev = p.step("scalar", lambda e, m=m, pb=pb: e.copy(T["urev"][:, m, 0:512], pb[0][:]),
                    deps=mms, inc="sp")
        ev = p.step("scalar", lambda e, m=m, pb=pb: e.copy(T["urev"][:, m, 512:1024], pb[1][:]),
                    deps=mms, inc="sp")
        prev_rev[par] = p.barrier(("tensor", "vector", "scalar", "sync", "gpsimd"))
        prev_ap[par] = prev_rev[par]
    p.barrier()


def emit_ln_stats(p, T, src_ap, xcbuf, sqbuf, st, deps=()):
    v1 = p.step("vector", lambda e: e.tensor_reduce(out=st["s1"][:], in_=src_ap, axis=AX.X,
                op=ALU.add), deps=deps, inc="vp")
    v2 = p.step("vector", lambda e: e.tensor_scalar(out=st["negmu"][:], in0=st["s1"][:],
                scalar1=-1.0 / D, scalar2=None, op0=ALU.mult), deps=[v1], inc="vp")
    v3 = p.step("vector", lambda e: e.tensor_scalar(out=xcbuf[:], in0=src_ap,
                scalar1=st["negmu"][:], scalar2=None, op0=ALU.add), deps=[v2], inc="vp")
    s4 = p.step("scalar", lambda e: e.activation(out=sqbuf[:], in_=xcbuf[:], func=AF.Square),
                deps=[v3] + list(deps), inc="sp")
    v5 = p.step("vector", lambda e: e.tensor_reduce(out=st["s2"][:], in_=sqbuf[:], axis=AX.X,
                op=ALU.add), deps=[s4], inc="vp")
    s6 = p.step("scalar", lambda e: e.activation(out=st["rstd"][:], in_=st["s2"][:],
                func=AF.Ln, bias=T["eps_col"][:], scale=1.0 / D), deps=[v5], inc="sp")
    s7 = p.step("scalar", lambda e: e.activation(out=st["rstd"][:], in_=st["rstd"][:],
                func=AF.Exp, scale=-0.5), deps=[s6], inc="sp")
    return s7


SSM_STOP = "all"

def phase_ssm(p, T, DR, dbg, ngrp=NGRP):
    sb = p.sbuf
    T["y_all"] = sb("y_all", [128, NT, D], BF16)
    Aa = [sb(f"Aa{q}", [128, 128], F16) for q in range(GRP)]
    Ab = [sb(f"Ab{q}", [128, 128], F16) for q in range(GRP)]
    Ata = [sb(f"Ata{q}", [128, 128], F16) for q in range(GRP)]
    Atb = [sb(f"Atb{q}", [128, 128], F16) for q in range(GRP)]
    U = [sb(f"U{q}", [128, 256], F16) for q in range(GRP)]
    P2 = [sb(f"P2{q}", [128, 128], F16) for q in range(GRP)]
    Pk = [sb(f"Pk{q}", [128, 128], F16) for q in range(GRP)]
    Gt = [[sb(f"Gt{q}_{v}", [128, 128], F16) for v in range(5)] for q in range(GRP)]
    Uy = [[sb(f"Uy{q}_{hh}", [128, 128], F16) for hh in range(2)] for q in range(GRP)]
    H0 = [[sb(f"H0_{q}_{hh}", [128, 128], F16) for hh in range(2)] for q in range(GRP)]
    Xa = [sb(f"Xa{q}", [128, MC, 8], BF16) for q in range(GRP)]
    Xb = [sb(f"Xb{q}", [128, MC, 8], BF16) for q in range(GRP)]
    kstage = sb("kstage", [1, 256], F16)
    zeros16 = sb("zeros16", [128, 384], F16)
    g0 = p.step("gpsimd", lambda e: e.memset(zeros16[:], 0.0), inc="gp")
    p.step("gpsimd", lambda e: e.memset(T["y_all"][:], 0.0), inc="gp")
    zt = p.dma(DR["ksh"][0:128, :], zeros16[:], deps=[g0])
    zt = p.dma(DR["ksh"][128:256, :], zeros16[:], deps=[g0])
    banks = [T[f"pbank{i}"] for i in range(8)]
    d_ksh = dbg("ksh", [256, 384])
    d_y = dbg("yall", [TAU, D])

    for g in range(ngrp):
        pairs = list(range(g * GRP, (g + 1) * GRP))
        itoks = []
        for q, pp in enumerate(pairs):
            h0, h1 = 2 * pp, 2 * pp + 1
            vA = p.step("vector", lambda e, q=q: e.tensor_copy(Aa[q][:], T["subdiag"][:]), inc="vp")
            vAt = p.step("vector", lambda e, q=q: e.tensor_copy(Ata[q][:], T["superdiag"][:]), inc="vp")
            itoks.append(p.step("vector", lambda e, q=q, h0=h0: e.tensor_copy(
                Aa[q][0:64, 63:64], T["anT2"][0:64, h0:h0 + 1]), deps=[vA], inc="vp"))
            itoks.append(p.step("vector", lambda e, q=q, h1=h1: e.tensor_copy(
                Aa[q][64:128, 127:128], T["anT2"][64:128, h1:h1 + 1]), deps=[vA], inc="vp"))
            itoks.append(p.dma(Ata[q][63:64, 0:64],
                               T["an16"][h0 % 128:h0 % 128 + 1, h0 // 128, :],
                               deps=[vAt], queue="gpsimd", sem="gq"))
            itoks.append(p.dma(Ata[q][127:128, 64:128],
                               T["an16"][h1 % 128:h1 % 128 + 1, h1 // 128, :],
                               deps=[vAt], queue="gpsimd", sem="gq"))
            vz = p.step("vector", lambda e, q=q: e.tensor_copy(U[q][:, 0:128], zeros16[:, 0:128]),
                        deps=[g0], inc="vp")
            p.step("vector", lambda e, q=q: e.tensor_copy(U[q][:, 128:256], zeros16[:, 0:128]),
                   deps=[g0], inc="vp")
            itoks.append(p.step("vector", lambda e, q=q, h0=h0: e.tensor_copy(
                U[q][0:64, 0:1], T["cT2"][0:64, h0:h0 + 1]), deps=[vz], inc="vp"))
            itoks.append(p.step("vector", lambda e, q=q, h1=h1: e.tensor_copy(
                U[q][64:128, 0:1], T["cT2"][64:128, h1:h1 + 1]), deps=[vz], inc="vp"))
            vz2 = p.step("vector", lambda e, q=q: e.tensor_copy(P2[q][:], zeros16[:, 0:128]),
                         deps=[g0], inc="vp")
            itoks.append(p.step("vector", lambda e, q=q, h0=h0: e.tensor_copy(
                P2[q][0:64, 0:1], T["bT2"][0:64, h0:h0 + 1]), deps=[vz2], inc="vp"))
            itoks.append(p.step("vector", lambda e, q=q, h1=h1: e.tensor_copy(
                P2[q][64:128, 0:1], T["bT2"][64:128, h1:h1 + 1]), deps=[vz2], inc="vp"))
        init_tok = [p.tok("vp"), p.tok("gq")]
        if SSM_STOP == "init":
            p.barrier(); continue

        Acur, Anxt, Atcur, Atnxt = Aa, Ab, Ata, Atb
        ev_last = None
        nv = int(SSM_STOP[5:]) if SSM_STOP.startswith("chain") and SSM_STOP[5:].isdigit() else (1 if SSM_STOP == "chainA" else (4 if SSM_STOP == "chainB" else 8))
        for v in range(nv):
            w = 1 << v
            mm_toks = []
            for q in range(GRP):
                ps = banks[q]
                dep = [ev_last] + init_tok if ev_last else init_tok
                mm_toks.append(p.step("tensor", lambda e, q=q, w=w, ps=ps, Ac=Acur: e.matmul(
                    ps[:, 0:w], Ac[q][:], U[q][:, 0:w], start=True, stop=True),
                    deps=dep, inc="tp"))
                if v < 7:
                    mm_toks.append(p.step("tensor", lambda e, q=q, w=w, ps=ps, At_=Atcur: e.matmul(
                        ps[:, 128:128 + w], At_[q][:], P2[q][:, 0:w],
                        start=True, stop=True), inc="tp"))
                    mm_toks.append(p.step("tensor", lambda e, q=q, ps=ps, Ac=Acur, At_=Atcur: e.matmul(
                        ps[:, 256:384], At_[q][:], Ac[q][:], start=True, stop=True),
                        inc="tp"))
                    mm_toks.append(p.step("tensor", lambda e, q=q, ps=ps, Ac=Acur, At_=Atcur: e.matmul(
                        ps[:, 384:512], Ac[q][:], At_[q][:], start=True, stop=True),
                        inc="tp"))
            evs = []
            for q in range(GRP):
                ps = banks[q]
                evs.append(p.step("scalar", lambda e, q=q, w=w, ps=ps: e.copy(
                    U[q][:, w:2 * w], ps[:, 0:w]), deps=mm_toks, inc="sp"))
                if v < 7:
                    evs.append(p.step("scalar", lambda e, q=q, w=w, ps=ps: e.copy(
                        P2[q][:, w:2 * w], ps[:, 128:128 + w]), deps=mm_toks, inc="sp"))
                    evs.append(p.step("scalar", lambda e, q=q, ps=ps, An=Anxt: e.copy(
                        An[q][:], ps[:, 256:384]), deps=mm_toks, inc="sp"))
                    evs.append(p.step("scalar", lambda e, q=q, ps=ps, Atn=Atnxt: e.copy(
                        Atn[q][:], ps[:, 384:512]), deps=mm_toks, inc="sp"))
            ev_last = p.barrier()
            if v < 7:
                Acur, Anxt = Anxt, Acur
                Atcur, Atnxt = Atnxt, Atcur
        if SSM_STOP.startswith("chain") and SSM_STOP not in ("chain",):
            p.barrier(); continue
        for q in range(GRP):
            p.step("vector", lambda e, q=q, At_=Atcur: e.tensor_copy(Gt[q][0][:], At_[q][:]),
                   deps=[ev_last], inc="vp")
        for v in range(4):
            mm_toks = []
            for q in range(GRP):
                ps = banks[q]
                mm_toks.append(p.step("tensor", lambda e, q=q, ps=ps, Ac=Acur, At_=Atcur: e.matmul(
                    ps[:, 256:384], At_[q][:], Ac[q][:], start=True, stop=True),
                    deps=[ev_last, p.tok("vp")], inc="tp"))
                mm_toks.append(p.step("tensor", lambda e, q=q, ps=ps, Ac=Acur, At_=Atcur: e.matmul(
                    ps[:, 384:512], Ac[q][:], At_[q][:], start=True, stop=True), inc="tp"))
            for q in range(GRP):
                ps = banks[q]
                p.step("scalar", lambda e, q=q, ps=ps, An=Anxt: e.copy(An[q][:], ps[:, 256:384]),
                       deps=mm_toks, inc="sp")
                p.step("scalar", lambda e, q=q, ps=ps, Atn=Atnxt: e.copy(Atn[q][:], ps[:, 384:512]),
                       deps=mm_toks, inc="sp")
            ev_last = p.barrier()
            Acur, Anxt = Anxt, Acur
            Atcur, Atnxt = Atnxt, Atcur
            for q in range(GRP):
                p.step("vector", lambda e, q=q, v=v, At_=Atcur: e.tensor_copy(Gt[q][v + 1][:], At_[q][:]),
                       deps=[ev_last], inc="vp")
        gt_tok = p.tok("vp")
        if SSM_STOP in ("chain", "gt"):
            p.barrier(); continue

        uy_toks = []
        for q in range(GRP):
            for hh in range(2):
                base = 64 * hh
                g1 = p.step("gpsimd", lambda e, q=q, hh=hh: e.memset(Uy[q][hh][:], 0.0),
                            deps=[ev_last], inc="gp")
                uy_toks.append(p.step("scalar", lambda e, q=q, hh=hh, base=base: e.copy(
                    Uy[q][hh][base:base + 64, :], U[q][base:base + 64, 1:129]),
                    deps=[g1, ev_last, gt_tok], inc="sp"))

        pk_toks = []
        for q in range(GRP):
            ps16 = banks[q][:].bitcast(F16)
            tm = p.step("tensor", lambda e, q=q, ps16=ps16: e.transpose(
                ps16[:, 0:128], P2[q][:], T["ident_f16"][:]), deps=[ev_last, gt_tok], inc="tp")
            pk_toks.append(p.step("scalar", lambda e, q=q, ps16=ps16: e.copy(
                Pk[q][:], ps16[:, 0:128]), deps=[tm], inc="sp"))

        ktoks = []
        prev_kdma = None
        for q, pp in enumerate(pairs):
            for hh in range(2):
                h = 2 * pp + hh
                base = 64 * hh
                ps = banks[q]
                tm = p.step("tensor", lambda e, q=q, h=h, base=base, ps=ps: e.matmul(
                    ps[0:1, 0:256], T["bT2"][base:base + 64, h:h + 1],
                    U[q][base:base + 64, :], start=True, stop=True),
                    deps=[ev_last, gt_tok, pk_toks[-1], prev_kdma], inc="tp")
                ev = p.step("scalar", lambda e, ps=ps: e.copy(kstage[:], ps[0:1, 0:256]),
                            deps=[tm, prev_kdma], inc="sp")
                prev_kdma = p.dma(DR["ksh"][h:h + 1, 127:383], kstage[:], deps=[ev],
                                  queue="gpsimd", sem="gq")
                ktoks.append(prev_kdma)
        if SSM_STOP == "ksh":
            p.barrier(); continue
        htoks = []
        for q, pp in enumerate(pairs):
            for hh in range(2):
                h = 2 * pp + hh
                hsrc = mkap(DR["ksh"], [(1, 128), (1, 128)], h * 384)
                htoks.append(p.dma(H0[q][hh][:], hsrc, deps=[ktoks[-1]]))
        if SSM_STOP == "h0":
            p.barrier(); continue

        urev_flat = T["urev"][:]
        for q, pp in enumerate(pairs):
            psz = banks[q]
            zmm = p.step("tensor", lambda e, q=q, pp=pp, psz=psz: e.matmul(
                psz[:, 0:256], Pk[q][:],
                T["urev"][:, :, 8 * pp:8 * pp + 8],
                start=True, stop=True), deps=[pk_toks[-1], ktoks[-1]], inc="tp")
            ztok = p.step("scalar", lambda e, q=q, psz=psz: e.copy(
                Xa[q][:], psz[:, 0:256]), deps=[zmm], inc="sp")
            if SSM_STOP == "z":
                continue
            Xc, Xn = Xa, Xb
            for v in range(5 if SSM_STOP != "scan1" else 1):
                sh = 1 << v
                ps = banks[q]
                m2 = p.step("tensor", lambda e, q=q, v=v, sh=sh, ps=ps, Xc=Xc: e.matmul(
                    ps[:, 8 * sh:256], Gt[q][v][:],
                    Xc[q][:, 0:MC - sh, :],
                    start=True, stop=True), deps=[ztok, gt_tok], inc="tp")
                vt = p.step("vector", lambda e, q=q, sh=sh, ps=ps, Xc=Xc, Xn=Xn: e.tensor_tensor(
                    out=Xn[q][:, sh:MC, :],
                    in0=ps[:, 8 * sh:256],
                    in1=Xc[q][:, sh:MC, :], op=ALU.add),
                    deps=[m2], inc="vp")
                ztok = p.step("vector", lambda e, q=q, sh=sh, Xc=Xc, Xn=Xn: e.tensor_copy(
                    Xn[q][:, 0:sh, :],
                    Xc[q][:, 0:sh, :]), deps=[vt], inc="vp")
                Xc, Xn = Xn, Xc
            for hh in range(2):
                if SSM_STOP in ("scan", "scan1"):
                    continue
                h = 2 * pp + hh
                base = 64 * hh
                psy = banks[q]
                yc = 256 + 120 * hh
                m1 = p.step("tensor", lambda e, q=q, hh=hh, pp=pp, psy=psy, yc=yc: e.matmul(
                    psy[:, yc:yc + 120], H0[q][hh][:],
                    T["urev"][:, 2:32, 8 * pp + 4 * hh:8 * pp + 4 * hh + 4],
                    start=True, stop=False), deps=[htoks[-1], ztok], inc="tp")
                m2 = p.step("tensor", lambda e, q=q, hh=hh, psy=psy, Xc=Xc, yc=yc: e.matmul(
                    psy[:, yc:yc + 120], Uy[q][hh][:],
                    Xc[q][:, 1:31, 4 * hh:4 * hh + 4],
                    start=False, stop=True), deps=[uy_toks[-1]], inc="tp")
                p.step("scalar", lambda e, h=h, psy=psy, yc=yc: e.copy(
                    T["y_all"][:, :, 4 * h:4 * h + 4],
                    psy[:, yc:yc + 120]), deps=[m1, m2], inc="sp")
        p.barrier()

    if d_ksh is not None:
        ks32 = T["ut0"]
        ks16 = T["un_bf0"][:].bitcast(F16)
        for half in range(2):
            dt = p.dma(ks16[:, 0:384], DR["ksh"][128 * half:128 * (half + 1), :])
            cv = p.step("scalar", lambda e: e.copy(ks32[:, 0:384], ks16[:, 0:384]),
                        deps=[dt], inc="sp")
            p.dma(d_ksh[128 * half:128 * (half + 1), :], ks32[:, 0:384], deps=[cv],
                  queue="gpsimd", sem="gq")
            p.barrier(("scalar", "gpsimd", "sync"))
    if d_y is not None:
        ycast = T["usq"]
        for j in range(NT):
            v = p.step("scalar", lambda e, j=j: e.copy(ycast[:], T["y_all"][:, j, :]),
                       deps=[p.tok("gq")], inc="sp")
            p.dma(d_y[j * 128:(j + 1) * 128, :], ycast[:], deps=[v], queue="gpsimd", sem="gq")
            p.barrier(("scalar", "gpsimd"))
    p.barrier()


def phase_x1(p, T, IN, DR, dbg):
    sb = p.sbuf
    uxt = [sb(f"uxt{i}", [128, D], F32) for i in range(2)]
    ysq = sb("ysq", [128, D], F32)
    yxc = sb("yxc", [128, D], F32)
    x1t = sb("x1t", [128, D], F32)
    xt0t = sb("xt0t", [128, D], F32)
    xf = [sb(f"xf{i}", [128, 128], F32) for i in range(2)]
    st = {nm: sb("x1" + nm, [128, 1], F32) for nm in ("s1", "s2", "negmu", "var", "rstd")}

    prev = [None, None]
    prev_tr = None
    for j in range(NT):
        par = j % 2
        dtok = p.dma(uxt[par][:], IN["u_ssm"][j * 128 + 256:(j + 1) * 128 + 256, :],
                     deps=[prev[par]])
        ytile = T["y_all"][:, j, :]
        ln = emit_ln_stats(p, T, ytile, yxc, ysq, st, deps=[prev_tr])
        a1 = p.step("vector", lambda e: e.tensor_scalar(
            out=x1t[:], in0=yxc[:], scalar1=st["rstd"][:], scalar2=None,
            op0=ALU.mult), deps=[ln], inc="vp")
        a2 = p.step("vector", lambda e, par=par: e.tensor_tensor(
            out=x1t[:], in0=x1t[:], in1=uxt[par][:], op=ALU.add), deps=[a1, dtok], inc="vp")
        prev[par] = a2
        ln2 = emit_ln_stats(p, T, x1t[:], yxc, ysq, st, deps=[a2])
        a3 = p.step("vector", lambda e: e.tensor_scalar(
            out=xt0t[:], in0=yxc[:], scalar1=st["rstd"][:], scalar2=None,
            op0=ALU.mult), deps=[ln2], inc="vp")
        evs = []
        bank_free = {}
        xf_free = {}
        for (src, dst, tok, fpar) in ((x1t, DR["x1"], a2, 0), (xt0t, DR["xt0"], a3, 1)):
            for cch in range(8):
                bid = 4 + (cch % 2) + 2 * fpar
                ps = T[f"pbank{bid}"]
                tm = p.step("tensor", lambda e, src=src, cch=cch, ps=ps: e.transpose(
                    ps[:, 0:128], src[:, cch * 128:(cch + 1) * 128], T["ident_f32"][:]),
                    deps=[tok, prev_tr, bank_free.get(bid)], inc="tp")
                ev = p.step("scalar", lambda e, ps=ps, fpar=fpar: e.copy(xf[fpar][:], ps[:, 0:128]),
                            deps=[tm, bank_free.get(bid), xf_free.get(fpar)], inc="sp")
                bank_free[bid] = ev
                dt = p.dma(dst[cch * 128:(cch + 1) * 128, j * 128:(j + 1) * 128],
                           xf[fpar][:], deps=[ev], queue="gpsimd", sem="gq")
                xf_free[fpar] = dt
                evs.append(dt)
        prev_tr = p.barrier()
    p.barrier()


def phase_out(p, T, DR, OUT, src="xt0"):
    xin = p.sbuf("out_xin", [128, 128], F32)
    xrow = p.sbuf("out_xrow", [128, D], F32)
    final_tok = None
    for j in range(16):
        evs = []
        prev_dma = None
        bank_free = {}
        for cch in range(8):
            bid = cch % 4
            ps = T[f"pbank{bid}"]
            dt = p.dma(xin[:], DR[src][cch * 128:(cch + 1) * 128,
                       1792 + j * 128:1792 + (j + 1) * 128], deps=[prev_dma])
            tm = p.step("tensor", lambda e, ps=ps: e.transpose(
                ps[:, 0:128], xin[:], T["ident_f32"][:]),
                deps=[dt, bank_free.get(bid)], inc="tp")
            ev = p.step("scalar", lambda e, cch=cch, ps=ps: e.copy(
                xrow[:, cch * 128:(cch + 1) * 128], ps[:, 0:128]), deps=[tm], inc="sp")
            bank_free[bid] = ev
            evs.append(ev)
            prev_dma = tm
        final_tok = p.dma(OUT[j * 128:(j + 1) * 128, :], xrow[:], deps=evs,
                          queue="gpsimd", sem="gq")
        p.barrier(("tensor", "scalar", "gpsimd", "sync"))
    p.step("sync", lambda e: e.wait_ge(p.sem("gq"), p.semc["gq"]))



def phase_tables(p, T, IN):
    sb = p.sbuf
    T["Mrot"] = sb("Mrot", [128, 128], F16)
    T["band01"] = sb("band01", [128, 384], F32)
    T["jiota"] = sb("jiota", [128, 384], F32)
    T["if_row"] = sb("if_row", [1, 128], F32)
    T["lb128_row"] = sb("lb128_row", [1, 128], F32)
    T["lbm16col"] = sb("lbm16col", [128, 1], F32)
    T["lbp16col"] = sb("lbp16col", [128, 1], F32)
    T["onecol_bf"] = sb("onecol_bf", [128, 1], BF16)
    T["blkones2"] = sb("blkones2", [128, 2], BF16)
    T["bc2"] = sb("bc2", [2, 128], BF16)
    T["pihalf"] = sb("pihalf", [128, 1], F32)
    T["deneps"] = sb("deneps", [128, 1], F32)
    T["cthr_sb"] = sb("cthr_sb", [128, NT], F32)
    T["rowst"] = sb("rowst", [1, 512], F32)
    ia, ib, s0, s1 = T["iota_a"], T["iota_b"], T["scr0"], T["scr1"]

    g1 = p.step("gpsimd", lambda e: e.memset(T["pihalf"][:], np.pi / 2), inc="gp")
    g1 = p.step("gpsimd", lambda e: e.memset(T["deneps"][:], 1e-30), inc="gp")
    g1 = p.step("gpsimd", lambda e: e.memset(T["onecol_bf"][:], 1.0), inc="gp")

    g2 = p.step("gpsimd", lambda e: e.iota(T["if_row"][:].rearrange("a (b c) -> a b c", b=4),
               pattern=[[0, 4], [1, 32]], base=0, channel_multiplier=0,
               allow_small_or_imprecise_dtypes=True), inc="gp")
    v2 = p.step("vector", lambda e: e.tensor_scalar(out=T["lb128_row"][:], in0=T["if_row"][:],
                scalar1=2.0 / 89.6, scalar2=25.6 / 89.6, op0=ALU.mult, op1=ALU.add),
                deps=[g2], inc="vp")
    s2 = p.step("scalar", lambda e: e.activation(out=T["lb128_row"][:], in_=T["lb128_row"][:],
                func=AF.Ln), deps=[v2], inc="sp")
    tmc = p.step("tensor", lambda e: e.transpose(T["pbank0"][:, 0:1],
                 T["lb128_row"][:], T["ident_f32"][0:1, 0:1]), deps=[s2], inc="tp")
    vv = p.step("vector", lambda e: e.tensor_scalar(out=T["lbm16col"][:],
                in0=T["pbank0"][:, 0:1], scalar1=-16.0, scalar2=None, op0=ALU.mult),
                deps=[tmc], inc="vp")
    vv2 = p.step("vector", lambda e: e.tensor_scalar(out=T["lbp16col"][:],
                 in0=T["pbank0"][:, 0:1], scalar1=16.0, scalar2=None, op0=ALU.mult),
                 deps=[tmc], inc="vp")
    v3 = p.step("vector", lambda e: e.tensor_scalar(out=T["lb128_row"][:],
                in0=T["lb128_row"][:], scalar1=1.0 / 128.0, scalar2=None, op0=ALU.mult),
                deps=[tmc], inc="vp")
    s3 = p.step("scalar", lambda e: e.activation(out=T["if_row"][:], in_=T["if_row"][:],
                func=AF.Exp, scale=-np.log(10000.0) / 32.0), deps=[g2, v2], inc="sp")

    g3 = p.step("gpsimd", lambda e: e.iota(ia[:], pattern=[[1, 128]], base=0,
                channel_multiplier=-1, allow_small_or_imprecise_dtypes=True), inc="gp")
    g4 = p.step("gpsimd", lambda e: e.iota(ib[:].rearrange("p (a b) -> p a b", a=2),
                pattern=[[0, 2], [1, 64]], base=0, channel_multiplier=0,
                allow_small_or_imprecise_dtypes=True), inc="gp")
    v = p.step("vector", lambda e: e.tensor_scalar(out=ib[:], in0=ib[:], scalar1=32.0,
               scalar2=None, op0=ALU.is_ge), deps=[g4], inc="vp")
    g5 = p.step("gpsimd", lambda e: e.iota(T["pcol"][:], pattern=[[0, 1]], base=0,
                channel_multiplier=1, allow_small_or_imprecise_dtypes=True), inc="gp")
    v3b = p.step("vector", lambda e: e.tensor_scalar(out=T["pcol"][:], in0=T["pcol"][:],
                 scalar1=64.0, scalar2=None, op0=ALU.is_ge), deps=[g5], inc="vp")
    g6 = p.step("gpsimd", lambda e: e.iota(s0[:], pattern=[[1, 128]], base=0,
                channel_multiplier=0, allow_small_or_imprecise_dtypes=True), inc="gp")
    v4 = p.step("vector", lambda e: e.tensor_scalar(out=s0[:], in0=s0[:], scalar1=64.0,
                scalar2=None, op0=ALU.is_ge), deps=[g6], inc="vp")
    v5 = p.step("vector", lambda e: e.tensor_scalar(out=s0[:], in0=s0[:],
                scalar1=T["pcol"][:], scalar2=None, op0=ALU.is_equal), deps=[v4, v3b], inc="vp")
    v6 = p.step("vector", lambda e: e.tensor_scalar(out=s1[:], in0=ia[:], scalar1=32.0,
                scalar2=None, op0=ALU.is_equal), deps=[g3], inc="vp")
    v7 = p.step("vector", lambda e: e.tensor_tensor(out=s1[:], in0=s1[:], in1=ib[:],
                op=ALU.mult), deps=[v6, v], inc="vp")
    v8 = p.step("vector", lambda e: e.tensor_scalar(out=ib[:], in0=ib[:], scalar1=-1.0,
                scalar2=1.0, op0=ALU.mult, op1=ALU.add), deps=[v7], inc="vp")
    v9 = p.step("vector", lambda e: e.tensor_scalar(out=ia[:], in0=ia[:], scalar1=-32.0,
                scalar2=None, op0=ALU.is_equal), deps=[v8], inc="vp")
    v10 = p.step("vector", lambda e: e.tensor_tensor(out=ia[:], in0=ia[:], in1=ib[:],
                 op=ALU.mult), deps=[v9], inc="vp")
    v11 = p.step("vector", lambda e: e.tensor_tensor(out=s1[:], in0=s1[:], in1=ia[:],
                 op=ALU.subtract), deps=[v10], inc="vp")
    p.step("vector", lambda e: e.tensor_tensor(out=T["Mrot"][:], in0=s1[:], in1=s0[:],
           op=ALU.mult), deps=[v11, v5], inc="vp")

    g8 = p.step("gpsimd", lambda e: e.iota(T["band01"][:], pattern=[[1, 384]], base=0,
                channel_multiplier=-1, allow_small_or_imprecise_dtypes=True), inc="gp")
    v12 = p.step("vector", lambda e: e.tensor_scalar(out=T["jiota"][:], in0=T["band01"][:],
                 scalar1=0.0, scalar2=None, op0=ALU.is_ge), deps=[g8], inc="vp")
    v13 = p.step("vector", lambda e: e.tensor_scalar(out=T["band01"][:], in0=T["band01"][:],
                 scalar1=256.0, scalar2=None, op0=ALU.is_le), deps=[v12], inc="vp")
    v14 = p.step("vector", lambda e: e.tensor_tensor(out=T["band01"][:], in0=T["band01"][:],
                 in1=T["jiota"][:], op=ALU.mult), deps=[v13], inc="vp")
    g7 = p.step("gpsimd", lambda e: e.iota(T["jiota"][:], pattern=[[1, 384]], base=0,
                channel_multiplier=0, allow_small_or_imprecise_dtypes=True),
                deps=[v14], inc="gp")

    g10 = p.step("gpsimd", lambda e: e.iota(T["pcol"][:], pattern=[[0, 1]], base=0,
                 channel_multiplier=1, allow_small_or_imprecise_dtypes=True), inc="gp")
    v18 = p.step("vector", lambda e: e.tensor_scalar(out=T["blkones2"][:, 1:2],
                 in0=T["pcol"][:], scalar1=64.0, scalar2=None, op0=ALU.is_ge),
                 deps=[g10], inc="vp")
    v19 = p.step("vector", lambda e: e.tensor_scalar(out=T["blkones2"][:, 0:1],
                 in0=T["pcol"][:], scalar1=64.0, scalar2=None, op0=ALU.is_lt),
                 deps=[g10], inc="vp")
    tmm = p.step("tensor", lambda e: e.transpose(
        T["pbank1"][:].bitcast(BF16)[0:2, 0:128], T["blkones2"][:], T["ident_bf"][:]),
        deps=[v18, v19], inc="tp")
    p.step("scalar", lambda e: e.copy(T["bc2"][:], T["pbank1"][:].bitcast(BF16)[0:2, 0:128]),
           deps=[tmm], inc="sp")
    T["tstage"] = sb("tstage", [64, 128], F32)
    dct = p.dma(T["tstage"][0:NT, :], IN["cthr"])
    tc = p.step("tensor", lambda e: e.transpose(T["pbank2"][:, 0:NT],
                T["tstage"][0:NT, :], T["ident_f32"][0:NT, 0:NT]),
                deps=[dct, p.tok("sp")], inc="tp")
    p.step("scalar", lambda e: e.copy(T["cthr_sb"][:], T["pbank2"][:, 0:NT]),
           deps=[tc], inc="sp")
    T["svb"] = sb("svb", [128, TAU], BF16)
    prev_mm = tc
    prev_cp = p.tok("sp")
    for i in range(8):
        c0 = i * 480
        d = p.dma(T["rowst"][0:1, 0:480], IN["sv"][0:1, c0:c0 + 480], deps=[prev_mm])
        prev_mm = p.step("tensor", lambda e, c0=c0: e.matmul(
            T["pbank3"][:, 0:480], T["ones_row"][:], T["rowst"][0:1, 0:480],
            start=True, stop=True), deps=[d, prev_cp], inc="tp")
        prev_cp = p.step("scalar", lambda e, c0=c0: e.copy(
            T["svb"][:, c0:c0 + 480], T["pbank3"][:, 0:480]),
            deps=[prev_mm], inc="sp")
    p.barrier()


def ln_row_stats(p, T, y2view, sqview, nm, n, deps):
    pbs = T["pbank6"]
    pbs2 = T["pbank7"]
    mms = []
    for dc in range(8):
        mms.append(p.step("tensor", lambda e, dc=dc: e.matmul(
            pbs[0:1, 0:n], T["onecol_bf"][:], y2view[:, dc, :],
            start=(dc == 0), stop=(dc == 7)), deps=deps, inc="tp"))
    sq = p.step("scalar", lambda e: e.activation(out=sqview[:], in_=y2view[:],
                func=AF.Square), deps=deps, inc="sp")
    mms2 = []
    for dc in range(8):
        mms2.append(p.step("tensor", lambda e, dc=dc: e.matmul(
            pbs2[0:1, 0:n], T["onecol_bf"][:], sqview[:, dc, :],
            start=(dc == 0), stop=(dc == 7)), deps=[sq], inc="tp"))
    st = T[nm]
    v1 = p.step("vector", lambda e: e.tensor_scalar(out=st["nmu"][0:1, 0:n],
                in0=pbs[0:1, 0:n], scalar1=-1.0 / D, scalar2=None, op0=ALU.mult),
                deps=mms, inc="vp")
    v2 = p.step("vector", lambda e: e.tensor_scalar(out=st["ex2"][0:1, 0:n],
                in0=pbs2[0:1, 0:n], scalar1=1.0 / D, scalar2=None, op0=ALU.mult),
                deps=mms2, inc="vp")
    v3 = p.step("vector", lambda e: e.tensor_tensor(out=st["musq"][0:1, 0:n],
                in0=st["nmu"][0:1, 0:n], in1=st["nmu"][0:1, 0:n], op=ALU.mult),
                deps=[v1], inc="vp")
    v4 = p.step("vector", lambda e: e.tensor_tensor(out=st["var"][0:1, 0:n],
                in0=st["ex2"][0:1, 0:n], in1=st["musq"][0:1, 0:n], op=ALU.subtract),
                deps=[v2, v3], inc="vp")
    s1 = p.step("scalar", lambda e: e.activation(out=st["rst"][0:1, 0:n],
                in_=st["var"][0:1, 0:n], func=AF.Ln, bias=T["eps_col"][0:1, :]),
                deps=[v4], inc="sp")
    s2 = p.step("scalar", lambda e: e.activation(out=st["rst"][0:1, 0:n],
                in_=st["rst"][0:1, 0:n], func=AF.Exp, scale=-0.5), deps=[s1], inc="sp")
    return v1, s2



def phase_transformer(p, T, IN, DR, dbg, depth):
    sb = p.sbuf
    OVL = sb("OVL", [128, 70144], BF16)
    qkvW = OVL[:, 0:24576].rearrange("p (a b) -> p a b", b=3072)
    woW = OVL[:, 24576:32768].rearrange("p (a b) -> p a b", b=1024)
    tabsW = OVL[:, 32768:47872].rearrange("p (a b) -> p a b", b=3776)
    y2b = OVL[:, 47872:51968].rearrange("p (a b) -> p a b", b=512)
    klin = OVL[:, 51968:58112].rearrange("p (a b) -> p a b", b=768)
    vt = [OVL[:, 58112 + 1024 * i:58112 + 1024 * (i + 1)] for i in range(6)]
    kshift = OVL[:, 64256:66304].rearrange("p (a b) -> p a b", b=256)
    Smb = OVL[:, 66304:67072].bitcast(F32)
    mbt = OVL[:, 67072:67456]
    Pb = OVL[:, 67456:67840]
    Ps = OVL[:, 67840:68224]
    PT = OVL[:, 68224:68608]
    rn2 = OVL[0:2, 68608:69632].bitcast(F32)
    rn2b = OVL[0:2, 69632:70144]
    w1W = OVL[:, 0:45056].rearrange("p (a b) -> p a b", b=5632)
    w2W = OVL[:, 45056:67584].rearrange("p (a b) -> p a b", b=1024)

    qb = sb("qb", [128, 8, 512], BF16)
    hbuf = sb("hbuf", [128, 11264], BF16)
    attnb = hbuf[:, 0:4096].rearrange("p (a b) -> p a b", b=512)
    sqv = hbuf[:, 4096:8192].rearrange("p (a b) -> p a b", b=512)
    hview = hbuf[:].rearrange("p (a b) -> p a b", b=512)
    tmpq = sb("tmpq", [128, 512], F32)
    tmpq2 = sb("tmpq2", [128, 512], F32)
    qnb = sb("qnb", [128, 512], BF16)
    rotb = sb("rotb", [128, 512], BF16)
    t1b = sb("t1b", [128, 512], BF16)
    t2b = sb("t2b", [128, 512], BF16)
    sqc = sb("sqc", [128, 512], BF16)
    xinoc = sb("xinoc", [128, 512], F32)
    xmc = sb("xmc", [128, 512], F32)
    den16 = sb("den16", [128, 16], F32)
    rec16 = sb("rec16", [128, 16], F32)
    nmax = sb("nmax", [128, 1], F32)
    Smb2 = tmpq[:, 0:384]
    Pb2 = t1b[:, 0:384]
    st_r0 = sb("st_nmu", [1, 512], F32)
    st_r1 = sb("st_ex2", [1, 512], F32)
    st_r2 = sb("st_musq", [1, 512], F32)
    strow = {"nmu": st_r0, "ex2": st_r1, "musq": st_r2, "var": st_r1, "rst": st_r1}
    T["strow"] = strow
    T["tmpA"], T["tmpB"], T["tmpC"], T["tmpD"] = tmpq, tmpq2, xinoc, xmc
    gacol = sb("gacol", [128, 8], F32)
    becol = sb("becol", [128, 8], F32)
    b2col = sb("b2col", [128, 8], F32)
    b1acol = sb("b1acol", [128, 22], F32)
    b1gcol = sb("b1gcol", [128, 22], F32)
    qscol = sb("qscol", [128, 1], F32)
    kscol = sb("kscol", [128, 1], F32)
    vstage = T["tstage"]
    glub = OVL[:, 68608:69120]

    banks = [T[f"pbank{i}"] for i in range(8)]
    DR["xmid"] = p.nc.dram_tensor("xmid_dram", [D, TAU], F32).ap()
    DR["xn"] = p.nc.dram_tensor("xn_dram", [D, TAU], BF16).ap()

    gz0 = p.step("gpsimd", lambda e: e.memset(xinoc[:, 0:8], 0.0), inc="gp")
    for drn in ("xmid", "xt1"):
        dr = DR[drn]
        p.dma(mkap(dr, [(TAU, 128), (128 * TAU, 8), (1, 1)], dr.offset + 383),
              xinoc[:, 0:8].rearrange("p (a b) -> p a b", b=1), deps=[gz0],
              queue="gpsimd", sem="gq", allow_slow_non_contiguous=True)

    for l in range(depth):
        xin_d = DR["xt0"] if l % 2 == 0 else DR["xt1"]
        xout_d = DR["xt1"] if l % 2 == 0 else DR["xt0"]
        A0 = AST[l]
        E = EXT[l]
        kv0 = A0 - 256
        lvec_toks = []
        for (dst, srcap, n) in ((gacol, IN["ln_g"][l:l + 1, :], 8),
                                (becol, IN["ln_b"][l:l + 1, :], 8),
                                (b2col, IN["b2"][l:l + 1, :], 8)):
            dt = p.dma(vstage[0:n, 0:128], mkap(srcap, [(128, n), (1, 128)], srcap.offset),
                       deps=[p.tok("tp")])
            tm = p.step("tensor", lambda e, dst=dst, n=n: e.transpose(
                banks[7][:, 0:n], vstage[0:n, 0:128], T["ident_f32"][0:n, 0:n]),
                deps=[dt, p.tok("sp")], inc="tp")
            lvec_toks.append(p.step("scalar", lambda e, dst=dst, n=n: e.copy(
                dst[:, 0:n], banks[7][:, 0:n]), deps=[tm], inc="sp"))
        for (dst, off) in ((b1acol, 0), (b1gcol, 2730)):
            gz = p.step("gpsimd", lambda e: e.memset(vstage[:, :], 0.0),
                        deps=[p.tok("tp")], inc="gp")
            dt1 = p.dma(vstage[0:21, 0:128], mkap(IN["b1"], [(128, 21), (1, 128)],
                        IN["b1"].offset + l * 5460 + off), deps=[gz])
            dt2 = p.dma(vstage[21:22, 0:42], mkap(IN["b1"], [(128, 1), (1, 42)],
                        IN["b1"].offset + l * 5460 + off + 21 * 128), deps=[gz])
            tm = p.step("tensor", lambda e: e.transpose(
                banks[7][:, 0:22], vstage[0:22, 0:128], T["ident_f32"][0:22, 0:22]),
                deps=[dt1, dt2, p.tok("sp")], inc="tp")
            p.step("scalar", lambda e, dst=dst: e.copy(dst[:], banks[7][:, 0:22]),
                   deps=[tm], inc="sp")
        prev_fold = None
        for (dst, srcap, fold8) in ((qscol, IN["qs"], True), (kscol, IN["ks"], False)):
            dt = p.dma(vstage[0:1, 0:64], mkap(srcap, [(64, 1), (1, 64)],
                       srcap.offset + l * 64), deps=[p.tok("tp")])
            tm = p.step("tensor", lambda e: e.transpose(
                banks[7][0:64, 0:1], vstage[0:1, 0:64], T["ident_f32"][0:1, 0:1]),
                deps=[dt, p.tok("sp")], inc="tp")
            sv_ = p.step("scalar", lambda e, dst=dst: e.copy(dst[0:64, :], banks[7][0:64, 0:1]),
                         deps=[tm], inc="sp")
            dd = p.dma(dst[64:128, :], dst[0:64, :], deps=[sv_, prev_fold],
                       queue="gpsimd", sem="gq")
            if fold8:
                prev_fold = p.step("vector", lambda e, dst=dst: e.tensor_scalar(
                    out=dst[:], in0=dst[:], scalar1=8.0, scalar2=None, op0=ALU.mult),
                    deps=[dd], inc="vp")
        for dc in range(8):
            p.dma(qkvW[:, dc, :], IN["Wqkv"][l, dc * 128:(dc + 1) * 128, :],
                  queue="gpsimd", sem="gq", deps=[p.tok("tp")])
            p.dma(woW[:, dc, :], IN["Wo"][l, dc * 128:(dc + 1) * 128, :],
                  queue="gpsimd", sem="gq")
        Ekv = E + 256
        emit_tables_layer(p, T, IN, tabsW, kv0, Ekv, banks)
        p.barrier()

        sizes = []
        rem = E
        first = E - (E // 512) * 512
        if first:
            sizes.append(first)
            rem -= first
        sizes += [512] * (rem // 512)
        emit_attn_block(p, T, IN, DR, l, xin_d, kv0, 256, None, True,
                        qkvW, woW, tabsW, y2b, klin, vt, qb, attnb, sqv, kshift,
                        tmpq, tmpq2, qnb, rotb, t1b, t2b, sqc, Smb, Smb2, mbt, Pb, Pb2, Ps, PT,
                        xinoc, xmc, rn2, rn2b, den16, rec16, nmax, strow,
                        gacol, becol, qscol, kscol, banks, kv0, -1)
        p.barrier()
        t0 = A0
        prev_s = 256
        vbase = 2
        for bi, s_ in enumerate(sizes):
            emit_attn_block(p, T, IN, DR, l, xin_d, t0, s_, prev_s, False,
                            qkvW, woW, tabsW, y2b, klin, vt, qb, attnb, sqv, kshift,
                            tmpq, tmpq2, qnb, rotb, t1b, t2b, sqc, Smb, Smb2, mbt, Pb, Pb2, Ps, PT,
                            xinoc, xmc, rn2, rn2b, den16, rec16, nmax, strow,
                            gacol, becol, qscol, kscol, banks, kv0, vbase)
            p.barrier()
            t0 += s_
            prev_s = s_
            vbase += s_ // 128
        for dc in range(8):
            p.dma(w1W[:, dc, 0:2730], IN["W1"][l, dc * 128:(dc + 1) * 128, 0:2730],
                  queue="gpsimd", sem="gq", deps=[p.tok("tp"), p.tok("vp"), p.tok("sp")])
            p.dma(w1W[:, dc, 2816:5546], IN["W1"][l, dc * 128:(dc + 1) * 128, 2730:5460],
                  queue="gpsimd", sem="gq")
            g1 = p.step("gpsimd", lambda e, dc=dc: e.memset(w1W[:, dc, 2730:2816], 0.0), inc="gp")
            g1 = p.step("gpsimd", lambda e, dc=dc: e.memset(w1W[:, dc, 5546:5632], 0.0), inc="gp")
        for hc in range(22):
            if hc < 21:
                p.dma(w2W[:, hc, :], IN["W2"][l, hc * 128:(hc + 1) * 128, :],
                      queue="gpsimd", sem="gq")
            else:
                g2 = p.step("gpsimd", lambda e, hc=hc: e.memset(
                    w2W[:, hc, :].rearrange("p a -> p a"), 0.0), inc="gp")
                p.dma(w2W[0:42, hc, :], IN["W2"][l, 21 * 128:2730, :],
                      queue="gpsimd", sem="gq", deps=[g2])
        p.barrier()
        t0 = A0
        for bi, s_ in enumerate(sizes):
            emit_b2_block(p, T, DR, l, xout_d, t0, s_, w1W, w2W, hview, qb, glub,
                          xinoc, xmc, b1acol, b1gcol, b2col, banks)
            p.barrier()
            t0 += s_


def emit_tables_layer(p, T, IN, tabsW, kv0, Ekv, banks):
    npcs = (Ekv + 511) // 512
    ta, tb = T["tmpA"], T["tmpB"]
    tc_, td = T["tmpC"], T["tmpD"]
    TMP = {"tmpA": ta, "tmpB": tb, "tmpC": tc_, "tmpD": td}
    prev = [p.tok("tp")]
    for i in range(npcs):
        c0 = i * 512
        n = min(512, Ekv - c0)
        tau0 = kv0 + c0
        pb = banks[6]
        pc = banks[7]
        d = p.dma(T["rowst"][0:1, 0:n], IN["pos"][0:1, tau0:tau0 + n], deps=prev)
        m1 = p.step("tensor", lambda e, n=n, pb=pb: e.matmul(
            pb[:, 0:n], T["if_row"][:], T["rowst"][0:1, 0:n],
            start=True, stop=True), deps=[d, p.tok("sp"), p.tok("vp")], inc="tp")
        m2 = p.step("tensor", lambda e, n=n, pc=pc: e.matmul(
            pc[:, 0:n], T["lb128_row"][:], T["rowst"][0:1, 0:n],
            start=True, stop=True), deps=[d], inc="tp")
        prev = [m1, m2]
        v1 = p.step("vector", lambda e, n=n, pb=pb: e.tensor_scalar(
            out=ta[:, 0:n], in0=pb[:, 0:n], scalar1=1.0 / (2 * np.pi),
            scalar2=12582912.0, op0=ALU.mult, op1=ALU.add), deps=[m1], inc="vp")
        v2 = p.step("vector", lambda e, n=n: e.tensor_scalar(
            out=ta[:, 0:n], in0=ta[:, 0:n], scalar1=-12582912.0,
            scalar2=None, op0=ALU.add), deps=[v1], inc="vp")
        v3 = p.step("vector", lambda e, n=n, pb=pb: e.scalar_tensor_tensor(
            out=tb[:, 0:n], in0=ta[:, 0:n], scalar=-2 * np.pi,
            in1=pb[:, 0:n], op0=ALU.mult, op1=ALU.add), deps=[v2], inc="vp")
        v3c = p.step("vector", lambda e, n=n: e.tensor_scalar(
            out=tb[:, 0:n], in0=tb[:, 0:n], scalar1=3.141592, scalar2=-3.141592,
            op0=ALU.min, op1=ALU.max), deps=[v3], inc="vp")
        s1 = p.step("scalar", lambda e, n=n: e.activation(
            out=tc_[:, 0:n], in_=tb[:, 0:n], func=AF.Sin), deps=[v3c], inc="sp")
        v4 = p.step("vector", lambda e, n=n, pb=pb: e.tensor_scalar(
            out=ta[:, 0:n], in0=pb[:, 0:n], scalar1=1.0 / (2 * np.pi),
            scalar2=0.25, op0=ALU.mult, op1=ALU.add), deps=[s1], inc="vp")
        v4b = p.step("vector", lambda e, n=n: e.tensor_scalar(
            out=ta[:, 0:n], in0=ta[:, 0:n], scalar1=12582912.0,
            scalar2=None, op0=ALU.add), deps=[v4], inc="vp")
        v5 = p.step("vector", lambda e, n=n: e.tensor_scalar(
            out=ta[:, 0:n], in0=ta[:, 0:n], scalar1=-12582912.0,
            scalar2=None, op0=ALU.add), deps=[v4b], inc="vp")
        v6 = p.step("vector", lambda e, n=n, pb=pb: e.scalar_tensor_tensor(
            out=tb[:, 0:n], in0=ta[:, 0:n], scalar=-2 * np.pi,
            in1=pb[:, 0:n], op0=ALU.mult, op1=ALU.add), deps=[v5], inc="vp")
        v6c = p.step("vector", lambda e, n=n: e.tensor_scalar(
            out=tb[:, 0:n], in0=tb[:, 0:n], scalar1=-3.141592 - np.pi / 2,
            scalar2=3.141592 - np.pi / 2, op0=ALU.max, op1=ALU.min),
            deps=[v6], inc="vp")
        s2 = p.step("scalar", lambda e, n=n: e.activation(
            out=td[:, 0:n], in_=tb[:, 0:n], func=AF.Sin,
            bias=T["pihalf"][:]), deps=[v6c], inc="sp")
        s3 = p.step("scalar", lambda e, n=n, pc=pc: e.activation(
            out=ta[:, 0:n], in_=pc[:, 0:n], func=AF.Exp,
            bias=T["lbm16col"][:]), deps=[m2, s2], inc="sp")
        s4 = p.step("scalar", lambda e, n=n, pc=pc: e.activation(
            out=tb[:, 0:n], in_=pc[:, 0:n], func=AF.Exp, scale=-1.0,
            bias=T["lbp16col"][:]), deps=[s3], inc="sp")
        for (ti, srcT, scT) in ((0, "tmpD", "tmpA"), (1, "tmpC", "tmpA"),
                                (2, "tmpD", "tmpB"), (3, "tmpC", "tmpB")):
            prev.append(p.step("vector", lambda e, ti=ti, srcT=srcT, scT=scT, n=n,
                        c0=c0: e.tensor_tensor(
                out=tabsW[:, ti, c0:c0 + n], in0=TMP[srcT][:, 0:n], in1=TMP[scT][:, 0:n],
                op=ALU.mult), deps=[s2, s4], inc="vp"))

def emit_attn_block(p, T, IN, DR, l, xin_d, t0, s, prev_s, is_prologue,
                    qkvW, woW, tabsW, y2b, klin, vt, qb, attnb, sqv, kshift,
                    tmpq, tmpq2, qnb, rotb, t1b, t2b, sqc, Smb, Smb2, mbt, Pb, Pb2, Ps, PT,
                    xinoc, xmc, rn2, rn2b, den16, rec16, nmax, strow,
                    gacol, becol, qscol, kscol, banks, kv0, vbase):
    bk = {}

    def bank_dep(b):
        return bk.get(b)

    if not is_prologue:
        if prev_s >= 256:
            sh = p.step("vector", lambda e: e.tensor_copy(
                klin[:, :, 0:256], klin[:, :, prev_s:prev_s + 256]), inc="vp")
        else:
            sh1 = p.step("vector", lambda e: e.tensor_copy(
                kshift[:, :, :], klin[:, :, prev_s:prev_s + 256]), inc="vp")
            sh = p.step("vector", lambda e: e.tensor_copy(
                klin[:, :, 0:256], kshift[:, :, :]), deps=[sh1], inc="vp")
    else:
        sh = None

    d1 = p.dma(y2b[:, 0:4, 0:s], mkap(xin_d, [(TAU, 128), (128 * TAU, 4), (1, s)],
               xin_d.offset + t0), queue="gpsimd", sem="gq")
    d2 = p.dma(y2b[:, 4:8, 0:s], mkap(xin_d, [(TAU, 128), (128 * TAU, 4), (1, s)],
               xin_d.offset + 512 * TAU + t0 - 1), queue="gpsimd", sem="gq")
    yv = p.step("vector", lambda e: e.tensor_tensor(
        out=y2b[:, 4:8, 0:s], in0=y2b[:, 4:8, 0:s],
        in1=T["svb"][:, t0:t0 + s].unsqueeze(1).broadcast_to((128, 4, s)),
        op=ALU.mult), deps=[d1, d2], inc="vp")

    tb0 = t0 - kv0

    for oc in range(16):
        b = banks[oc % 2]
        mms = []
        for dc in range(8):
            mms.append(p.step("tensor", lambda e, oc=oc, dc=dc, b=b: e.matmul(
                b[:, 0:s], qkvW[:, dc, 128 * oc:128 * (oc + 1)], y2b[:, dc, 0:s],
                start=(dc == 0), stop=(dc == 7)),
                deps=[yv, bank_dep(oc % 2)], inc="tp"))
        e1 = p.step("scalar", lambda e, b=b: e.copy(tmpq[:, 0:s], b[:, 0:s]),
                    deps=mms + [bank_dep("tmpq")], inc="sp")
        e2 = p.step("scalar", lambda e, b=b: e.activation(out=sqc[:, 0:s], in_=b[:, 0:s],
                    func=AF.Square), deps=mms + [bank_dep("sqc")], inc="sp")
        bk[oc % 2] = e2
        nm = p.step("tensor", lambda e: e.matmul(banks[3][0:2, 0:s], T["blkones2"][:],
                    sqc[:, 0:s], start=True, stop=True),
                    deps=[e2, bank_dep(3)], inc="tp")
        bk["sqc"] = nm
        v1 = p.step("vector", lambda e: e.tensor_scalar(out=rn2[:, 0:s],
                    in0=banks[3][0:2, 0:s], scalar1=1e-24, scalar2=None, op0=ALU.max),
                    deps=[nm, bank_dep("rn")], inc="vp")
        bk[3] = v1
        s1 = p.step("scalar", lambda e: e.activation(out=rn2[:, 0:s], in_=rn2[:, 0:s],
                    func=AF.Ln), deps=[v1], inc="sp")
        s2 = p.step("scalar", lambda e: e.activation(out=rn2b[:, 0:s], in_=rn2[:, 0:s],
                    func=AF.Exp, scale=-0.5), deps=[s1, bank_dep("rb")], inc="sp")
        bk["rn"] = s2
        rb = p.step("tensor", lambda e: e.matmul(banks[4][:, 0:s], T["bc2"][:],
                    rn2b[:, 0:s], start=True, stop=True), deps=[s2, bank_dep(4)], inc="tp")
        bk["rb"] = rb
        qcol = qscol if oc < 8 else kscol
        ap = p.step("vector", lambda e, qcol=qcol: e.scalar_tensor_tensor(
            out=qnb[:, 0:s], in0=tmpq[:, 0:s], scalar=qcol[:], in1=banks[4][:, 0:s],
            op0=ALU.mult, op1=ALU.mult), deps=[rb, e1, bank_dep("qnb")], inc="vp")
        bk[4] = ap
        bk["tmpq"] = ap
        rm = p.step("tensor", lambda e: e.matmul(banks[5][:, 0:s], T["Mrot"][:],
                    qnb[:, 0:s], start=True, stop=True), deps=[ap, bank_dep(5)], inc="tp")
        re = p.step("scalar", lambda e: e.copy(rotb[:, 0:s], banks[5][:, 0:s]),
                    deps=[rm, bank_dep("rotb")], inc="sp")
        bk[5] = re
        ti_c, ti_s = (0, 1) if oc < 8 else (2, 3)
        c1 = p.step("vector", lambda e, ti_c=ti_c: e.tensor_tensor(
            out=t1b[:, 0:s], in0=qnb[:, 0:s], in1=tabsW[:, ti_c, tb0:tb0 + s],
            op=ALU.mult), deps=[ap, bank_dep("t12")], inc="vp")
        c2 = p.step("vector", lambda e, ti_s=ti_s: e.tensor_tensor(
            out=t2b[:, 0:s], in0=rotb[:, 0:s], in1=tabsW[:, ti_s, tb0:tb0 + s],
            op=ALU.mult), deps=[re], inc="vp")
        bk["qnb"] = c1
        bk["rotb"] = c2
        if oc < 8:
            dstap = qb[:, oc, 0:s]
        else:
            dstap = klin[:, oc - 8, 256:256 + s]
        dd = [sh] if sh is not None else []
        fa = p.step("vector", lambda e, dstap=dstap: e.tensor_tensor(
            out=dstap, in0=t1b[:, 0:s], in1=t2b[:, 0:s], op=ALU.add),
            deps=[c1, c2] + dd, inc="vp")
        bk["t12"] = fa

    ntt = s // 128
    for tt in range(ntt):
        vidx = (vbase + tt) % 6 if vbase >= 0 else tt
        for vh in range(2):
            mms = []
            for dc in range(8):
                mms.append(p.step("tensor", lambda e, tt=tt, vh=vh, dc=dc: e.matmul(
                    banks[2][:, 0:512], y2b[:, dc, 128 * tt:128 * (tt + 1)],
                    qkvW[:, dc, 2048 + 512 * vh:2048 + 512 * (vh + 1)],
                    start=(dc == 0), stop=(dc == 7)),
                    deps=[yv, bank_dep(2)], inc="tp"))
            bk[2] = p.step("scalar", lambda e, vidx=vidx, vh=vh: e.copy(
                vt[vidx][:, 512 * vh:512 * (vh + 1)], banks[2][:, 0:512]),
                deps=mms, inc="sp")

    if is_prologue:
        return

    qk_tok = p.tok("vp")
    for t in range(ntt):
        tau_tile = (t0 + 128 * t) // 128
        m1 = p.step("vector", lambda e, tau_tile=tau_tile: e.tensor_scalar(
            out=Smb[:], in0=T["jiota"][:], scalar1=T["cthr_sb"][:, tau_tile:tau_tile + 1],
            scalar2=None, op0=ALU.is_ge), deps=[qk_tok, bank_dep("Smb")], inc="vp")
        m2 = p.step("vector", lambda e: e.tensor_tensor(out=Smb[:], in0=Smb[:],
                    in1=T["band01"][:], op=ALU.mult), deps=[m1], inc="vp")
        m3 = p.step("vector", lambda e: e.tensor_scalar(out=mbt[:], in0=Smb[:],
                    scalar1=-1.0, scalar2=3.0e38, op0=ALU.add, op1=ALU.mult),
                    deps=[m2], inc="vp")
        for hp in range(8):
            h0, h1 = 2 * hp, 2 * hp + 1
            ab_ = banks[4 + hp // 4]
            ccol = (hp % 4) * 128
            smA = p.step("tensor", lambda e, hp=hp, t=t: e.matmul(
                banks[2][:, 0:384], qb[0:64, hp, 128 * t:128 * (t + 1)],
                klin[0:64, hp, 128 * t:128 * t + 384],
                start=True, stop=True), deps=[m3, bank_dep(2), qk_tok], inc="tp")
            smB = p.step("tensor", lambda e, hp=hp, t=t: e.matmul(
                banks[3][:, 0:384], qb[64:128, hp, 128 * t:128 * (t + 1)],
                klin[64:128, hp, 128 * t:128 * t + 384],
                start=True, stop=True), deps=[bank_dep(3)], inc="tp")
            maA = p.step("vector", lambda e: e.tensor_tensor(out=Smb[:],
                         in0=banks[2][:, 0:384], in1=mbt[:], op=ALU.add),
                         deps=[smA, bank_dep("Smb")], inc="vp")
            bk[2] = maA
            maB = p.step("vector", lambda e: e.tensor_tensor(out=Smb2[:],
                         in0=banks[3][:, 0:384], in1=mbt[:], op=ALU.add),
                         deps=[smB, bank_dep("Smb2")], inc="vp")
            bk[3] = maB
            exA = p.step("scalar", lambda e, h0=h0: e.activation(out=Pb[:], in_=Smb[:],
                         func=AF.Exp, accum_out=den16[:, h0:h0 + 1]),
                         deps=[maA, bank_dep("Pb"), bank_dep(("den", hp))], inc="sp")
            bk["Smb"] = exA
            exB = p.step("scalar", lambda e, h1=h1: e.activation(out=Pb2[:], in_=Smb2[:],
                         func=AF.Exp, accum_out=den16[:, h1:h1 + 1]),
                         deps=[maB, bank_dep("Pb2")], inc="sp")
            bk["Smb2"] = exB
            l1 = p.step("scalar", lambda e, h0=h0: e.activation(out=rec16[:, h0:h0 + 2],
                        in_=den16[:, h0:h0 + 2], func=AF.Ln, bias=T["deneps"][:]),
                        deps=[exA, exB, bank_dep(("rec", hp))], inc="sp")
            l2 = p.step("scalar", lambda e, h0=h0: e.activation(out=rec16[:, h0:h0 + 2],
                        in_=rec16[:, h0:h0 + 2], func=AF.Exp, scale=-1.0),
                        deps=[l1], inc="sp")
            bk[("den", hp)] = l1
            tsA = p.step("vector", lambda e, h0=h0: e.tensor_scalar(out=Ps[:], in0=Pb[:],
                         scalar1=rec16[:, h0:h0 + 1], scalar2=None, op0=ALU.mult),
                         deps=[l2, bank_dep("Ps")], inc="vp")
            bk["Pb"] = tsA
            ptA = []
            for kc in range(3):
                ptA.append(p.step("tensor", lambda e, kc=kc: e.transpose(
                    banks[7][:].bitcast(BF16)[:, 128 * kc:128 * (kc + 1)],
                    Ps[:, 128 * kc:128 * (kc + 1)], T["ident_bf"][:]),
                    deps=[tsA, bank_dep(7)], inc="tp"))
            peA = p.step("vector", lambda e: e.tensor_copy(PT[:],
                         banks[7][:].bitcast(BF16)[:, 0:384]),
                         deps=ptA + [bank_dep("PT")], inc="vp")
            bk[7] = peA
            avA = []
            for kc in range(3):
                vidx = (vbase + t - 2 + kc) % 6 if vbase >= 0 else t - 2 + kc
                avA.append(p.step("tensor", lambda e, h0=h0, kc=kc, vidx=vidx, ab_=ab_,
                           ccol=ccol: e.matmul(
                    ab_[0:64, ccol:ccol + 128], vt[vidx][:, 64 * h0:64 * (h0 + 1)],
                    PT[:, 128 * kc:128 * (kc + 1)],
                    start=(kc == 0), stop=(kc == 2)),
                    deps=[peA, bank_dep(4 + hp // 4)], inc="tp"))
            tsB = p.step("vector", lambda e, h1=h1: e.tensor_scalar(out=Ps[:], in0=Pb2[:],
                         scalar1=rec16[:, h1:h1 + 1], scalar2=None, op0=ALU.mult),
                         deps=[l2, ptA[-1]], inc="vp")
            bk["Pb2"] = tsB
            bk[("rec", hp)] = tsB
            ptB = []
            for kc in range(3):
                ptB.append(p.step("tensor", lambda e, kc=kc: e.transpose(
                    banks[7][:].bitcast(BF16)[:, 128 * kc:128 * (kc + 1)],
                    Ps[:, 128 * kc:128 * (kc + 1)], T["ident_bf"][:]),
                    deps=[tsB, peA], inc="tp"))
            bk["Ps"] = ptB[-1]
            peB = p.step("vector", lambda e: e.tensor_copy(PT[:],
                         banks[7][:].bitcast(BF16)[:, 0:384]),
                         deps=ptB + avA, inc="vp")
            bk[7] = peB
            for kc in range(3):
                vidx = (vbase + t - 2 + kc) % 6 if vbase >= 0 else t - 2 + kc
                p.step("tensor", lambda e, h1=h1, kc=kc, vidx=vidx, ab_=ab_,
                       ccol=ccol: e.matmul(
                    ab_[64:128, ccol:ccol + 128], vt[vidx][:, 64 * h1:64 * (h1 + 1)],
                    PT[:, 128 * kc:128 * (kc + 1)],
                    start=(kc == 0), stop=(kc == 2)),
                    deps=[peB], inc="tp")
            bk[4 + hp // 4] = p.tok("tp")
            bk["PT"] = p.tok("tp")
        e4 = p.step("scalar", lambda e, t=t: e.copy(
            attnb[:, 0:4, 128 * t:128 * (t + 1)],
            banks[4][:, 0:512].rearrange("p (a b) -> p a b", b=128)), deps=[p.tok("tp")], inc="sp")
        e5 = p.step("scalar", lambda e, t=t: e.copy(
            attnb[:, 4:8, 128 * t:128 * (t + 1)],
            banks[5][:, 0:512].rearrange("p (a b) -> p a b", b=128)), deps=[p.tok("tp")], inc="sp")
        bk[4] = e4
        bk[5] = e5

    for oc in range(8):
        b = banks[oc % 2]
        mms = []
        for dc in range(8):
            mms.append(p.step("tensor", lambda e, oc=oc, dc=dc, b=b: e.matmul(
                b[:, 0:s], woW[:, dc, 128 * oc:128 * (oc + 1)], attnb[:, dc, 0:s],
                start=(dc == 0), stop=(dc == 7)),
                deps=[bk[4], bk[5], bank_dep(oc % 2)], inc="tp"))
        di = p.dma(xinoc[:, 0:s], xin_d[128 * oc:128 * (oc + 1), t0:t0 + s],
                   deps=[bank_dep("xin")])
        rr = p.step("vector", lambda e, b=b: e.tensor_tensor(out=xmc[:, 0:s],
                    in0=b[:, 0:s], in1=xinoc[:, 0:s], op=ALU.add),
                    deps=mms + [di, bank_dep("xmc")], inc="vp")
        bk[oc % 2] = rr
        bk["xin"] = rr
        bk["xmc"] = p.dma(DR["xmid"][128 * oc:128 * (oc + 1), t0:t0 + s], xmc[:, 0:s],
                          deps=[rr], queue="gpsimd", sem="gq")

    d1 = p.dma(attnb[:, 0:4, 0:s], mkap(DR["xmid"], [(TAU, 128), (128 * TAU, 4), (1, s)],
               DR["xmid"].offset + t0), queue="gpsimd", sem="gq", deps=[bk["xmc"]])
    d2 = p.dma(attnb[:, 4:8, 0:s], mkap(DR["xmid"], [(TAU, 128), (128 * TAU, 4), (1, s)],
               DR["xmid"].offset + 512 * TAU + t0 - 1), queue="gpsimd", sem="gq")
    yv2 = p.step("vector", lambda e: e.tensor_tensor(
        out=attnb[:, 4:8, 0:s], in0=attnb[:, 4:8, 0:s],
        in1=T["svb"][:, t0:t0 + s].unsqueeze(1).broadcast_to((128, 4, s)),
        op=ALU.mult), deps=[d1, d2], inc="vp")
    ymu, yrs = ln_row_stats(p, T, attnb[:, :, 0:s], sqv[:, :, 0:s], "strow", s, [yv2])
    bn = p.step("tensor", lambda e: e.matmul(banks[6][:, 0:s], T["ones_row"][:],
                strow["nmu"][0:1, 0:s], start=True, stop=True),
                deps=[ymu, yrs, bank_dep(6)], inc="tp")
    br = p.step("tensor", lambda e: e.matmul(banks[7][:, 0:s], T["ones_row"][:],
                strow["rst"][0:1, 0:s], start=True, stop=True),
                deps=[yrs, bank_dep(7)], inc="tp")
    w1 = p.step("vector", lambda e, s=s: e.tensor_tensor(out=sqv[:, :, 0:s],
                in0=attnb[:, :, 0:s],
                in1=banks[6][:, 0:s].unsqueeze(1).broadcast_to((128, 8, s)),
                op=ALU.add), deps=[bn, yrs, yv2, bank_dep("tq")], inc="vp")
    w2 = p.step("vector", lambda e, s=s: e.tensor_tensor(out=sqv[:, :, 0:s],
                in0=sqv[:, :, 0:s],
                in1=banks[7][:, 0:s].unsqueeze(1).broadcast_to((128, 8, s)),
                op=ALU.mult), deps=[w1, br], inc="vp")
    for dc in range(8):
        w3 = p.step("vector", lambda e, dc=dc, s=s: e.tensor_scalar(out=qb[:, dc, 0:s],
                    in0=sqv[:, dc, 0:s], scalar1=gacol[:, dc:dc + 1],
                    scalar2=becol[:, dc:dc + 1], op0=ALU.mult, op1=ALU.add),
                    deps=[w2], inc="vp")
        bk["tq"] = w3
    p.dma(mkap(DR["xn"], [(TAU, 128), (128 * TAU, 8), (1, s)], DR["xn"].offset + t0),
          qb[:, :, 0:s], deps=[bk["tq"]], queue="gpsimd", sem="gq")
    bk[6] = bk["tq"]
    bk[7] = bk["tq"]


def emit_b2_block(p, T, DR, l, xout_d, t0, s, w1W, w2W, hview, qb, glub,
                  xinoc, xmc, b1acol, b1gcol, b2col, banks):
    bk = {}
    dn = p.dma(qb[:, :, 0:s], mkap(DR["xn"], [(TAU, 128), (128 * TAU, 8), (1, s)],
               DR["xn"].offset + t0))
    for ocp in range(22):
        ba, bg = banks[0], banks[1]
        mms = []
        for dc in range(8):
            mms.append(p.step("tensor", lambda e, ocp=ocp, dc=dc: e.matmul(
                ba[:, 0:s], w1W[:, dc, 128 * ocp:128 * (ocp + 1)], qb[:, dc, 0:s],
                start=(dc == 0), stop=(dc == 7)), deps=[dn, bk.get(0)], inc="tp"))
        mmsg = []
        for dc in range(8):
            mmsg.append(p.step("tensor", lambda e, ocp=ocp, dc=dc: e.matmul(
                bg[:, 0:s], w1W[:, dc, 2816 + 128 * ocp:2816 + 128 * (ocp + 1)],
                qb[:, dc, 0:s], start=(dc == 0), stop=(dc == 7)),
                deps=[dn, bk.get(1)], inc="tp"))
        ge = p.step("scalar", lambda e, ocp=ocp: e.activation(out=glub[:, 0:s],
                    in_=bg[:, 0:s], func=AF.Gelu, bias=b1gcol[:, ocp:ocp + 1]),
                    deps=mmsg + [bk.get("glub")], inc="sp")
        bk[1] = ge
        hh = p.step("vector", lambda e, ocp=ocp: e.scalar_tensor_tensor(
            out=hview[:, ocp, 0:s], in0=ba[:, 0:s], scalar=b1acol[:, ocp:ocp + 1],
            in1=glub[:, 0:s], op0=ALU.add, op1=ALU.mult), deps=mms + [ge], inc="vp")
        bk[0] = hh
        bk["glub"] = hh
        bk["h"] = hh
    for oc in range(8):
        b = banks[2 + oc % 2]
        mms = []
        for hc in range(22):
            mms.append(p.step("tensor", lambda e, oc=oc, hc=hc, b=b: e.matmul(
                b[:, 0:s], w2W[:, hc, 128 * oc:128 * (oc + 1)], hview[:, hc, 0:s],
                start=(hc == 0), stop=(hc == 21)),
                deps=[bk["h"], bk.get(2 + oc % 2)], inc="tp"))
        di = p.dma(xinoc[:, 0:s], DR["xmid"][128 * oc:128 * (oc + 1), t0:t0 + s],
                   deps=[bk.get("xi"), bk["h"]])
        rr = p.step("vector", lambda e, oc=oc, b=b: e.scalar_tensor_tensor(
            out=xmc[:, 0:s], in0=b[:, 0:s], scalar=b2col[:, oc:oc + 1],
            in1=xinoc[:, 0:s], op0=ALU.add, op1=ALU.add),
            deps=mms + [di, bk.get("xo")], inc="vp")
        bk[2 + oc % 2] = rr
        bk["xi"] = rr
        bk["xo"] = p.dma(xout_d[128 * oc:128 * (oc + 1), t0:t0 + s], xmc[:, 0:s],
                         deps=[rr], queue="gpsimd", sem="gq")


def phase_final(p, T, DR, OUT, depth, dbg):
    stream = DR["xt0"] if depth % 2 == 0 else DR["xt1"]
    sb = p.sbuf
    ffin = sb("ffin", [128, 8, 128], F32)
    fbf = sb("fbf", [128, 8, 128], BF16)
    fsq = sb("fsq", [128, 8, 128], BF16)
    fx1 = sb("fx1", [128, 8, 128], F32)
    outr = sb("outr", [128, 1024], F32)
    frow = {"nmu": sb("f_nmu", [1, 128], F32), "ex2": sb("f_ex2", [1, 128], F32),
            "musq": sb("f_musq", [1, 128], F32), "var": sb("f_var", [1, 128], F32),
            "rst": sb("f_rst", [1, 128], F32)}
    T["frow"] = frow
    T["tmpA"] = sb("f_tmpA", [128, 128], F32)
    T["tmpB"] = sb("f_tmpB", [128, 128], F32)
    banks = [T[f"pbank{i}"] for i in range(8)]
    for j in range(16):
        tau = 1792 + 128 * j
        d1 = p.dma(ffin[:, :, :], mkap(stream, [(TAU, 128), (128 * TAU, 8), (1, 128)],
                   stream.offset + tau))
        d2 = p.dma(fx1[:, :, :], mkap(DR["x1"], [(TAU, 128), (128 * TAU, 8), (1, 128)],
                   DR["x1"].offset + tau))
        cv = p.step("vector", lambda e: e.tensor_copy(fbf[:], ffin[:]), deps=[d1], inc="vp")
        ymu, yrs = ln_row_stats(p, T, fbf[:, :, :], fsq[:, :, :], "frow", 128, [cv])
        bn = p.step("tensor", lambda e: e.matmul(banks[6][:, 0:128], T["ones_row"][:],
                    frow["nmu"][0:1, :], start=True, stop=True), deps=[ymu, yrs], inc="tp")
        br = p.step("tensor", lambda e: e.matmul(banks[7][:, 0:128], T["ones_row"][:],
                    frow["rst"][0:1, :], start=True, stop=True), deps=[yrs], inc="tp")
        evs = []
        prev_tm = None
        for dc in range(8):
            w1 = p.step("vector", lambda e, dc=dc: e.tensor_tensor(out=T["tmpA"][:, 0:128],
                        in0=fbf[:, dc, :], in1=banks[6][:, 0:128], op=ALU.add),
                        deps=[bn, cv, prev_tm], inc="vp")
            w2 = p.step("vector", lambda e: e.tensor_tensor(out=T["tmpB"][:, 0:128],
                        in0=T["tmpA"][:, 0:128], in1=banks[7][:, 0:128], op=ALU.mult),
                        deps=[w1, br], inc="vp")
            w3 = p.step("vector", lambda e, dc=dc: e.tensor_tensor(out=T["tmpA"][:, 0:128],
                        in0=T["tmpB"][:, 0:128], in1=fx1[:, dc, :], op=ALU.add),
                        deps=[w2, d2], inc="vp")
            tm = p.step("tensor", lambda e, dc=dc: e.transpose(banks[dc % 2][:, 0:128],
                        T["tmpA"][:, 0:128], T["ident_f32"][:]),
                        deps=[w3, evs[dc - 2] if dc >= 2 else None], inc="tp")
            prev_tm = tm
            ev = p.step("scalar", lambda e, dc=dc: e.copy(outr[:, 128 * dc:128 * (dc + 1)],
                        banks[dc % 2][:, 0:128]), deps=[tm], inc="sp")
            evs.append(ev)
        p.dma(OUT[128 * j:128 * (j + 1), :], outr[:], deps=evs, queue="gpsimd", sem="gq")
        p.barrier()
    p.step("sync", lambda e: e.wait_ge(p.sem("gq"), p.semc["gq"]))
    p.step("sync", lambda e: e.wait_ge(p.sem("dq"), p.semc["dq"]))


def host_prep(inputs):
    u = np.ascontiguousarray(inputs["L0_attn_u"])
    per_core = []
    for c in range(NCORES):
        b, half = c // 2, c % 2
        if half == 0:
            u_ssm = np.concatenate([np.zeros((2048, D), np.float32), u[b, :2048]], axis=0)
        else:
            u_ssm = np.ascontiguousarray(u[b])
        goff = -1792 + 2048 * half
        pos = (np.arange(TAU, dtype=np.float32) + goff)[None, :]
        sv = ((np.arange(TAU) + goff - 1) >= 0).astype(np.float32)[None, :]
        cthr = np.zeros((NT, 128), np.float32)
        for i in range(NT):
            cthr[i, :] = 256.0 - (i * 128 + goff)
        per_core.append(dict(u_ssm=u_ssm, pos=pos, sv=sv, cthr=cthr))
    return per_core


def make_inmaps(inputs, per_core):
    maps = []
    for c in range(NCORES):
        pc = per_core[c]
        maps.append({
            "u_ssm": pc["u_ssm"],
            "in_a": inputs["L0_attn_a"], "in_b": inputs["L0_attn_b"], "in_c": inputs["L0_attn_c"],
            "Wqkv": inputs["L0_ffn_Wqkv"], "qs": inputs["L0_ffn_q_scale"],
            "ks": inputs["L0_ffn_k_scale"], "Wo": inputs["L0_ffn_Wo"],
            "ln_g": inputs["L0_ffn_ln_g"], "ln_b": inputs["L0_ffn_ln_b"],
            "W1": inputs["L0_ffn_W1"], "b1": inputs["L0_ffn_b1"],
            "W2": inputs["L0_ffn_W2"], "b2": inputs["L0_ffn_b2"],
            "pos": pc["pos"], "sv": pc["sv"], "cthr": pc["cthr"],
        })
    return maps


def kernel(**inputs):
    per_core = host_prep(inputs)
    nc, _, _ = build_program(depth=DEPTH, debug=False, upto="all")
    maps = make_inmaps(inputs, per_core)
    res = run_bass_kernel_spmd(nc, maps, core_ids=list(range(NCORES)))
    out = np.zeros((4, 4096, D), np.float32)
    for c in range(NCORES):
        b, half = c // 2, c % 2
        out[b, half * 2048:(half + 1) * 2048, :] = res.results[c]["out_slice"]
    return out


# version 05 (final)
# speedup vs baseline: 1.1611x; 1.0075x over previous
import numpy as np
from contextlib import ExitStack

import concourse.bass as bass
import concourse.mybir as mybir
import bass_rust
from concourse.bass_utils import run_bass_kernel_spmd

F32 = mybir.dt.float32
F16 = mybir.dt.float16
BF16 = mybir.dt.bfloat16
AF = mybir.ActivationFunctionType
ALU = mybir.AluOpType
AX = mybir.AxisListType

NCORES = 8
TAU = 3840
NT = 30
MC = 32
D = 1024
EXT = [3456, 3200, 2944, 2688, 2432, 2176]
AST = [3840 - e for e in EXT]
DEPTH = 6
GRP = 8
NGRP = 16


def mkap(base_ap, pairs, offset):
    c = base_ap.copy()
    c.ap = bass_rust.VecI64Pair(pairs)
    c.offset = offset
    return c


class PB:
    ENG = ("sync", "tensor", "vector", "scalar", "gpsimd")
    OWN = {"tensor": "tp"}
    DMA_SEMS = ("dq", "gq")

    def __init__(self, nc, es):
        self.nc, self.es = nc, es
        self.ops = {e: [] for e in self.ENG}
        self.semc = {}
        self.sems = {}
        self.waited = {e: {} for e in self.ENG}
        self.sbuf_bytes = 0
        self.ninstr = 0
        self.bank_owner = {}
        self.scopes = []

    def sem(self, name):
        if name not in self.sems:
            self.sems[name] = self.es.enter_context(self.nc.semaphore(name))
            self.semc[name] = 0
        return self.sems[name]

    def tok(self, name):
        return (name, self.semc.get(name, 0))

    def push_scope(self):
        self.scopes.append(ExitStack())

    def pop_scope(self):
        self.scopes.pop().close()

    def sbuf(self, name, shape, dtype):
        st_ = self.scopes[-1] if self.scopes else self.es
        t = st_.enter_context(self.nc.sbuf_tensor(name, list(shape), dtype))
        pp = int(np.prod(shape[1:])) * mybir.dt.size(dtype)
        self.sbuf_bytes += ((pp + 31) // 32) * 32 * 128
        return t

    def psum(self, name, shape, dtype):
        return self.es.enter_context(self.nc.psum_tensor(name, list(shape), dtype))

    def step(self, eng, fn, deps=(), inc=None, amt=1):
        own = self.OWN.get(eng)
        for d in deps:
            if d is None:
                continue
            sn, v = d
            if sn in self.DMA_SEMS:
                v = self.semc.get(sn, 0)
            if v <= 0 or sn == own:
                continue
            w = self.waited[eng].get(sn, 0)
            if v > w:
                s = self.sem(sn)
                self.ops[eng].append(lambda e, s=s, v=v: e.wait_ge(s, v))
                self.waited[eng][sn] = v
                self.ninstr += 1
        self.ninstr += 1
        if inc is not None:
            s = self.sem(inc)
            self.semc[inc] += amt
            v = self.semc[inc]
            self.ops[eng].append(lambda e, fn=fn, s=s, amt=amt: fn(e).then_inc(s, amt))
            return (inc, v)
        self.ops[eng].append(fn)
        return None

    def dma(self, dst, src, deps=(), queue="sync", sem="dq", **kw):
        return self.step(queue, lambda e, dst=dst, src=src, kw=kw: e.dma_start(out=dst, in_=src, **kw),
                         deps=deps, inc=sem, amt=16)

    FENCE = {"tensor": ("tp",), "vector": ("vp",), "scalar": ("sp",),
             "gpsimd": ("gp", "gq"), "sync": ("dq",)}

    def barrier(self, engines=None):
        engines = engines or self.ENG
        covered = set()
        for e in engines:
            for sn in self.FENCE[e]:
                covered.add(sn)
                v = self.semc.get(sn, 0)
                if v > self.waited[e].get(sn, 0):
                    sd = self.sem(sn)
                    self.ops[e].append(lambda eng, sd=sd, v=v: eng.wait_ge(sd, v))
                    self.waited[e][sn] = v
                    self.ninstr += 1
        e0 = engines[0]
        for ds in self.DMA_SEMS:
            if ds in covered:
                continue
            v = self.semc.get(ds, 0)
            if v > self.waited[e0].get(ds, 0):
                sd = self.sem(ds)
                self.ops[e0].append(lambda eng, sd=sd, v=v: eng.wait_ge(sd, v))
                self.waited[e0][ds] = v
                self.ninstr += 1
        s = self.sem("bar")
        tgt = self.semc["bar"] + len(engines)
        self.semc["bar"] = tgt
        for e in engines:
            self.ops[e].append(lambda eng, s=s: eng.nop().then_inc(s, 1))
        for e in engines:
            self.ops[e].append(lambda eng, s=s, v=tgt: eng.wait_ge(s, v))
            self.waited[e]["bar"] = tgt
        self.ninstr += 2 * len(engines)
        return ("bar", tgt)


def build_program(depth=DEPTH, debug=False, upto="all", ngrp=NGRP):
    nc = bass.Bass("TRN2", target_bir_lowering=False)
    es = ExitStack()
    p = PB(nc, es)

    IN = {}
    IN["u_ssm"] = nc.dram_tensor("u_ssm", [4096, D], F32, kind="ExternalInput").ap()
    for nm in ("a", "b", "c"):
        IN[nm] = nc.dram_tensor("in_" + nm, [256, 64], F32, kind="ExternalInput").ap()
    IN["Wqkv"] = nc.dram_tensor("Wqkv", [6, 1024, 3072], F32, kind="ExternalInput").ap()
    IN["qs"] = nc.dram_tensor("qs", [6, 64], F32, kind="ExternalInput").ap()
    IN["ks"] = nc.dram_tensor("ks", [6, 64], F32, kind="ExternalInput").ap()
    IN["Wo"] = nc.dram_tensor("Wo", [6, 1024, 1024], F32, kind="ExternalInput").ap()
    IN["ln_g"] = nc.dram_tensor("ln_g", [6, 1024], F32, kind="ExternalInput").ap()
    IN["ln_b"] = nc.dram_tensor("ln_b", [6, 1024], F32, kind="ExternalInput").ap()
    IN["W1"] = nc.dram_tensor("W1", [6, 1024, 5460], F32, kind="ExternalInput").ap()
    IN["b1"] = nc.dram_tensor("b1", [6, 5460], F32, kind="ExternalInput").ap()
    IN["W2"] = nc.dram_tensor("W2", [6, 2730, 1024], F32, kind="ExternalInput").ap()
    IN["b2"] = nc.dram_tensor("b2", [6, 1024], F32, kind="ExternalInput").ap()
    IN["pos"] = nc.dram_tensor("pos", [1, TAU], F32, kind="ExternalInput").ap()
    IN["sv"] = nc.dram_tensor("sv", [1, TAU], F32, kind="ExternalInput").ap()
    IN["cthr"] = nc.dram_tensor("cthr", [NT, 128], F32, kind="ExternalInput").ap()

    OUT = nc.dram_tensor("out_slice", [2048, D], F32, kind="ExternalOutput").ap()

    dbg_outs = {}

    def dbg(name, shape, dtype=F32):
        if not debug:
            return None
        ap = nc.dram_tensor("dbg_" + name, list(shape), dtype, kind="ExternalOutput").ap()
        dbg_outs[name] = ap
        return ap

    DR = {}
    DR["x1"] = nc.dram_tensor("x1_dram", [D, TAU], F32).ap()
    DR["xt0"] = nc.dram_tensor("xt0_dram", [D, TAU], F32).ap()
    DR["xt1"] = nc.dram_tensor("xt1_dram", [D, TAU], F32).ap()
    DR["ksh"] = nc.dram_tensor("ksh_dram", [256, 384], F16).ap()

    with nc.Block() as block:
        T = {}
        for i in range(8):
            T[f"pbank{i}"] = p.psum(f"pbank{i}", [128, 512], F32)
        phase_static(p, T)
        p.push_scope()
        if upto not in ("static",):
            phase_uln(p, T, IN, dbg)
        if upto not in ("static", "uln"):
            phase_abc(p, T, IN)
            phase_ssm(p, T, DR, dbg, ngrp=ngrp)
        if upto in ("x1", "all", "tf"):
            phase_x1(p, T, IN, DR, dbg)
        p.pop_scope()
        if upto in ("all", "tf"):
            phase_tables(p, T, IN)
            p.push_scope()
            phase_transformer(p, T, IN, DR, dbg, depth)
            p.pop_scope()
            phase_final(p, T, DR, OUT, depth, dbg)
        else:
            phase_out(p, T, DR, OUT, src="x1")
        for ename in PB.ENG:
            lst = p.ops[ename]
            getattr(block, ename)(lambda eng, lst=lst: [fn(eng) for fn in lst])
    es.close()
    return nc, list(dbg_outs.keys()), p


def phase_static(p, T):
    sb = p.sbuf
    for nm, shape, dt in (("iota_a", [128, 128], F32), ("iota_b", [128, 128], F32),
                          ("scr0", [128, 128], F32), ("scr1", [128, 128], F32),
                          ("pcol", [128, 1], F32), ("subdiag", [128, 128], F16),
                          ("superdiag", [128, 128], F16), ("ident_bf", [128, 128], BF16),
                          ("ident_f16", [128, 128], F16), ("ident_f32", [128, 128], F32),
                          ("rev_bf", [128, 128], BF16), ("ones_col", [128, 1], F32),
                          ("ones_row", [1, 128], F32), ("eps_col", [128, 1], F32)):
        T[nm] = sb(nm, shape, dt)

    ia, ib, s0, s1, pc = T["iota_a"], T["iota_b"], T["scr0"], T["scr1"], T["pcol"]
    t0 = p.step("gpsimd", lambda e: e.iota(ia[:], pattern=[[1, 128]], base=1,
                channel_multiplier=-1, allow_small_or_imprecise_dtypes=True), inc="gp")
    t1 = p.step("gpsimd", lambda e: e.iota(ib[:], pattern=[[1, 128]], base=0,
                channel_multiplier=0, allow_small_or_imprecise_dtypes=True), inc="gp")
    t2 = p.step("gpsimd", lambda e: e.iota(pc[:], pattern=[[0, 1]], base=0,
                channel_multiplier=1, allow_small_or_imprecise_dtypes=True), inc="gp")
    p.step("gpsimd", lambda e: e.memset(T["ones_col"][:], 1.0), inc="gp")
    p.step("gpsimd", lambda e: e.memset(T["ones_row"][:], 1.0), inc="gp")
    p.step("gpsimd", lambda e: e.memset(T["eps_col"][:], 1e-5), inc="gp")

    v = p.step("vector", lambda e: e.tensor_scalar(out=s0[:], in0=ib[:], scalar1=64.0,
               scalar2=None, op0=ALU.is_ge), deps=[t1], inc="vp")
    v = p.step("vector", lambda e: e.tensor_scalar(out=pc[:], in0=pc[:], scalar1=64.0,
               scalar2=None, op0=ALU.is_ge), deps=[t2], inc="vp")
    blk = p.step("vector", lambda e: e.tensor_scalar(out=s0[:], in0=s0[:], scalar1=pc[:],
                 scalar2=None, op0=ALU.is_equal), deps=[v], inc="vp")
    v = p.step("vector", lambda e: e.tensor_scalar(out=s1[:], in0=ia[:], scalar1=0.0,
               scalar2=None, op0=ALU.is_equal), deps=[t0], inc="vp")
    v = p.step("vector", lambda e: e.tensor_tensor(out=T["subdiag"][:], in0=s1[:], in1=s0[:],
               op=ALU.mult), deps=[v, blk], inc="vp")
    v = p.step("vector", lambda e: e.tensor_scalar(out=s1[:], in0=ia[:], scalar1=2.0,
               scalar2=None, op0=ALU.is_equal), deps=[v], inc="vp")
    v = p.step("vector", lambda e: e.tensor_tensor(out=T["superdiag"][:], in0=s1[:], in1=s0[:],
               op=ALU.mult), deps=[v], inc="vp")
    for nm in ("ident_bf", "ident_f16", "ident_f32"):
        v = p.step("vector", lambda e, nm=nm: e.tensor_scalar(out=T[nm][:], in0=ia[:],
                   scalar1=1.0, scalar2=None, op0=ALU.is_equal), deps=[t0], inc="vp")
    t5 = p.step("gpsimd", lambda e: e.iota(ib[:], pattern=[[1, 128]], base=-127,
                channel_multiplier=1, allow_small_or_imprecise_dtypes=True),
                deps=[blk], inc="gp")
    p.step("vector", lambda e: e.tensor_scalar(out=T["rev_bf"][:], in0=ib[:], scalar1=0.0,
           scalar2=None, op0=ALU.is_equal), deps=[t5], inc="vp")
    p.barrier()


def phase_abc(p, T, IN):
    sb = p.sbuf
    for nm, shape, dt in (("a_tm", [128, 2, 64], F32), ("b_tm", [128, 2, 64], F32),
                          ("c_tm", [128, 2, 64], F32), ("an16", [128, 2, 64], F16),
                          ("b16", [128, 2, 64], F16), ("c16", [128, 2, 64], F16),
                          ("anorm", [128, 2], F32), ("arecip", [128, 2], F32),
                          ("sel128", [128, 1], F32), ("sm2", [2, 1], F32),
                          ("sm1", [1, 1], F32), ("anT2", [128, 256], F16),
                          ("bT2", [128, 256], F16), ("cT2", [128, 256], F16)):
        T[nm] = sb(nm, shape, dt)

    pgi = [(64, 128), (8192, 2), (1, 64)]
    da = p.dma(T["a_tm"][:], mkap(IN["a"], pgi, 0))
    db = p.dma(T["b_tm"][:], mkap(IN["b"], pgi, 0))
    dc = p.dma(T["c_tm"][:], mkap(IN["c"], pgi, 0))

    v = p.step("vector", lambda e: e.tensor_reduce(out=T["anorm"][:], in_=T["a_tm"][:],
               axis=AX.X, op=ALU.add, apply_absolute_value=True), deps=[da], inc="vp")
    tm = p.step("tensor", lambda e: e.matmul(T["pbank0"][0:2, 0:1], T["anorm"][:],
                T["ones_col"][:], start=True, stop=True), deps=[v], inc="tp")
    v = p.step("vector", lambda e: e.tensor_copy(T["sm2"][:], T["pbank0"][0:2, 0:1]),
               deps=[tm], inc="vp")
    tm = p.step("tensor", lambda e: e.matmul(T["pbank0"][0:1, 4:5], T["sm2"][:],
                T["ones_col"][0:2, :], start=True, stop=True), deps=[v], inc="tp")
    v = p.step("vector", lambda e: e.tensor_scalar(out=T["sm1"][:], in0=T["pbank0"][0:1, 4:5],
               scalar1=1.0 / 256.0, scalar2=1e-4, op0=ALU.mult, op1=ALU.is_gt),
               deps=[tm], inc="vp")
    tm = p.step("tensor", lambda e: e.matmul(T["pbank0"][0:128, 8:9], T["ones_row"][:],
                T["sm1"][:], start=True, stop=True), deps=[v], inc="tp")
    v = p.step("vector", lambda e: e.tensor_copy(T["sel128"][:], T["pbank0"][0:128, 8:9]),
               deps=[tm], inc="vp")
    v1 = p.step("vector", lambda e: e.reciprocal(T["arecip"][:], T["anorm"][:]), deps=[v], inc="vp")
    v1 = p.step("vector", lambda e: e.tensor_scalar(out=T["arecip"][:], in0=T["arecip"][:],
                scalar1=T["sel128"][:], scalar2=None, op0=ALU.mult), deps=[v1], inc="vp")
    v2 = p.step("vector", lambda e: e.tensor_scalar(out=T["sel128"][:], in0=T["sel128"][:],
                scalar1=-1.0, scalar2=1.0, op0=ALU.mult, op1=ALU.add), deps=[v1], inc="vp")
    v = p.step("vector", lambda e: e.tensor_scalar(out=T["arecip"][:], in0=T["arecip"][:],
               scalar1=T["sel128"][:], scalar2=None, op0=ALU.add), deps=[v2], inc="vp")
    van = p.step("vector", lambda e: e.tensor_tensor(
        out=T["an16"][:], in0=T["a_tm"][:],
        in1=T["arecip"][:].unsqueeze(2).broadcast_to((128, 2, 64)), op=ALU.mult),
        deps=[v], inc="vp")
    vb = p.step("vector", lambda e: e.tensor_copy(T["b16"][:], T["b_tm"][:]), deps=[db], inc="vp")
    vc = p.step("vector", lambda e: e.tensor_copy(T["c16"][:], T["c_tm"][:]), deps=[dc], inc="vp")

    ps16 = T["pbank1"][:].bitcast(F16)
    toks = []
    for idx, src in enumerate((T["an16"], T["b16"], T["c16"])):
        for g in range(2):
            col = (idx * 2 + g) * 128
            toks.append(p.step("tensor", lambda e, src=src, g=g, col=col: e.transpose(
                ps16[0:64, col:col + 128], src[:, g, :], T["ident_f16"][:]),
                deps=[van, vb, vc], inc="tp"))
    s1 = p.step("scalar", lambda e: e.copy(T["anT2"][0:64, :], ps16[0:64, 0:256]),
                deps=toks, inc="sp")
    s2 = p.step("scalar", lambda e: e.copy(T["bT2"][0:64, :], ps16[0:64, 256:512]),
                deps=toks, inc="sp")
    s3 = p.step("scalar", lambda e: e.copy(T["cT2"][0:64, :], ps16[0:64, 512:768]),
                deps=toks, inc="sp")
    for nm, s in (("anT2", s1), ("bT2", s2), ("cT2", s3)):
        p.dma(T[nm][64:128, :], T[nm][0:64, :], deps=[s], queue="gpsimd", sem="gq")
    p.barrier()


def phase_uln(p, T, IN, dbg):
    sb = p.sbuf
    T["urev"] = sb("urev", [128, MC, D], BF16)
    ut = [sb(f"ut{i}", [128, D], F32) for i in range(2)]
    usq = sb("usq", [128, D], F32)
    uxc = sb("uxc", [128, D], F32)
    un_bf = [sb(f"un_bf{i}", [128, D], BF16) for i in range(2)]
    T["usq"] = usq
    T["ut0"] = ut[0]
    T["un_bf0"] = un_bf[0]
    st = {nm: sb("ul" + nm, [128, 1], F32) for nm in ("s1", "s2", "negmu", "var", "rstd")}
    d_un = dbg("un", [4096, D])

    prev_rev = [None, None]
    prev_ap = [None, None]
    for m in range(MC):
        par = m % 2
        dtok = p.dma(ut[par][:], IN["u_ssm"][m * 128:(m + 1) * 128, :],
                     deps=[prev_ap[par]])
        ln = emit_ln_stats(p, T, ut[par][:], uxc, usq, st, deps=[dtok])
        ap = p.step("vector", lambda e, par=par: e.tensor_scalar(
            out=un_bf[par][:], in0=uxc[:], scalar1=st["rstd"][:], scalar2=None,
            op0=ALU.mult), deps=[ln, prev_rev[par]], inc="vp")
        prev_ap[par] = ap
        if d_un is not None:
            cast = p.step("scalar", lambda e, par=par: e.copy(usq[:], un_bf[par][:]),
                          deps=[ap], inc="sp")
            p.dma(d_un[m * 128:(m + 1) * 128, :], usq[:], deps=[cast], queue="gpsimd", sem="gq")
        pb = [T["pbank0"], T["pbank1"]] if par == 0 else [T["pbank2"], T["pbank3"]]
        mms = []
        for h in range(2):
            mms.append(p.step("tensor", lambda e, par=par, h=h, pb=pb: e.matmul(
                pb[h][:, 0:512], T["rev_bf"][:], un_bf[par][:, h * 512:(h + 1) * 512],
                start=True, stop=True), deps=[ap, prev_rev[par]], inc="tp"))
        ev = p.step("scalar", lambda e, m=m, pb=pb: e.copy(T["urev"][:, m, 0:512], pb[0][:]),
                    deps=mms, inc="sp")
        ev = p.step("scalar", lambda e, m=m, pb=pb: e.copy(T["urev"][:, m, 512:1024], pb[1][:]),
                    deps=mms, inc="sp")
        prev_rev[par] = p.barrier(("tensor", "vector", "scalar", "sync", "gpsimd"))
        prev_ap[par] = prev_rev[par]
    p.barrier()


def emit_ln_stats(p, T, src_ap, xcbuf, sqbuf, st, deps=()):
    v1 = p.step("vector", lambda e: e.tensor_reduce(out=st["s1"][:], in_=src_ap, axis=AX.X,
                op=ALU.add), deps=deps, inc="vp")
    v2 = p.step("vector", lambda e: e.tensor_scalar(out=st["negmu"][:], in0=st["s1"][:],
                scalar1=-1.0 / D, scalar2=None, op0=ALU.mult), deps=[v1], inc="vp")
    v3 = p.step("vector", lambda e: e.tensor_scalar(out=xcbuf[:], in0=src_ap,
                scalar1=st["negmu"][:], scalar2=None, op0=ALU.add), deps=[v2], inc="vp")
    s4 = p.step("scalar", lambda e: e.activation(out=sqbuf[:], in_=xcbuf[:], func=AF.Square),
                deps=[v3] + list(deps), inc="sp")
    v5 = p.step("vector", lambda e: e.tensor_reduce(out=st["s2"][:], in_=sqbuf[:], axis=AX.X,
                op=ALU.add), deps=[s4], inc="vp")
    s6 = p.step("scalar", lambda e: e.activation(out=st["rstd"][:], in_=st["s2"][:],
                func=AF.Ln, bias=T["eps_col"][:], scale=1.0 / D), deps=[v5], inc="sp")
    s7 = p.step("scalar", lambda e: e.activation(out=st["rstd"][:], in_=st["rstd"][:],
                func=AF.Exp, scale=-0.5), deps=[s6], inc="sp")
    return s7


SSM_STOP = "all"

def phase_ssm(p, T, DR, dbg, ngrp=NGRP):
    sb = p.sbuf
    T["y_all"] = sb("y_all", [128, NT, D], BF16)
    p.push_scope()
    Aa = [sb(f"Aa{q}", [128, 128], F16) for q in range(GRP)]
    Ab = [sb(f"Ab{q}", [128, 128], F16) for q in range(GRP)]
    Ata = [sb(f"Ata{q}", [128, 128], F16) for q in range(GRP)]
    Atb = [sb(f"Atb{q}", [128, 128], F16) for q in range(GRP)]
    U = [sb(f"U{q}", [128, 256], F16) for q in range(GRP)]
    P2 = [sb(f"P2{q}", [128, 128], F16) for q in range(GRP)]
    Pk = [sb(f"Pk{q}", [128, 128], F16) for q in range(GRP)]
    Gt = [[sb(f"Gt{q}_{v}", [128, 128], F16) for v in range(5)] for q in range(GRP)]
    Uy = [[sb(f"Uy{q}_{hh}", [128, 128], F16) for hh in range(2)] for q in range(GRP)]
    H0 = [[sb(f"H0_{q}_{hh}", [128, 128], F16) for hh in range(2)] for q in range(GRP)]
    Xa = [sb(f"Xa{q}", [128, MC, 8], BF16) for q in range(GRP)]
    Xb = [sb(f"Xb{q}", [128, MC, 8], BF16) for q in range(GRP)]
    kstage = sb("kstage", [1, 256], F16)
    zeros16 = sb("zeros16", [128, 384], F16)
    g0 = p.step("gpsimd", lambda e: e.memset(zeros16[:], 0.0), inc="gp")
    p.step("gpsimd", lambda e: e.memset(T["y_all"][:], 0.0), inc="gp")
    zt = p.dma(DR["ksh"][0:128, :], zeros16[:], deps=[g0])
    zt = p.dma(DR["ksh"][128:256, :], zeros16[:], deps=[g0])
    banks = [T[f"pbank{i}"] for i in range(8)]
    d_ksh = dbg("ksh", [256, 384])
    d_y = dbg("yall", [TAU, D])

    for g in range(ngrp):
        pairs = list(range(g * GRP, (g + 1) * GRP))
        itoks = []
        for q, pp in enumerate(pairs):
            h0, h1 = 2 * pp, 2 * pp + 1
            vA = p.step("vector", lambda e, q=q: e.tensor_copy(Aa[q][:], T["subdiag"][:]), inc="vp")
            vAt = p.step("vector", lambda e, q=q: e.tensor_copy(Ata[q][:], T["superdiag"][:]), inc="vp")
            itoks.append(p.step("vector", lambda e, q=q, h0=h0: e.tensor_copy(
                Aa[q][0:64, 63:64], T["anT2"][0:64, h0:h0 + 1]), deps=[vA], inc="vp"))
            itoks.append(p.step("vector", lambda e, q=q, h1=h1: e.tensor_copy(
                Aa[q][64:128, 127:128], T["anT2"][64:128, h1:h1 + 1]), deps=[vA], inc="vp"))
            itoks.append(p.dma(Ata[q][63:64, 0:64],
                               T["an16"][h0 % 128:h0 % 128 + 1, h0 // 128, :],
                               deps=[vAt], queue="gpsimd", sem="gq"))
            itoks.append(p.dma(Ata[q][127:128, 64:128],
                               T["an16"][h1 % 128:h1 % 128 + 1, h1 // 128, :],
                               deps=[vAt], queue="gpsimd", sem="gq"))
            vz = p.step("vector", lambda e, q=q: e.tensor_copy(U[q][:, 0:128], zeros16[:, 0:128]),
                        deps=[g0], inc="vp")
            p.step("vector", lambda e, q=q: e.tensor_copy(U[q][:, 128:256], zeros16[:, 0:128]),
                   deps=[g0], inc="vp")
            itoks.append(p.step("vector", lambda e, q=q, h0=h0: e.tensor_copy(
                U[q][0:64, 0:1], T["cT2"][0:64, h0:h0 + 1]), deps=[vz], inc="vp"))
            itoks.append(p.step("vector", lambda e, q=q, h1=h1: e.tensor_copy(
                U[q][64:128, 0:1], T["cT2"][64:128, h1:h1 + 1]), deps=[vz], inc="vp"))
            vz2 = p.step("vector", lambda e, q=q: e.tensor_copy(P2[q][:], zeros16[:, 0:128]),
                         deps=[g0], inc="vp")
            itoks.append(p.step("vector", lambda e, q=q, h0=h0: e.tensor_copy(
                P2[q][0:64, 0:1], T["bT2"][0:64, h0:h0 + 1]), deps=[vz2], inc="vp"))
            itoks.append(p.step("vector", lambda e, q=q, h1=h1: e.tensor_copy(
                P2[q][64:128, 0:1], T["bT2"][64:128, h1:h1 + 1]), deps=[vz2], inc="vp"))
        init_tok = [p.tok("vp"), p.tok("gq")]
        if SSM_STOP == "init":
            p.barrier(); continue

        Acur, Anxt, Atcur, Atnxt = Aa, Ab, Ata, Atb
        ev_last = None
        nv = int(SSM_STOP[5:]) if SSM_STOP.startswith("chain") and SSM_STOP[5:].isdigit() else (1 if SSM_STOP == "chainA" else (4 if SSM_STOP == "chainB" else 8))
        for v in range(nv):
            w = 1 << v
            mm_toks = []
            for q in range(GRP):
                ps = banks[q]
                dep = [ev_last] + init_tok if ev_last else init_tok
                mm_toks.append(p.step("tensor", lambda e, q=q, w=w, ps=ps, Ac=Acur: e.matmul(
                    ps[:, 0:w], Ac[q][:], U[q][:, 0:w], start=True, stop=True),
                    deps=dep, inc="tp"))
                if v < 7:
                    mm_toks.append(p.step("tensor", lambda e, q=q, w=w, ps=ps, At_=Atcur: e.matmul(
                        ps[:, 128:128 + w], At_[q][:], P2[q][:, 0:w],
                        start=True, stop=True), inc="tp"))
                    mm_toks.append(p.step("tensor", lambda e, q=q, ps=ps, Ac=Acur, At_=Atcur: e.matmul(
                        ps[:, 256:384], At_[q][:], Ac[q][:], start=True, stop=True),
                        inc="tp"))
                    mm_toks.append(p.step("tensor", lambda e, q=q, ps=ps, Ac=Acur, At_=Atcur: e.matmul(
                        ps[:, 384:512], Ac[q][:], At_[q][:], start=True, stop=True),
                        inc="tp"))
            evs = []
            for q in range(GRP):
                ps = banks[q]
                evs.append(p.step("scalar", lambda e, q=q, w=w, ps=ps: e.copy(
                    U[q][:, w:2 * w], ps[:, 0:w]), deps=mm_toks, inc="sp"))
                if v < 7:
                    evs.append(p.step("scalar", lambda e, q=q, w=w, ps=ps: e.copy(
                        P2[q][:, w:2 * w], ps[:, 128:128 + w]), deps=mm_toks, inc="sp"))
                    evs.append(p.step("scalar", lambda e, q=q, ps=ps, An=Anxt: e.copy(
                        An[q][:], ps[:, 256:384]), deps=mm_toks, inc="sp"))
                    evs.append(p.step("scalar", lambda e, q=q, ps=ps, Atn=Atnxt: e.copy(
                        Atn[q][:], ps[:, 384:512]), deps=mm_toks, inc="sp"))
            ev_last = p.barrier()
            if v < 7:
                Acur, Anxt = Anxt, Acur
                Atcur, Atnxt = Atnxt, Atcur
        if SSM_STOP.startswith("chain") and SSM_STOP not in ("chain",):
            p.barrier(); continue
        for q in range(GRP):
            p.step("vector", lambda e, q=q, At_=Atcur: e.tensor_copy(Gt[q][0][:], At_[q][:]),
                   deps=[ev_last], inc="vp")
        for v in range(4):
            mm_toks = []
            for q in range(GRP):
                ps = banks[q]
                mm_toks.append(p.step("tensor", lambda e, q=q, ps=ps, Ac=Acur, At_=Atcur: e.matmul(
                    ps[:, 256:384], At_[q][:], Ac[q][:], start=True, stop=True),
                    deps=[ev_last, p.tok("vp")], inc="tp"))
                mm_toks.append(p.step("tensor", lambda e, q=q, ps=ps, Ac=Acur, At_=Atcur: e.matmul(
                    ps[:, 384:512], Ac[q][:], At_[q][:], start=True, stop=True), inc="tp"))
            for q in range(GRP):
                ps = banks[q]
                p.step("scalar", lambda e, q=q, ps=ps, An=Anxt: e.copy(An[q][:], ps[:, 256:384]),
                       deps=mm_toks, inc="sp")
                p.step("scalar", lambda e, q=q, ps=ps, Atn=Atnxt: e.copy(Atn[q][:], ps[:, 384:512]),
                       deps=mm_toks, inc="sp")
            ev_last = p.barrier()
            Acur, Anxt = Anxt, Acur
            Atcur, Atnxt = Atnxt, Atcur
            for q in range(GRP):
                p.step("vector", lambda e, q=q, v=v, At_=Atcur: e.tensor_copy(Gt[q][v + 1][:], At_[q][:]),
                       deps=[ev_last], inc="vp")
        gt_tok = p.tok("vp")
        if SSM_STOP in ("chain", "gt"):
            p.barrier(); continue

        uy_toks = []
        for q in range(GRP):
            for hh in range(2):
                base = 64 * hh
                g1 = p.step("gpsimd", lambda e, q=q, hh=hh: e.memset(Uy[q][hh][:], 0.0),
                            deps=[ev_last], inc="gp")
                uy_toks.append(p.step("scalar", lambda e, q=q, hh=hh, base=base: e.copy(
                    Uy[q][hh][base:base + 64, :], U[q][base:base + 64, 1:129]),
                    deps=[g1, ev_last, gt_tok], inc="sp"))

        pk_toks = []
        for q in range(GRP):
            ps16 = banks[q][:].bitcast(F16)
            tm = p.step("tensor", lambda e, q=q, ps16=ps16: e.transpose(
                ps16[:, 0:128], P2[q][:], T["ident_f16"][:]), deps=[ev_last, gt_tok], inc="tp")
            pk_toks.append(p.step("scalar", lambda e, q=q, ps16=ps16: e.copy(
                Pk[q][:], ps16[:, 0:128]), deps=[tm], inc="sp"))

        ktoks = []
        prev_kdma = None
        for q, pp in enumerate(pairs):
            for hh in range(2):
                h = 2 * pp + hh
                base = 64 * hh
                ps = banks[q]
                tm = p.step("tensor", lambda e, q=q, h=h, base=base, ps=ps: e.matmul(
                    ps[0:1, 0:256], T["bT2"][base:base + 64, h:h + 1],
                    U[q][base:base + 64, :], start=True, stop=True),
                    deps=[ev_last, gt_tok, pk_toks[-1], prev_kdma], inc="tp")
                ev = p.step("scalar", lambda e, ps=ps: e.copy(kstage[:], ps[0:1, 0:256]),
                            deps=[tm, prev_kdma], inc="sp")
                prev_kdma = p.dma(DR["ksh"][h:h + 1, 127:383], kstage[:], deps=[ev],
                                  queue="gpsimd", sem="gq")
                ktoks.append(prev_kdma)
        if SSM_STOP == "ksh":
            p.barrier(); continue
        htoks = []
        for q, pp in enumerate(pairs):
            for hh in range(2):
                h = 2 * pp + hh
                hsrc = mkap(DR["ksh"], [(1, 128), (1, 128)], h * 384)
                htoks.append(p.dma(H0[q][hh][:], hsrc, deps=[ktoks[-1]]))
        if SSM_STOP == "h0":
            p.barrier(); continue

        urev_flat = T["urev"][:]
        for q, pp in enumerate(pairs):
            psz = banks[q]
            zmm = p.step("tensor", lambda e, q=q, pp=pp, psz=psz: e.matmul(
                psz[:, 0:256], Pk[q][:],
                T["urev"][:, :, 8 * pp:8 * pp + 8],
                start=True, stop=True), deps=[pk_toks[-1], ktoks[-1]], inc="tp")
            ztok = p.step("scalar", lambda e, q=q, psz=psz: e.copy(
                Xa[q][:], psz[:, 0:256]), deps=[zmm], inc="sp")
            if SSM_STOP == "z":
                continue
            Xc, Xn = Xa, Xb
            for v in range(5 if SSM_STOP != "scan1" else 1):
                sh = 1 << v
                ps = banks[q]
                m2 = p.step("tensor", lambda e, q=q, v=v, sh=sh, ps=ps, Xc=Xc: e.matmul(
                    ps[:, 8 * sh:256], Gt[q][v][:],
                    Xc[q][:, 0:MC - sh, :],
                    start=True, stop=True), deps=[ztok, gt_tok], inc="tp")
                vt = p.step("vector", lambda e, q=q, sh=sh, ps=ps, Xc=Xc, Xn=Xn: e.tensor_tensor(
                    out=Xn[q][:, sh:MC, :],
                    in0=ps[:, 8 * sh:256],
                    in1=Xc[q][:, sh:MC, :], op=ALU.add),
                    deps=[m2], inc="vp")
                ztok = p.step("vector", lambda e, q=q, sh=sh, Xc=Xc, Xn=Xn: e.tensor_copy(
                    Xn[q][:, 0:sh, :],
                    Xc[q][:, 0:sh, :]), deps=[vt], inc="vp")
                Xc, Xn = Xn, Xc
            for hh in range(2):
                if SSM_STOP in ("scan", "scan1"):
                    continue
                h = 2 * pp + hh
                base = 64 * hh
                psy = banks[q]
                yc = 256 + 120 * hh
                m1 = p.step("tensor", lambda e, q=q, hh=hh, pp=pp, psy=psy, yc=yc: e.matmul(
                    psy[:, yc:yc + 120], H0[q][hh][:],
                    T["urev"][:, 2:32, 8 * pp + 4 * hh:8 * pp + 4 * hh + 4],
                    start=True, stop=False), deps=[htoks[-1], ztok], inc="tp")
                m2 = p.step("tensor", lambda e, q=q, hh=hh, psy=psy, Xc=Xc, yc=yc: e.matmul(
                    psy[:, yc:yc + 120], Uy[q][hh][:],
                    Xc[q][:, 1:31, 4 * hh:4 * hh + 4],
                    start=False, stop=True), deps=[uy_toks[-1]], inc="tp")
                p.step("scalar", lambda e, h=h, psy=psy, yc=yc: e.copy(
                    T["y_all"][:, :, 4 * h:4 * h + 4],
                    psy[:, yc:yc + 120]), deps=[m1, m2], inc="sp")
        p.barrier()

    if d_ksh is not None:
        ks32 = T["ut0"]
        ks16 = T["un_bf0"][:].bitcast(F16)
        for half in range(2):
            dt = p.dma(ks16[:, 0:384], DR["ksh"][128 * half:128 * (half + 1), :])
            cv = p.step("scalar", lambda e: e.copy(ks32[:, 0:384], ks16[:, 0:384]),
                        deps=[dt], inc="sp")
            p.dma(d_ksh[128 * half:128 * (half + 1), :], ks32[:, 0:384], deps=[cv],
                  queue="gpsimd", sem="gq")
            p.barrier(("scalar", "gpsimd", "sync"))
    if d_y is not None:
        ycast = T["usq"]
        for j in range(NT):
            v = p.step("scalar", lambda e, j=j: e.copy(ycast[:], T["y_all"][:, j, :]),
                       deps=[p.tok("gq")], inc="sp")
            p.dma(d_y[j * 128:(j + 1) * 128, :], ycast[:], deps=[v], queue="gpsimd", sem="gq")
            p.barrier(("scalar", "gpsimd"))
    p.barrier()
    p.pop_scope()


def phase_x1(p, T, IN, DR, dbg):
    sb = p.sbuf
    uxt = [sb(f"uxt{i}", [128, D], F32) for i in range(2)]
    ysq = sb("ysq", [128, D], F32)
    yxc = sb("yxc", [128, D], F32)
    x1t = sb("x1t", [128, D], F32)
    xt0t = sb("xt0t", [128, D], F32)
    xf = [sb(f"xf{i}", [128, 128], F32) for i in range(2)]
    st = {nm: sb("x1" + nm, [128, 1], F32) for nm in ("s1", "s2", "negmu", "var", "rstd")}

    prev = [None, None]
    prev_tr = None
    for j in range(NT):
        par = j % 2
        dtok = p.dma(uxt[par][:], IN["u_ssm"][j * 128 + 256:(j + 1) * 128 + 256, :],
                     deps=[prev[par]])
        ytile = T["y_all"][:, j, :]
        ln = emit_ln_stats(p, T, ytile, yxc, ysq, st, deps=[prev_tr])
        a1 = p.step("vector", lambda e: e.tensor_scalar(
            out=x1t[:], in0=yxc[:], scalar1=st["rstd"][:], scalar2=None,
            op0=ALU.mult), deps=[ln], inc="vp")
        a2 = p.step("vector", lambda e, par=par: e.tensor_tensor(
            out=x1t[:], in0=x1t[:], in1=uxt[par][:], op=ALU.add), deps=[a1, dtok], inc="vp")
        prev[par] = a2
        ln2 = emit_ln_stats(p, T, x1t[:], yxc, ysq, st, deps=[a2])
        a3 = p.step("vector", lambda e: e.tensor_scalar(
            out=xt0t[:], in0=yxc[:], scalar1=st["rstd"][:], scalar2=None,
            op0=ALU.mult), deps=[ln2], inc="vp")
        evs = []
        bank_free = {}
        xf_free = {}
        for (src, dst, tok, fpar) in ((x1t, DR["x1"], a2, 0), (xt0t, DR["xt0"], a3, 1)):
            for cch in range(8):
                bid = 4 + (cch % 2) + 2 * fpar
                ps = T[f"pbank{bid}"]
                tm = p.step("tensor", lambda e, src=src, cch=cch, ps=ps: e.transpose(
                    ps[:, 0:128], src[:, cch * 128:(cch + 1) * 128], T["ident_f32"][:]),
                    deps=[tok, prev_tr, bank_free.get(bid)], inc="tp")
                ev = p.step("scalar", lambda e, ps=ps, fpar=fpar: e.copy(xf[fpar][:], ps[:, 0:128]),
                            deps=[tm, bank_free.get(bid), xf_free.get(fpar)], inc="sp")
                bank_free[bid] = ev
                dt = p.dma(dst[cch * 128:(cch + 1) * 128, j * 128:(j + 1) * 128],
                           xf[fpar][:], deps=[ev], queue="gpsimd", sem="gq")
                xf_free[fpar] = dt
                evs.append(dt)
        prev_tr = p.barrier()
    p.barrier()


def phase_out(p, T, DR, OUT, src="xt0"):
    xin = p.sbuf("out_xin", [128, 128], F32)
    xrow = p.sbuf("out_xrow", [128, D], F32)
    final_tok = None
    for j in range(16):
        evs = []
        prev_dma = None
        bank_free = {}
        for cch in range(8):
            bid = cch % 4
            ps = T[f"pbank{bid}"]
            dt = p.dma(xin[:], DR[src][cch * 128:(cch + 1) * 128,
                       1792 + j * 128:1792 + (j + 1) * 128], deps=[prev_dma])
            tm = p.step("tensor", lambda e, ps=ps: e.transpose(
                ps[:, 0:128], xin[:], T["ident_f32"][:]),
                deps=[dt, bank_free.get(bid)], inc="tp")
            ev = p.step("scalar", lambda e, cch=cch, ps=ps: e.copy(
                xrow[:, cch * 128:(cch + 1) * 128], ps[:, 0:128]), deps=[tm], inc="sp")
            bank_free[bid] = ev
            evs.append(ev)
            prev_dma = tm
        final_tok = p.dma(OUT[j * 128:(j + 1) * 128, :], xrow[:], deps=evs,
                          queue="gpsimd", sem="gq")
        p.barrier(("tensor", "scalar", "gpsimd", "sync"))
    p.step("sync", lambda e: e.wait_ge(p.sem("gq"), p.semc["gq"]))



def phase_tables(p, T, IN):
    sb = p.sbuf
    T["Mrot"] = sb("Mrot", [128, 128], F16)
    T["band01"] = sb("band01", [128, 384], F32)
    T["jiota"] = sb("jiota", [128, 384], F32)
    T["if_row"] = sb("if_row", [1, 128], F32)
    T["lb128_row"] = sb("lb128_row", [1, 128], F32)
    T["lbm16col"] = sb("lbm16col", [128, 1], F32)
    T["lbp16col"] = sb("lbp16col", [128, 1], F32)
    T["onecol_bf"] = sb("onecol_bf", [128, 1], BF16)
    T["blkones2"] = sb("blkones2", [128, 2], BF16)
    T["bc2"] = sb("bc2", [66, 128], BF16)
    T["pihalf"] = sb("pihalf", [128, 1], F32)
    T["deneps"] = sb("deneps", [128, 1], F32)
    T["cthr_sb"] = sb("cthr_sb", [128, NT], F32)
    T["rowst"] = sb("rowst", [1, 512], F32)
    ia, ib, s0, s1 = T["iota_a"], T["iota_b"], T["scr0"], T["scr1"]

    g1 = p.step("gpsimd", lambda e: e.memset(T["pihalf"][:], np.pi / 2), inc="gp")
    g1 = p.step("gpsimd", lambda e: e.memset(T["deneps"][:], 1e-30), inc="gp")
    g1 = p.step("gpsimd", lambda e: e.memset(T["onecol_bf"][:], 1.0), inc="gp")

    g2 = p.step("gpsimd", lambda e: e.iota(T["if_row"][:].rearrange("a (b c) -> a b c", b=4),
               pattern=[[0, 4], [1, 32]], base=0, channel_multiplier=0,
               allow_small_or_imprecise_dtypes=True), inc="gp")
    v2 = p.step("vector", lambda e: e.tensor_scalar(out=T["lb128_row"][:], in0=T["if_row"][:],
                scalar1=2.0 / 89.6, scalar2=25.6 / 89.6, op0=ALU.mult, op1=ALU.add),
                deps=[g2], inc="vp")
    s2 = p.step("scalar", lambda e: e.activation(out=T["lb128_row"][:], in_=T["lb128_row"][:],
                func=AF.Ln), deps=[v2], inc="sp")
    tmc = p.step("tensor", lambda e: e.transpose(T["pbank0"][:, 0:1],
                 T["lb128_row"][:], T["ident_f32"][0:1, 0:1]), deps=[s2], inc="tp")
    vv = p.step("vector", lambda e: e.tensor_scalar(out=T["lbm16col"][:],
                in0=T["pbank0"][:, 0:1], scalar1=-16.0, scalar2=None, op0=ALU.mult),
                deps=[tmc], inc="vp")
    vv2 = p.step("vector", lambda e: e.tensor_scalar(out=T["lbp16col"][:],
                 in0=T["pbank0"][:, 0:1], scalar1=16.0, scalar2=None, op0=ALU.mult),
                 deps=[tmc], inc="vp")
    v3 = p.step("vector", lambda e: e.tensor_scalar(out=T["lb128_row"][:],
                in0=T["lb128_row"][:], scalar1=1.0 / 128.0, scalar2=None, op0=ALU.mult),
                deps=[tmc], inc="vp")
    s3 = p.step("scalar", lambda e: e.activation(out=T["if_row"][:], in_=T["if_row"][:],
                func=AF.Exp, scale=-np.log(10000.0) / 32.0), deps=[g2, v2], inc="sp")

    g3 = p.step("gpsimd", lambda e: e.iota(ia[:], pattern=[[1, 128]], base=0,
                channel_multiplier=-1, allow_small_or_imprecise_dtypes=True), inc="gp")
    g4 = p.step("gpsimd", lambda e: e.iota(ib[:].rearrange("p (a b) -> p a b", a=2),
                pattern=[[0, 2], [1, 64]], base=0, channel_multiplier=0,
                allow_small_or_imprecise_dtypes=True), inc="gp")
    v = p.step("vector", lambda e: e.tensor_scalar(out=ib[:], in0=ib[:], scalar1=32.0,
               scalar2=None, op0=ALU.is_ge), deps=[g4], inc="vp")
    g5 = p.step("gpsimd", lambda e: e.iota(T["pcol"][:], pattern=[[0, 1]], base=0,
                channel_multiplier=1, allow_small_or_imprecise_dtypes=True), inc="gp")
    v3b = p.step("vector", lambda e: e.tensor_scalar(out=T["pcol"][:], in0=T["pcol"][:],
                 scalar1=64.0, scalar2=None, op0=ALU.is_ge), deps=[g5], inc="vp")
    g6 = p.step("gpsimd", lambda e: e.iota(s0[:], pattern=[[1, 128]], base=0,
                channel_multiplier=0, allow_small_or_imprecise_dtypes=True), inc="gp")
    v4 = p.step("vector", lambda e: e.tensor_scalar(out=s0[:], in0=s0[:], scalar1=64.0,
                scalar2=None, op0=ALU.is_ge), deps=[g6], inc="vp")
    v5 = p.step("vector", lambda e: e.tensor_scalar(out=s0[:], in0=s0[:],
                scalar1=T["pcol"][:], scalar2=None, op0=ALU.is_equal), deps=[v4, v3b], inc="vp")
    v6 = p.step("vector", lambda e: e.tensor_scalar(out=s1[:], in0=ia[:], scalar1=32.0,
                scalar2=None, op0=ALU.is_equal), deps=[g3], inc="vp")
    v7 = p.step("vector", lambda e: e.tensor_tensor(out=s1[:], in0=s1[:], in1=ib[:],
                op=ALU.mult), deps=[v6, v], inc="vp")
    v8 = p.step("vector", lambda e: e.tensor_scalar(out=ib[:], in0=ib[:], scalar1=-1.0,
                scalar2=1.0, op0=ALU.mult, op1=ALU.add), deps=[v7], inc="vp")
    v9 = p.step("vector", lambda e: e.tensor_scalar(out=ia[:], in0=ia[:], scalar1=-32.0,
                scalar2=None, op0=ALU.is_equal), deps=[v8], inc="vp")
    v10 = p.step("vector", lambda e: e.tensor_tensor(out=ia[:], in0=ia[:], in1=ib[:],
                 op=ALU.mult), deps=[v9], inc="vp")
    v11 = p.step("vector", lambda e: e.tensor_tensor(out=s1[:], in0=s1[:], in1=ia[:],
                 op=ALU.subtract), deps=[v10], inc="vp")
    p.step("vector", lambda e: e.tensor_tensor(out=T["Mrot"][:], in0=s1[:], in1=s0[:],
           op=ALU.mult), deps=[v11, v5], inc="vp")

    g8 = p.step("gpsimd", lambda e: e.iota(T["band01"][:], pattern=[[1, 384]], base=0,
                channel_multiplier=-1, allow_small_or_imprecise_dtypes=True), inc="gp")
    v12 = p.step("vector", lambda e: e.tensor_scalar(out=T["jiota"][:], in0=T["band01"][:],
                 scalar1=0.0, scalar2=None, op0=ALU.is_ge), deps=[g8], inc="vp")
    v13 = p.step("vector", lambda e: e.tensor_scalar(out=T["band01"][:], in0=T["band01"][:],
                 scalar1=256.0, scalar2=None, op0=ALU.is_le), deps=[v12], inc="vp")
    v14 = p.step("vector", lambda e: e.tensor_tensor(out=T["band01"][:], in0=T["band01"][:],
                 in1=T["jiota"][:], op=ALU.mult), deps=[v13], inc="vp")
    g7 = p.step("gpsimd", lambda e: e.iota(T["jiota"][:], pattern=[[1, 384]], base=0,
                channel_multiplier=0, allow_small_or_imprecise_dtypes=True),
                deps=[v14], inc="gp")

    g10 = p.step("gpsimd", lambda e: e.iota(T["pcol"][:], pattern=[[0, 1]], base=0,
                 channel_multiplier=1, allow_small_or_imprecise_dtypes=True), inc="gp")
    v18 = p.step("vector", lambda e: e.tensor_scalar(out=T["blkones2"][:, 1:2],
                 in0=T["pcol"][:], scalar1=64.0, scalar2=None, op0=ALU.is_ge),
                 deps=[g10], inc="vp")
    v19 = p.step("vector", lambda e: e.tensor_scalar(out=T["blkones2"][:, 0:1],
                 in0=T["pcol"][:], scalar1=64.0, scalar2=None, op0=ALU.is_lt),
                 deps=[g10], inc="vp")
    tmm = p.step("tensor", lambda e: e.transpose(
        T["pbank1"][:].bitcast(BF16)[0:2, 0:128], T["blkones2"][:], T["ident_bf"][:]),
        deps=[v18, v19], inc="tp")
    p.step("scalar", lambda e: e.copy(T["bc2"][0:2, :], T["pbank1"][:].bitcast(BF16)[0:2, 0:128]),
           deps=[tmm], inc="sp")
    tmm2 = p.step("tensor", lambda e: e.transpose(
        T["pbank1"][:].bitcast(BF16)[64:66, 0:128], T["blkones2"][:], T["ident_bf"][:]),
        deps=[v18, v19], inc="tp")
    p.step("scalar", lambda e: e.copy(T["bc2"][64:66, :], T["pbank1"][:].bitcast(BF16)[64:66, 0:128]),
           deps=[tmm2], inc="sp")
    T["tstage"] = sb("tstage", [64, 128], F32)
    dct = p.dma(T["tstage"][0:NT, :], IN["cthr"])
    tc = p.step("tensor", lambda e: e.transpose(T["pbank2"][:, 0:NT],
                T["tstage"][0:NT, :], T["ident_f32"][0:NT, 0:NT]),
                deps=[dct, p.tok("sp")], inc="tp")
    p.step("scalar", lambda e: e.copy(T["cthr_sb"][:], T["pbank2"][:, 0:NT]),
           deps=[tc], inc="sp")
    T["svb"] = sb("svb", [128, TAU], BF16)
    prev_mm = tc
    prev_cp = p.tok("sp")
    for i in range(8):
        c0 = i * 480
        d = p.dma(T["rowst"][0:1, 0:480], IN["sv"][0:1, c0:c0 + 480], deps=[prev_mm])
        prev_mm = p.step("tensor", lambda e, c0=c0: e.matmul(
            T["pbank3"][:, 0:480], T["ones_row"][:], T["rowst"][0:1, 0:480],
            start=True, stop=True), deps=[d, prev_cp], inc="tp")
        prev_cp = p.step("scalar", lambda e, c0=c0: e.copy(
            T["svb"][:, c0:c0 + 480], T["pbank3"][:, 0:480]),
            deps=[prev_mm], inc="sp")
    p.barrier()


def ln_row_stats(p, T, y2view, sqview, nm, n, deps):
    pbs = T["pbank6"]
    pbs2 = T["pbank7"]
    mms = []
    for dc in range(8):
        mms.append(p.step("tensor", lambda e, dc=dc: e.matmul(
            pbs[0:1, 0:n], T["onecol_bf"][:], y2view[:, dc, :],
            start=(dc == 0), stop=(dc == 7)), deps=deps, inc="tp"))
    sq = p.step("scalar", lambda e: e.activation(out=sqview[:], in_=y2view[:],
                func=AF.Square), deps=deps, inc="sp")
    mms2 = []
    for dc in range(8):
        mms2.append(p.step("tensor", lambda e, dc=dc: e.matmul(
            pbs2[0:1, 0:n], T["onecol_bf"][:], sqview[:, dc, :],
            start=(dc == 0), stop=(dc == 7)), deps=[sq], inc="tp"))
    st = T[nm]
    v1 = p.step("vector", lambda e: e.tensor_scalar(out=st["nmu"][0:1, 0:n],
                in0=pbs[0:1, 0:n], scalar1=-1.0 / D, scalar2=None, op0=ALU.mult),
                deps=mms, inc="vp")
    v2 = p.step("vector", lambda e: e.tensor_scalar(out=st["ex2"][0:1, 0:n],
                in0=pbs2[0:1, 0:n], scalar1=1.0 / D, scalar2=None, op0=ALU.mult),
                deps=mms2, inc="vp")
    v3 = p.step("vector", lambda e: e.tensor_tensor(out=st["musq"][0:1, 0:n],
                in0=st["nmu"][0:1, 0:n], in1=st["nmu"][0:1, 0:n], op=ALU.mult),
                deps=[v1], inc="vp")
    v4 = p.step("vector", lambda e: e.tensor_tensor(out=st["var"][0:1, 0:n],
                in0=st["ex2"][0:1, 0:n], in1=st["musq"][0:1, 0:n], op=ALU.subtract),
                deps=[v2, v3], inc="vp")
    s1 = p.step("scalar", lambda e: e.activation(out=st["rst"][0:1, 0:n],
                in_=st["var"][0:1, 0:n], func=AF.Ln, bias=T["eps_col"][0:1, :]),
                deps=[v4], inc="sp")
    s2 = p.step("scalar", lambda e: e.activation(out=st["rst"][0:1, 0:n],
                in_=st["rst"][0:1, 0:n], func=AF.Exp, scale=-0.5), deps=[s1], inc="sp")
    return v1, s2



def phase_transformer(p, T, IN, DR, dbg, depth):
    sb = p.sbuf
    OVL = sb("OVL", [128, 70144], BF16)
    qkvW = OVL[:, 0:24576].rearrange("p (a b) -> p a b", b=3072)
    woW = OVL[:, 24576:32768].rearrange("p (a b) -> p a b", b=1024)
    tabsW = OVL[:, 32768:47872].rearrange("p (a b) -> p a b", b=3776)
    y2b = OVL[:, 47872:51968].rearrange("p (a b) -> p a b", b=512)
    klin = OVL[:, 51968:58112].rearrange("p (a b) -> p a b", b=768)
    vt = [OVL[:, 58112 + 1024 * i:58112 + 1024 * (i + 1)] for i in range(6)]
    kshift = OVL[:, 64256:66304].rearrange("p (a b) -> p a b", b=256)
    Smb = OVL[:, 66304:67072].bitcast(F32)
    mbt = OVL[:, 67072:67456]
    Pb = OVL[:, 67456:67840]
    Ps = OVL[:, 67840:68224]
    PT = OVL[:, 68224:68608]
    rn2 = OVL[0:2, 68608:69632].bitcast(F32)
    rn2b = OVL[0:2, 69632:70144]
    w1W = OVL[:, 0:45056].rearrange("p (a b) -> p a b", b=5632)
    w2W = OVL[:, 45056:67584].rearrange("p (a b) -> p a b", b=1024)

    qb = sb("qb", [128, 8, 512], BF16)
    hbuf = sb("hbuf", [128, 11264], BF16)
    attnb = hbuf[:, 0:4096].rearrange("p (a b) -> p a b", b=512)
    sqv = hbuf[:, 4096:8192].rearrange("p (a b) -> p a b", b=512)
    hview = hbuf[:].rearrange("p (a b) -> p a b", b=512)
    tmpq = sb("tmpq", [128, 512], F32)
    tmpq2 = sb("tmpq2", [128, 512], F32)
    qnb = sb("qnb", [128, 512], BF16)
    rotb = sb("rotb", [128, 512], BF16)
    t1b = sb("t1b", [128, 512], BF16)
    t2b = sb("t2b", [128, 512], BF16)
    sqc = sb("sqc", [128, 512], BF16)
    xinoc = sb("xinoc", [128, 512], F32)
    xmc = sb("xmc", [128, 512], F32)
    den16 = sb("den16", [128, 16], F32)
    rec16 = sb("rec16", [128, 16], F32)
    nmax = sb("nmax", [128, 1], F32)
    Smb2 = tmpq[:, 0:384]
    Pb2 = t1b[:, 0:384]
    T["QO"] = dict(
        sqc_o=OVL[:, 64256:64768], qnb_o=OVL[:, 64768:65280],
        rotb_o=OVL[:, 65280:65792], t1b_o=OVL[:, 65792:66304],
        t2b_o=OVL[:, 67072:67584],
        rn2o=OVL[64:66, 68608:69632].bitcast(F32),
        rn2bo=OVL[64:66, 69632:70144])
    st_r0 = sb("st_nmu", [1, 512], F32)
    st_r1 = sb("st_ex2", [1, 512], F32)
    st_r2 = sb("st_musq", [1, 512], F32)
    strow = {"nmu": st_r0, "ex2": st_r1, "musq": st_r2, "var": st_r1, "rst": st_r1}
    T["strow"] = strow
    T["tmpA"], T["tmpB"], T["tmpC"], T["tmpD"] = tmpq, tmpq2, xinoc, xmc
    gacol = sb("gacol", [128, 8], F32)
    becol = sb("becol", [128, 8], F32)
    b2col = sb("b2col", [128, 8], F32)
    b1acol = sb("b1acol", [128, 22], F32)
    b1gcol = sb("b1gcol", [128, 22], F32)
    qscol = sb("qscol", [128, 1], F32)
    kscol = sb("kscol", [128, 1], F32)
    vstage = T["tstage"]
    glub = OVL[:, 68608:69120]

    banks = [T[f"pbank{i}"] for i in range(8)]
    DR["xmid"] = p.nc.dram_tensor("xmid_dram", [D, TAU], F32).ap()
    DR["xn"] = p.nc.dram_tensor("xn_dram", [D, TAU], BF16).ap()

    gz0 = p.step("gpsimd", lambda e: e.memset(xinoc[:, 0:8], 0.0), inc="gp")
    for drn in ("xmid", "xt1"):
        dr = DR[drn]
        p.dma(mkap(dr, [(TAU, 128), (128 * TAU, 8), (1, 1)], dr.offset + 383),
              xinoc[:, 0:8].rearrange("p (a b) -> p a b", b=1), deps=[gz0],
              queue="gpsimd", sem="gq", allow_slow_non_contiguous=True)

    for l in range(depth):
        xin_d = DR["xt0"] if l % 2 == 0 else DR["xt1"]
        xout_d = DR["xt1"] if l % 2 == 0 else DR["xt0"]
        A0 = AST[l]
        E = EXT[l]
        kv0 = A0 - 256
        lvec_toks = []
        for (dst, srcap, n) in ((gacol, IN["ln_g"][l:l + 1, :], 8),
                                (becol, IN["ln_b"][l:l + 1, :], 8),
                                (b2col, IN["b2"][l:l + 1, :], 8)):
            dt = p.dma(vstage[0:n, 0:128], mkap(srcap, [(128, n), (1, 128)], srcap.offset),
                       deps=[p.tok("tp")])
            tm = p.step("tensor", lambda e, dst=dst, n=n: e.transpose(
                banks[7][:, 0:n], vstage[0:n, 0:128], T["ident_f32"][0:n, 0:n]),
                deps=[dt, p.tok("sp")], inc="tp")
            lvec_toks.append(p.step("scalar", lambda e, dst=dst, n=n: e.copy(
                dst[:, 0:n], banks[7][:, 0:n]), deps=[tm], inc="sp"))
        for (dst, off) in ((b1acol, 0), (b1gcol, 2730)):
            gz = p.step("gpsimd", lambda e: e.memset(vstage[:, :], 0.0),
                        deps=[p.tok("tp")], inc="gp")
            dt1 = p.dma(vstage[0:21, 0:128], mkap(IN["b1"], [(128, 21), (1, 128)],
                        IN["b1"].offset + l * 5460 + off), deps=[gz])
            dt2 = p.dma(vstage[21:22, 0:42], mkap(IN["b1"], [(128, 1), (1, 42)],
                        IN["b1"].offset + l * 5460 + off + 21 * 128), deps=[gz])
            tm = p.step("tensor", lambda e: e.transpose(
                banks[7][:, 0:22], vstage[0:22, 0:128], T["ident_f32"][0:22, 0:22]),
                deps=[dt1, dt2, p.tok("sp")], inc="tp")
            p.step("scalar", lambda e, dst=dst: e.copy(dst[:], banks[7][:, 0:22]),
                   deps=[tm], inc="sp")
        prev_fold = None
        for (dst, srcap, fold8) in ((qscol, IN["qs"], True), (kscol, IN["ks"], False)):
            dt = p.dma(vstage[0:1, 0:64], mkap(srcap, [(64, 1), (1, 64)],
                       srcap.offset + l * 64), deps=[p.tok("tp")])
            tm = p.step("tensor", lambda e: e.transpose(
                banks[7][0:64, 0:1], vstage[0:1, 0:64], T["ident_f32"][0:1, 0:1]),
                deps=[dt, p.tok("sp")], inc="tp")
            sv_ = p.step("scalar", lambda e, dst=dst: e.copy(dst[0:64, :], banks[7][0:64, 0:1]),
                         deps=[tm], inc="sp")
            dd = p.dma(dst[64:128, :], dst[0:64, :], deps=[sv_, prev_fold],
                       queue="gpsimd", sem="gq")
            if fold8:
                prev_fold = p.step("vector", lambda e, dst=dst: e.tensor_scalar(
                    out=dst[:], in0=dst[:], scalar1=8.0, scalar2=None, op0=ALU.mult),
                    deps=[dd], inc="vp")
        for dc in range(8):
            p.dma(qkvW[:, dc, :], IN["Wqkv"][l, dc * 128:(dc + 1) * 128, :],
                  queue="gpsimd", sem="gq", deps=[p.tok("tp")])
            p.dma(woW[:, dc, :], IN["Wo"][l, dc * 128:(dc + 1) * 128, :],
                  queue="gpsimd", sem="gq")
        Ekv = E + 256
        emit_tables_layer(p, T, IN, tabsW, kv0, Ekv, banks)
        p.barrier()

        sizes = []
        rem = E
        first = E - (E // 512) * 512
        if first:
            sizes.append(first)
            rem -= first
        sizes += [512] * (rem // 512)
        emit_attn_block(p, T, IN, DR, l, xin_d, kv0, 256, None, True,
                        qkvW, woW, tabsW, y2b, klin, vt, qb, attnb, sqv, kshift,
                        tmpq, tmpq2, qnb, rotb, t1b, t2b, sqc, Smb, Smb2, mbt, Pb, Pb2, Ps, PT,
                        xinoc, xmc, rn2, rn2b, den16, rec16, nmax, strow,
                        gacol, becol, qscol, kscol, banks, kv0, -1)
        p.barrier()
        t0 = A0
        prev_s = 256
        vbase = 2
        for bi, s_ in enumerate(sizes):
            emit_attn_block(p, T, IN, DR, l, xin_d, t0, s_, prev_s, False,
                            qkvW, woW, tabsW, y2b, klin, vt, qb, attnb, sqv, kshift,
                            tmpq, tmpq2, qnb, rotb, t1b, t2b, sqc, Smb, Smb2, mbt, Pb, Pb2, Ps, PT,
                            xinoc, xmc, rn2, rn2b, den16, rec16, nmax, strow,
                            gacol, becol, qscol, kscol, banks, kv0, vbase)
            p.barrier()
            t0 += s_
            prev_s = s_
            vbase += s_ // 128
        for dc in range(8):
            p.dma(w1W[:, dc, 0:2730], IN["W1"][l, dc * 128:(dc + 1) * 128, 0:2730],
                  queue="gpsimd", sem="gq", deps=[p.tok("tp"), p.tok("vp"), p.tok("sp")])
            p.dma(w1W[:, dc, 2816:5546], IN["W1"][l, dc * 128:(dc + 1) * 128, 2730:5460],
                  queue="gpsimd", sem="gq")
            g1 = p.step("gpsimd", lambda e, dc=dc: e.memset(w1W[:, dc, 2730:2816], 0.0), inc="gp")
            g1 = p.step("gpsimd", lambda e, dc=dc: e.memset(w1W[:, dc, 5546:5632], 0.0), inc="gp")
        for hc in range(22):
            if hc < 21:
                p.dma(w2W[:, hc, :], IN["W2"][l, hc * 128:(hc + 1) * 128, :],
                      queue="gpsimd", sem="gq")
            else:
                g2 = p.step("gpsimd", lambda e, hc=hc: e.memset(
                    w2W[:, hc, :].rearrange("p a -> p a"), 0.0), inc="gp")
                p.dma(w2W[0:42, hc, :], IN["W2"][l, 21 * 128:2730, :],
                      queue="gpsimd", sem="gq", deps=[g2])
        p.barrier()
        t0 = A0
        for bi, s_ in enumerate(sizes):
            emit_b2_block(p, T, DR, l, xout_d, t0, s_, w1W, w2W, hview, qb, glub,
                          xinoc, xmc, b1acol, b1gcol, b2col, banks)
            p.barrier()
            t0 += s_


def emit_tables_layer(p, T, IN, tabsW, kv0, Ekv, banks):
    npcs = (Ekv + 511) // 512
    ta, tb = T["tmpA"], T["tmpB"]
    tc_, td = T["tmpC"], T["tmpD"]
    TMP = {"tmpA": ta, "tmpB": tb, "tmpC": tc_, "tmpD": td}
    prev = [p.tok("tp")]
    for i in range(npcs):
        c0 = i * 512
        n = min(512, Ekv - c0)
        tau0 = kv0 + c0
        pb = banks[6]
        pc = banks[7]
        d = p.dma(T["rowst"][0:1, 0:n], IN["pos"][0:1, tau0:tau0 + n], deps=prev)
        m1 = p.step("tensor", lambda e, n=n, pb=pb: e.matmul(
            pb[:, 0:n], T["if_row"][:], T["rowst"][0:1, 0:n],
            start=True, stop=True), deps=[d, p.tok("sp"), p.tok("vp")], inc="tp")
        m2 = p.step("tensor", lambda e, n=n, pc=pc: e.matmul(
            pc[:, 0:n], T["lb128_row"][:], T["rowst"][0:1, 0:n],
            start=True, stop=True), deps=[d], inc="tp")
        prev = [m1, m2]
        v1 = p.step("vector", lambda e, n=n, pb=pb: e.tensor_scalar(
            out=ta[:, 0:n], in0=pb[:, 0:n], scalar1=1.0 / (2 * np.pi),
            scalar2=12582912.0, op0=ALU.mult, op1=ALU.add), deps=[m1], inc="vp")
        v2 = p.step("vector", lambda e, n=n: e.tensor_scalar(
            out=ta[:, 0:n], in0=ta[:, 0:n], scalar1=-12582912.0,
            scalar2=None, op0=ALU.add), deps=[v1], inc="vp")
        v3 = p.step("vector", lambda e, n=n, pb=pb: e.scalar_tensor_tensor(
            out=tb[:, 0:n], in0=ta[:, 0:n], scalar=-2 * np.pi,
            in1=pb[:, 0:n], op0=ALU.mult, op1=ALU.add), deps=[v2], inc="vp")
        v3c = p.step("vector", lambda e, n=n: e.tensor_scalar(
            out=tb[:, 0:n], in0=tb[:, 0:n], scalar1=3.141592, scalar2=-3.141592,
            op0=ALU.min, op1=ALU.max), deps=[v3], inc="vp")
        s1 = p.step("scalar", lambda e, n=n: e.activation(
            out=tc_[:, 0:n], in_=tb[:, 0:n], func=AF.Sin), deps=[v3c], inc="sp")
        v4 = p.step("vector", lambda e, n=n, pb=pb: e.tensor_scalar(
            out=ta[:, 0:n], in0=pb[:, 0:n], scalar1=1.0 / (2 * np.pi),
            scalar2=0.25, op0=ALU.mult, op1=ALU.add), deps=[s1], inc="vp")
        v4b = p.step("vector", lambda e, n=n: e.tensor_scalar(
            out=ta[:, 0:n], in0=ta[:, 0:n], scalar1=12582912.0,
            scalar2=None, op0=ALU.add), deps=[v4], inc="vp")
        v5 = p.step("vector", lambda e, n=n: e.tensor_scalar(
            out=ta[:, 0:n], in0=ta[:, 0:n], scalar1=-12582912.0,
            scalar2=None, op0=ALU.add), deps=[v4b], inc="vp")
        v6 = p.step("vector", lambda e, n=n, pb=pb: e.scalar_tensor_tensor(
            out=tb[:, 0:n], in0=ta[:, 0:n], scalar=-2 * np.pi,
            in1=pb[:, 0:n], op0=ALU.mult, op1=ALU.add), deps=[v5], inc="vp")
        v6c = p.step("vector", lambda e, n=n: e.tensor_scalar(
            out=tb[:, 0:n], in0=tb[:, 0:n], scalar1=-3.141592 - np.pi / 2,
            scalar2=3.141592 - np.pi / 2, op0=ALU.max, op1=ALU.min),
            deps=[v6], inc="vp")
        s2 = p.step("scalar", lambda e, n=n: e.activation(
            out=td[:, 0:n], in_=tb[:, 0:n], func=AF.Sin,
            bias=T["pihalf"][:]), deps=[v6c], inc="sp")
        s3 = p.step("scalar", lambda e, n=n, pc=pc: e.activation(
            out=ta[:, 0:n], in_=pc[:, 0:n], func=AF.Exp,
            bias=T["lbm16col"][:]), deps=[m2, s2], inc="sp")
        s4 = p.step("scalar", lambda e, n=n, pc=pc: e.activation(
            out=tb[:, 0:n], in_=pc[:, 0:n], func=AF.Exp, scale=-1.0,
            bias=T["lbp16col"][:]), deps=[s3], inc="sp")
        for (ti, srcT, scT) in ((0, "tmpD", "tmpA"), (1, "tmpC", "tmpA"),
                                (2, "tmpD", "tmpB"), (3, "tmpC", "tmpB")):
            prev.append(p.step("vector", lambda e, ti=ti, srcT=srcT, scT=scT, n=n,
                        c0=c0: e.tensor_tensor(
                out=tabsW[:, ti, c0:c0 + n], in0=TMP[srcT][:, 0:n], in1=TMP[scT][:, 0:n],
                op=ALU.mult), deps=[s2, s4], inc="vp"))

def emit_attn_block(p, T, IN, DR, l, xin_d, t0, s, prev_s, is_prologue,
                    qkvW, woW, tabsW, y2b, klin, vt, qb, attnb, sqv, kshift,
                    tmpq, tmpq2, qnb, rotb, t1b, t2b, sqc, Smb, Smb2, mbt, Pb, Pb2, Ps, PT,
                    xinoc, xmc, rn2, rn2b, den16, rec16, nmax, strow,
                    gacol, becol, qscol, kscol, banks, kv0, vbase):
    bk = {}

    def bank_dep(b):
        return bk.get(b)

    if not is_prologue:
        if prev_s >= 256:
            sh = p.step("vector", lambda e: e.tensor_copy(
                klin[:, :, 0:256], klin[:, :, prev_s:prev_s + 256]), inc="vp")
        else:
            sh1 = p.step("vector", lambda e: e.tensor_copy(
                kshift[:, :, :], klin[:, :, prev_s:prev_s + 256]), inc="vp")
            sh = p.step("vector", lambda e: e.tensor_copy(
                klin[:, :, 0:256], kshift[:, :, :]), deps=[sh1], inc="vp")
    else:
        sh = None

    d1 = p.dma(y2b[:, 0:4, 0:s], mkap(xin_d, [(TAU, 128), (128 * TAU, 4), (1, s)],
               xin_d.offset + t0), queue="gpsimd", sem="gq")
    d2 = p.dma(y2b[:, 4:8, 0:s], mkap(xin_d, [(TAU, 128), (128 * TAU, 4), (1, s)],
               xin_d.offset + 512 * TAU + t0 - 1), queue="gpsimd", sem="gq")
    yv = p.step("vector", lambda e: e.tensor_tensor(
        out=y2b[:, 4:8, 0:s], in0=y2b[:, 4:8, 0:s],
        in1=T["svb"][:, t0:t0 + s].unsqueeze(1).broadcast_to((128, 4, s)),
        op=ALU.mult), deps=[d1, d2], inc="vp")

    tb0 = t0 - kv0

    QO = T["QO"]
    strm = [
        dict(tq=tmpq, sq=sqc, qn=qnb, ro=rotb, t1=t1b, t2=t2b,
             rn=rn2, rnb=rn2b, nr=(0, 2), rbk=4, rok=5, bcb=0),
        dict(tq=tmpq2, sq=QO["sqc_o"], qn=QO["qnb_o"], ro=QO["rotb_o"],
             t1=QO["t1b_o"], t2=QO["t2b_o"], rn=QO["rn2o"], rnb=QO["rn2bo"],
             nr=(64, 66), rbk=6, rok=7, bcb=64),
    ]
    for oc in range(16):
        po = oc % 2
        S_ = strm[po]
        b = banks[po]
        nlo, nhi = S_["nr"]
        rbk, rok, bcb = S_["rbk"], S_["rok"], S_["bcb"]
        mms = []
        for dc in range(8):
            mms.append(p.step("tensor", lambda e, oc=oc, dc=dc, b=b: e.matmul(
                b[:, 0:s], qkvW[:, dc, 128 * oc:128 * (oc + 1)], y2b[:, dc, 0:s],
                start=(dc == 0), stop=(dc == 7)),
                deps=[yv, bank_dep(po)], inc="tp"))
        shd = [sh] if (sh is not None and po == 1) else []
        e1 = p.step("scalar", lambda e, b=b, S_=S_: e.copy(S_["tq"][:, 0:s], b[:, 0:s]),
                    deps=mms + [bank_dep(("tmpq", po))] + shd, inc="sp")
        e2 = p.step("scalar", lambda e, b=b, S_=S_: e.activation(out=S_["sq"][:, 0:s],
                    in_=b[:, 0:s], func=AF.Square),
                    deps=mms + [bank_dep(("sqc", po))] + shd, inc="sp")
        bk[po] = e2
        nm = p.step("tensor", lambda e, S_=S_, nlo=nlo, nhi=nhi: e.matmul(
            banks[3][nlo:nhi, 0:s], T["blkones2"][:], S_["sq"][:, 0:s],
            start=True, stop=True), deps=[e2, bank_dep(("n3", po))], inc="tp")
        bk[("sqc", po)] = nm
        v1 = p.step("vector", lambda e, S_=S_, nlo=nlo, nhi=nhi: e.tensor_scalar(
            out=S_["rn"][:, 0:s], in0=banks[3][nlo:nhi, 0:s], scalar1=1e-24,
            scalar2=None, op0=ALU.max), deps=[nm, bank_dep(("rn", po))], inc="vp")
        bk[("n3", po)] = v1
        bk[3] = v1
        s1 = p.step("scalar", lambda e, S_=S_: e.activation(out=S_["rn"][:, 0:s],
                    in_=S_["rn"][:, 0:s], func=AF.Ln), deps=[v1], inc="sp")
        s2 = p.step("scalar", lambda e, S_=S_: e.activation(out=S_["rnb"][:, 0:s],
                    in_=S_["rn"][:, 0:s], func=AF.Exp, scale=-0.5),
                    deps=[s1, bank_dep(("rb", po))], inc="sp")
        bk[("rn", po)] = s2
        rb = p.step("tensor", lambda e, S_=S_, rbk=rbk, bcb=bcb: e.matmul(
            banks[rbk][:, 0:s], T["bc2"][bcb:bcb + 2, :], S_["rnb"][:, 0:s],
            start=True, stop=True), deps=[s2, bank_dep(rbk)], inc="tp")
        bk[("rb", po)] = rb
        qcol = qscol if oc < 8 else kscol
        ap = p.step("vector", lambda e, qcol=qcol, S_=S_, rbk=rbk: e.scalar_tensor_tensor(
            out=S_["qn"][:, 0:s], in0=S_["tq"][:, 0:s], scalar=qcol[:],
            in1=banks[rbk][:, 0:s], op0=ALU.mult, op1=ALU.mult),
            deps=[rb, e1, bank_dep(("qnb", po))], inc="vp")
        bk[rbk] = ap
        bk[("tmpq", po)] = ap
        rm = p.step("tensor", lambda e, S_=S_, rok=rok: e.matmul(
            banks[rok][:, 0:s], T["Mrot"][:], S_["qn"][:, 0:s],
            start=True, stop=True), deps=[ap, bank_dep(rok)], inc="tp")
        re = p.step("scalar", lambda e, S_=S_, rok=rok: e.copy(S_["ro"][:, 0:s],
                    banks[rok][:, 0:s]), deps=[rm, bank_dep(("rotb", po))], inc="sp")
        bk[rok] = re
        ti_c, ti_s = (0, 1) if oc < 8 else (2, 3)
        c1 = p.step("vector", lambda e, ti_c=ti_c, S_=S_: e.tensor_tensor(
            out=S_["t1"][:, 0:s], in0=S_["qn"][:, 0:s], in1=tabsW[:, ti_c, tb0:tb0 + s],
            op=ALU.mult), deps=[ap, bank_dep(("t12", po))], inc="vp")
        c2 = p.step("vector", lambda e, ti_s=ti_s, S_=S_: e.tensor_tensor(
            out=S_["t2"][:, 0:s], in0=S_["ro"][:, 0:s], in1=tabsW[:, ti_s, tb0:tb0 + s],
            op=ALU.mult), deps=[re], inc="vp")
        bk[("qnb", po)] = c1
        bk[("rotb", po)] = c2
        if oc < 8:
            dstap = qb[:, oc, 0:s]
        else:
            dstap = klin[:, oc - 8, 256:256 + s]
        dd = [sh] if sh is not None else []
        fa = p.step("vector", lambda e, dstap=dstap, S_=S_: e.tensor_tensor(
            out=dstap, in0=S_["t1"][:, 0:s], in1=S_["t2"][:, 0:s], op=ALU.add),
            deps=[c1, c2] + dd, inc="vp")
        bk[("t12", po)] = fa

    ntt = s // 128
    for tt in range(ntt):
        vidx = (vbase + tt) % 6 if vbase >= 0 else tt
        for vh in range(2):
            mms = []
            for dc in range(8):
                mms.append(p.step("tensor", lambda e, tt=tt, vh=vh, dc=dc: e.matmul(
                    banks[2][:, 0:512], y2b[:, dc, 128 * tt:128 * (tt + 1)],
                    qkvW[:, dc, 2048 + 512 * vh:2048 + 512 * (vh + 1)],
                    start=(dc == 0), stop=(dc == 7)),
                    deps=[yv, bank_dep(2)], inc="tp"))
            bk[2] = p.step("scalar", lambda e, vidx=vidx, vh=vh: e.copy(
                vt[vidx][:, 512 * vh:512 * (vh + 1)], banks[2][:, 0:512]),
                deps=mms, inc="sp")

    if is_prologue:
        return

    qk_tok = p.tok("vp")
    for t in range(ntt):
        tau_tile = (t0 + 128 * t) // 128
        m1 = p.step("vector", lambda e, tau_tile=tau_tile: e.tensor_scalar(
            out=Smb[:], in0=T["jiota"][:], scalar1=T["cthr_sb"][:, tau_tile:tau_tile + 1],
            scalar2=None, op0=ALU.is_ge), deps=[qk_tok, bank_dep("Smb")], inc="vp")
        m2 = p.step("vector", lambda e: e.tensor_tensor(out=Smb[:], in0=Smb[:],
                    in1=T["band01"][:], op=ALU.mult), deps=[m1], inc="vp")
        m3 = p.step("vector", lambda e: e.tensor_scalar(out=mbt[:], in0=Smb[:],
                    scalar1=-1.0, scalar2=3.0e38, op0=ALU.add, op1=ALU.mult),
                    deps=[m2], inc="vp")
        for hp in range(8):
            h0, h1 = 2 * hp, 2 * hp + 1
            ab_ = banks[4 + hp // 4]
            ccol = (hp % 4) * 128
            smA = p.step("tensor", lambda e, hp=hp, t=t: e.matmul(
                banks[2][:, 0:384], qb[0:64, hp, 128 * t:128 * (t + 1)],
                klin[0:64, hp, 128 * t:128 * t + 384],
                start=True, stop=True), deps=[m3, bank_dep(2), qk_tok], inc="tp")
            smB = p.step("tensor", lambda e, hp=hp, t=t: e.matmul(
                banks[3][:, 0:384], qb[64:128, hp, 128 * t:128 * (t + 1)],
                klin[64:128, hp, 128 * t:128 * t + 384],
                start=True, stop=True), deps=[bank_dep(3)], inc="tp")
            maA = p.step("vector", lambda e: e.tensor_tensor(out=Smb[:],
                         in0=banks[2][:, 0:384], in1=mbt[:], op=ALU.add),
                         deps=[smA, bank_dep("Smb")], inc="vp")
            bk[2] = maA
            maB = p.step("vector", lambda e: e.tensor_tensor(out=Smb2[:],
                         in0=banks[3][:, 0:384], in1=mbt[:], op=ALU.add),
                         deps=[smB, bank_dep("Smb2")], inc="vp")
            bk[3] = maB
            exA = p.step("scalar", lambda e, h0=h0: e.activation(out=Pb[:], in_=Smb[:],
                         func=AF.Exp, accum_out=den16[:, h0:h0 + 1]),
                         deps=[maA, bank_dep("Pb"), bank_dep(("den", hp))], inc="sp")
            bk["Smb"] = exA
            exB = p.step("scalar", lambda e, h1=h1: e.activation(out=Pb2[:], in_=Smb2[:],
                         func=AF.Exp, accum_out=den16[:, h1:h1 + 1]),
                         deps=[maB, bank_dep("Pb2")], inc="sp")
            bk["Smb2"] = exB
            l1 = p.step("scalar", lambda e, h0=h0: e.activation(out=rec16[:, h0:h0 + 2],
                        in_=den16[:, h0:h0 + 2], func=AF.Ln, bias=T["deneps"][:]),
                        deps=[exA, exB, bank_dep(("rec", hp))], inc="sp")
            l2 = p.step("scalar", lambda e, h0=h0: e.activation(out=rec16[:, h0:h0 + 2],
                        in_=rec16[:, h0:h0 + 2], func=AF.Exp, scale=-1.0),
                        deps=[l1], inc="sp")
            bk[("den", hp)] = l1
            tsA = p.step("vector", lambda e, h0=h0: e.tensor_scalar(out=Ps[:], in0=Pb[:],
                         scalar1=rec16[:, h0:h0 + 1], scalar2=None, op0=ALU.mult),
                         deps=[l2, bank_dep("Ps")], inc="vp")
            bk["Pb"] = tsA
            ptA = []
            for kc in range(3):
                ptA.append(p.step("tensor", lambda e, kc=kc: e.transpose(
                    banks[7][:].bitcast(BF16)[:, 128 * kc:128 * (kc + 1)],
                    Ps[:, 128 * kc:128 * (kc + 1)], T["ident_bf"][:]),
                    deps=[tsA, bank_dep(7)], inc="tp"))
            peA = p.step("vector", lambda e: e.tensor_copy(PT[:],
                         banks[7][:].bitcast(BF16)[:, 0:384]),
                         deps=ptA + [bank_dep("PT")], inc="vp")
            bk[7] = peA
            avA = []
            for kc in range(3):
                vidx = (vbase + t - 2 + kc) % 6 if vbase >= 0 else t - 2 + kc
                avA.append(p.step("tensor", lambda e, h0=h0, kc=kc, vidx=vidx, ab_=ab_,
                           ccol=ccol: e.matmul(
                    ab_[0:64, ccol:ccol + 128], vt[vidx][:, 64 * h0:64 * (h0 + 1)],
                    PT[:, 128 * kc:128 * (kc + 1)],
                    start=(kc == 0), stop=(kc == 2)),
                    deps=[peA, bank_dep(4 + hp // 4)], inc="tp"))
            tsB = p.step("vector", lambda e, h1=h1: e.tensor_scalar(out=Ps[:], in0=Pb2[:],
                         scalar1=rec16[:, h1:h1 + 1], scalar2=None, op0=ALU.mult),
                         deps=[l2, ptA[-1]], inc="vp")
            bk["Pb2"] = tsB
            bk[("rec", hp)] = tsB
            ptB = []
            for kc in range(3):
                ptB.append(p.step("tensor", lambda e, kc=kc: e.transpose(
                    banks[7][:].bitcast(BF16)[:, 128 * kc:128 * (kc + 1)],
                    Ps[:, 128 * kc:128 * (kc + 1)], T["ident_bf"][:]),
                    deps=[tsB, peA], inc="tp"))
            bk["Ps"] = ptB[-1]
            peB = p.step("vector", lambda e: e.tensor_copy(PT[:],
                         banks[7][:].bitcast(BF16)[:, 0:384]),
                         deps=ptB + avA, inc="vp")
            bk[7] = peB
            for kc in range(3):
                vidx = (vbase + t - 2 + kc) % 6 if vbase >= 0 else t - 2 + kc
                p.step("tensor", lambda e, h1=h1, kc=kc, vidx=vidx, ab_=ab_,
                       ccol=ccol: e.matmul(
                    ab_[64:128, ccol:ccol + 128], vt[vidx][:, 64 * h1:64 * (h1 + 1)],
                    PT[:, 128 * kc:128 * (kc + 1)],
                    start=(kc == 0), stop=(kc == 2)),
                    deps=[peB], inc="tp")
            bk[4 + hp // 4] = p.tok("tp")
            bk["PT"] = p.tok("tp")
        e4 = p.step("scalar", lambda e, t=t: e.copy(
            attnb[:, 0:4, 128 * t:128 * (t + 1)],
            banks[4][:, 0:512].rearrange("p (a b) -> p a b", b=128)), deps=[p.tok("tp")], inc="sp")
        e5 = p.step("scalar", lambda e, t=t: e.copy(
            attnb[:, 4:8, 128 * t:128 * (t + 1)],
            banks[5][:, 0:512].rearrange("p (a b) -> p a b", b=128)), deps=[p.tok("tp")], inc="sp")
        bk[4] = e4
        bk[5] = e5

    for oc in range(8):
        b = banks[oc % 2]
        mms = []
        for dc in range(8):
            mms.append(p.step("tensor", lambda e, oc=oc, dc=dc, b=b: e.matmul(
                b[:, 0:s], woW[:, dc, 128 * oc:128 * (oc + 1)], attnb[:, dc, 0:s],
                start=(dc == 0), stop=(dc == 7)),
                deps=[bk[4], bk[5], bank_dep(oc % 2)], inc="tp"))
        di = p.dma(xinoc[:, 0:s], xin_d[128 * oc:128 * (oc + 1), t0:t0 + s],
                   deps=[bank_dep("xin")])
        rr = p.step("vector", lambda e, b=b: e.tensor_tensor(out=xmc[:, 0:s],
                    in0=b[:, 0:s], in1=xinoc[:, 0:s], op=ALU.add),
                    deps=mms + [di, bank_dep("xmc")], inc="vp")
        bk[oc % 2] = rr
        bk["xin"] = rr
        bk["xmc"] = p.dma(DR["xmid"][128 * oc:128 * (oc + 1), t0:t0 + s], xmc[:, 0:s],
                          deps=[rr], queue="gpsimd", sem="gq")

    d1 = p.dma(attnb[:, 0:4, 0:s], mkap(DR["xmid"], [(TAU, 128), (128 * TAU, 4), (1, s)],
               DR["xmid"].offset + t0), queue="gpsimd", sem="gq", deps=[bk["xmc"]])
    d2 = p.dma(attnb[:, 4:8, 0:s], mkap(DR["xmid"], [(TAU, 128), (128 * TAU, 4), (1, s)],
               DR["xmid"].offset + 512 * TAU + t0 - 1), queue="gpsimd", sem="gq")
    yv2 = p.step("vector", lambda e: e.tensor_tensor(
        out=attnb[:, 4:8, 0:s], in0=attnb[:, 4:8, 0:s],
        in1=T["svb"][:, t0:t0 + s].unsqueeze(1).broadcast_to((128, 4, s)),
        op=ALU.mult), deps=[d1, d2], inc="vp")
    ymu, yrs = ln_row_stats(p, T, attnb[:, :, 0:s], sqv[:, :, 0:s], "strow", s, [yv2])
    bn = p.step("tensor", lambda e: e.matmul(banks[6][:, 0:s], T["ones_row"][:],
                strow["nmu"][0:1, 0:s], start=True, stop=True),
                deps=[ymu, yrs, bank_dep(6)], inc="tp")
    br = p.step("tensor", lambda e: e.matmul(banks[7][:, 0:s], T["ones_row"][:],
                strow["rst"][0:1, 0:s], start=True, stop=True),
                deps=[yrs, bank_dep(7)], inc="tp")
    w1 = p.step("vector", lambda e, s=s: e.tensor_tensor(out=sqv[:, :, 0:s],
                in0=attnb[:, :, 0:s],
                in1=banks[6][:, 0:s].unsqueeze(1).broadcast_to((128, 8, s)),
                op=ALU.add), deps=[bn, yrs, yv2, bank_dep("tq")], inc="vp")
    w2 = p.step("vector", lambda e, s=s: e.tensor_tensor(out=sqv[:, :, 0:s],
                in0=sqv[:, :, 0:s],
                in1=banks[7][:, 0:s].unsqueeze(1).broadcast_to((128, 8, s)),
                op=ALU.mult), deps=[w1, br], inc="vp")
    for dc in range(8):
        w3 = p.step("vector", lambda e, dc=dc, s=s: e.tensor_scalar(out=qb[:, dc, 0:s],
                    in0=sqv[:, dc, 0:s], scalar1=gacol[:, dc:dc + 1],
                    scalar2=becol[:, dc:dc + 1], op0=ALU.mult, op1=ALU.add),
                    deps=[w2], inc="vp")
        bk["tq"] = w3
    p.dma(mkap(DR["xn"], [(TAU, 128), (128 * TAU, 8), (1, s)], DR["xn"].offset + t0),
          qb[:, :, 0:s], deps=[bk["tq"]], queue="gpsimd", sem="gq")
    bk[6] = bk["tq"]
    bk[7] = bk["tq"]


def emit_b2_block(p, T, DR, l, xout_d, t0, s, w1W, w2W, hview, qb, glub,
                  xinoc, xmc, b1acol, b1gcol, b2col, banks):
    bk = {}
    dn = p.dma(qb[:, :, 0:s], mkap(DR["xn"], [(TAU, 128), (128 * TAU, 8), (1, s)],
               DR["xn"].offset + t0))
    for ocp in range(22):
        ba, bg = banks[0], banks[1]
        mms = []
        for dc in range(8):
            mms.append(p.step("tensor", lambda e, ocp=ocp, dc=dc: e.matmul(
                ba[:, 0:s], w1W[:, dc, 128 * ocp:128 * (ocp + 1)], qb[:, dc, 0:s],
                start=(dc == 0), stop=(dc == 7)), deps=[dn, bk.get(0)], inc="tp"))
        mmsg = []
        for dc in range(8):
            mmsg.append(p.step("tensor", lambda e, ocp=ocp, dc=dc: e.matmul(
                bg[:, 0:s], w1W[:, dc, 2816 + 128 * ocp:2816 + 128 * (ocp + 1)],
                qb[:, dc, 0:s], start=(dc == 0), stop=(dc == 7)),
                deps=[dn, bk.get(1)], inc="tp"))
        ge = p.step("scalar", lambda e, ocp=ocp: e.activation(out=glub[:, 0:s],
                    in_=bg[:, 0:s], func=AF.Gelu, bias=b1gcol[:, ocp:ocp + 1]),
                    deps=mmsg + [bk.get("glub")], inc="sp")
        bk[1] = ge
        hh = p.step("vector", lambda e, ocp=ocp: e.scalar_tensor_tensor(
            out=hview[:, ocp, 0:s], in0=ba[:, 0:s], scalar=b1acol[:, ocp:ocp + 1],
            in1=glub[:, 0:s], op0=ALU.add, op1=ALU.mult), deps=mms + [ge], inc="vp")
        bk[0] = hh
        bk["glub"] = hh
        bk["h"] = hh
    for oc in range(8):
        b = banks[2 + oc % 2]
        mms = []
        for hc in range(22):
            mms.append(p.step("tensor", lambda e, oc=oc, hc=hc, b=b: e.matmul(
                b[:, 0:s], w2W[:, hc, 128 * oc:128 * (oc + 1)], hview[:, hc, 0:s],
                start=(hc == 0), stop=(hc == 21)),
                deps=[bk["h"], bk.get(2 + oc % 2)], inc="tp"))
        di = p.dma(xinoc[:, 0:s], DR["xmid"][128 * oc:128 * (oc + 1), t0:t0 + s],
                   deps=[bk.get("xi"), bk["h"]])
        rr = p.step("vector", lambda e, oc=oc, b=b: e.scalar_tensor_tensor(
            out=xmc[:, 0:s], in0=b[:, 0:s], scalar=b2col[:, oc:oc + 1],
            in1=xinoc[:, 0:s], op0=ALU.add, op1=ALU.add),
            deps=mms + [di, bk.get("xo")], inc="vp")
        bk[2 + oc % 2] = rr
        bk["xi"] = rr
        bk["xo"] = p.dma(xout_d[128 * oc:128 * (oc + 1), t0:t0 + s], xmc[:, 0:s],
                         deps=[rr], queue="gpsimd", sem="gq")


def phase_final(p, T, DR, OUT, depth, dbg):
    stream = DR["xt0"] if depth % 2 == 0 else DR["xt1"]
    sb = p.sbuf
    ffin = sb("ffin", [128, 8, 128], F32)
    fbf = sb("fbf", [128, 8, 128], BF16)
    fsq = sb("fsq", [128, 8, 128], BF16)
    fx1 = sb("fx1", [128, 8, 128], F32)
    outr = sb("outr", [128, 1024], F32)
    frow = {"nmu": sb("f_nmu", [1, 128], F32), "ex2": sb("f_ex2", [1, 128], F32),
            "musq": sb("f_musq", [1, 128], F32), "var": sb("f_var", [1, 128], F32),
            "rst": sb("f_rst", [1, 128], F32)}
    T["frow"] = frow
    T["tmpA"] = sb("f_tmpA", [128, 128], F32)
    T["tmpB"] = sb("f_tmpB", [128, 128], F32)
    banks = [T[f"pbank{i}"] for i in range(8)]
    for j in range(16):
        tau = 1792 + 128 * j
        d1 = p.dma(ffin[:, :, :], mkap(stream, [(TAU, 128), (128 * TAU, 8), (1, 128)],
                   stream.offset + tau))
        d2 = p.dma(fx1[:, :, :], mkap(DR["x1"], [(TAU, 128), (128 * TAU, 8), (1, 128)],
                   DR["x1"].offset + tau))
        cv = p.step("vector", lambda e: e.tensor_copy(fbf[:], ffin[:]), deps=[d1], inc="vp")
        ymu, yrs = ln_row_stats(p, T, fbf[:, :, :], fsq[:, :, :], "frow", 128, [cv])
        bn = p.step("tensor", lambda e: e.matmul(banks[6][:, 0:128], T["ones_row"][:],
                    frow["nmu"][0:1, :], start=True, stop=True), deps=[ymu, yrs], inc="tp")
        br = p.step("tensor", lambda e: e.matmul(banks[7][:, 0:128], T["ones_row"][:],
                    frow["rst"][0:1, :], start=True, stop=True), deps=[yrs], inc="tp")
        evs = []
        prev_tm = None
        for dc in range(8):
            w1 = p.step("vector", lambda e, dc=dc: e.tensor_tensor(out=T["tmpA"][:, 0:128],
                        in0=fbf[:, dc, :], in1=banks[6][:, 0:128], op=ALU.add),
                        deps=[bn, cv, prev_tm], inc="vp")
            w2 = p.step("vector", lambda e: e.tensor_tensor(out=T["tmpB"][:, 0:128],
                        in0=T["tmpA"][:, 0:128], in1=banks[7][:, 0:128], op=ALU.mult),
                        deps=[w1, br], inc="vp")
            w3 = p.step("vector", lambda e, dc=dc: e.tensor_tensor(out=T["tmpA"][:, 0:128],
                        in0=T["tmpB"][:, 0:128], in1=fx1[:, dc, :], op=ALU.add),
                        deps=[w2, d2], inc="vp")
            tm = p.step("tensor", lambda e, dc=dc: e.transpose(banks[dc % 2][:, 0:128],
                        T["tmpA"][:, 0:128], T["ident_f32"][:]),
                        deps=[w3, evs[dc - 2] if dc >= 2 else None], inc="tp")
            prev_tm = tm
            ev = p.step("scalar", lambda e, dc=dc: e.copy(outr[:, 128 * dc:128 * (dc + 1)],
                        banks[dc % 2][:, 0:128]), deps=[tm], inc="sp")
            evs.append(ev)
        p.dma(OUT[128 * j:128 * (j + 1), :], outr[:], deps=evs, queue="gpsimd", sem="gq")
        p.barrier()
    p.step("sync", lambda e: e.wait_ge(p.sem("gq"), p.semc["gq"]))
    p.step("sync", lambda e: e.wait_ge(p.sem("dq"), p.semc["dq"]))


def host_prep(inputs):
    u = np.ascontiguousarray(inputs["L0_attn_u"])
    per_core = []
    for c in range(NCORES):
        b, half = c // 2, c % 2
        if half == 0:
            u_ssm = np.concatenate([np.zeros((2048, D), np.float32), u[b, :2048]], axis=0)
        else:
            u_ssm = np.ascontiguousarray(u[b])
        goff = -1792 + 2048 * half
        pos = (np.arange(TAU, dtype=np.float32) + goff)[None, :]
        sv = ((np.arange(TAU) + goff - 1) >= 0).astype(np.float32)[None, :]
        cthr = np.zeros((NT, 128), np.float32)
        for i in range(NT):
            cthr[i, :] = 256.0 - (i * 128 + goff)
        per_core.append(dict(u_ssm=u_ssm, pos=pos, sv=sv, cthr=cthr))
    return per_core


def make_inmaps(inputs, per_core):
    maps = []
    for c in range(NCORES):
        pc = per_core[c]
        maps.append({
            "u_ssm": pc["u_ssm"],
            "in_a": inputs["L0_attn_a"], "in_b": inputs["L0_attn_b"], "in_c": inputs["L0_attn_c"],
            "Wqkv": inputs["L0_ffn_Wqkv"], "qs": inputs["L0_ffn_q_scale"],
            "ks": inputs["L0_ffn_k_scale"], "Wo": inputs["L0_ffn_Wo"],
            "ln_g": inputs["L0_ffn_ln_g"], "ln_b": inputs["L0_ffn_ln_b"],
            "W1": inputs["L0_ffn_W1"], "b1": inputs["L0_ffn_b1"],
            "W2": inputs["L0_ffn_W2"], "b2": inputs["L0_ffn_b2"],
            "pos": pc["pos"], "sv": pc["sv"], "cthr": pc["cthr"],
        })
    return maps


def kernel(**inputs):
    per_core = host_prep(inputs)
    nc, _, _ = build_program(depth=DEPTH, debug=False, upto="all")
    maps = make_inmaps(inputs, per_core)
    res = run_bass_kernel_spmd(nc, maps, core_ids=list(range(NCORES)))
    out = np.zeros((4, 4096, D), np.float32)
    for c in range(NCORES):
        b, half = c // 2, c % 2
        out[b, half * 2048:(half + 1) * 2048, :] = res.results[c]["out_slice"]
    return out
